# version 15
# speedup vs baseline: 1.5740x; 1.0574x over previous
import numpy as np
from contextlib import ExitStack

import concourse.bass as bass
import concourse.mybir as mybir
from concourse.bass_utils import run_bass_kernel_spmd

F32 = mybir.dt.float32
BF16 = mybir.dt.bfloat16
I32 = mybir.dt.int32
AF = mybir.ActivationFunctionType
ALU = mybir.AluOpType

D = 1024
H = 16
HD = 64
FT = 16
TCORE = 2048
NTT = 17
NG = 4
NCH = 8
DI = 2816
NKI = 22
EPS_NORM = 1e-6
EPS_LIN = 1e-6
EPS_LN = 1e-5

ENGS = ("sync", "vector", "scalar", "tensor", "gpsimd")
PARANOID = False
EMIT_WIN = True
EMIT_LIN = True
LIN_INTRA = True
LIN_INTER = True
INTER_KK = (0, 1, 2)
INTER_SU_ONES = False


class Prog:
    def __init__(self):
        self.ops = []
        self.cnt = {}

    def c(self, sem):
        return self.cnt.get(sem, 0)

    def bump(self, sem, n):
        self.cnt[sem] = self.c(sem) + n
        return self.cnt[sem]

    def wait(self, eng, sem, val=None):
        if val is None:
            val = self.c(sem)
        if val > 0:
            self.ops.append((eng, "wait", (sem, val)))

    def op(self, eng, fn, inc=None, inc_n=1):
        if eng == "vector":
            assert inc in (None, "vv")
            self.wait(eng, "vv")
            inc, inc_n = "vv", 1
        elif eng == "scalar":
            assert inc in (None, "ss")
            self.wait(eng, "ss")
            inc, inc_n = "ss", 1
        tgt = self.bump(inc, inc_n) if inc is not None else None
        self.ops.append((eng, "op", (fn, inc, inc_n)))
        return tgt

    def dma(self, eng, out, in_, sem="dq"):
        tgt = self.bump(sem, 16)
        self.ops.append((eng, "dma", (out, in_, sem)))
        if sem == "dqo":
            self.ops.append((eng, "wait", (sem, tgt)))
        return tgt

    def memset(self, eng, ap, const, sem="ms"):
        tgt = self.bump(sem, 16)
        self.ops.append((eng, "memset", (ap, const, sem)))
        return tgt

    def sync3(self, *engs):
        sems = [s for s in self.cnt if s not in ("bar", "cc")]
        for e in engs:
            for s in sems:
                self.wait(e, s)
        for s in ("tt", "vv", "ss"):
            self.wait("sync", s)

    def barrier(self):
        tgt = self.bump("bar", len(ENGS))
        for e in ENGS:
            self.ops.append((e, "barinc", None))
        for e in ENGS:
            self.ops.append((e, "wait", ("bar", tgt)))

    def run(self, engname, eng, sems):
        lastw = {}
        for e, kind, data in self.ops:
            if e != engname:
                continue
            if kind == "wait":
                sem, val = data
                if lastw.get(sem, -1) >= val:
                    continue
                eng.wait_ge(sems[sem], val)
                lastw[sem] = val
            elif kind == "op":
                fn, inc, inc_n = data
                inst = fn(eng)
                if inc is not None:
                    inst.then_inc(sems[inc], inc_n)
            elif kind == "dma":
                out, in_, sem = data
                eng.dma_start(out=out, in_=in_).then_inc(sems[sem], 16)
            elif kind == "memset":
                ap, const, sem = data
                eng.memset(ap, const).then_inc(sems[sem], 16)
            elif kind == "barinc":
                eng.sem_inc(sems["bar"], 1)


class Arena:
    def __init__(self, handle, words):
        self.h = handle
        self.words = words
        self.off = 0
        self.peak = 0

    def alloc(self, n, name=""):
        n = (n + 1) & ~1
        assert self.off + n <= self.words, (
            f"SBUF arena overflow at {name}: {self.off}+{n}>{self.words}")
        ap = self.h[:, self.off:self.off + n]
        self.off += n
        self.peak = max(self.peak, self.off)
        return ap


def dap(t, offset, dims):
    return bass.AP(t.tensor, offset, dims)


class Ctx:
    pass


def build_program(stage="F", debug=False):
    nc = bass.Bass("TRN2", target_bir_lowering=False, debug=False)
    p = Prog()
    T = {}

    def din(name, shape, dt=F32):
        T[name] = nc.dram_tensor(name, shape, dt, kind="ExternalInput").ap()

    din("x_t", [D, NTT * 128])
    din("wq", [D, D])
    din("wk", [D, D])
    din("wv", [D, D])
    din("wqf_lo", [128, 128])
    din("wqf_hi", [128, 128])
    din("wkf_lo", [128, 128])
    din("wkf_hi", [128, 128])
    din("wout", [2048, D])
    din("normw", [128, 8])
    din("m_diag", [128, 128])
    din("m_prev", [128, 128])
    din("m_first", [128, 128])
    din("m_tril", [128, 128])
    din("ident", [128, 128])
    din("ident16", [128, 128], BF16)
    din("lin_flag", [128, 1])
    din("clip_x", [128, 2])
    din("clip_dn", [128, 2])
    din("w_gate", [DI, D])
    din("w_up", [DI, D])
    din("w_down", [D, DI])

    T["out_t"] = nc.dram_tensor("out_t", [D, TCORE], F32, kind="ExternalOutput").ap()

    SP = {}
    for name, shape in (("h_sp", [128, 4 * 4096]), ("k_sp", [128, 8 * NTT * 128]),
                        ("v_sp", [128, NTT * 1024]), ("y1_sp", [128, 8 * TCORE]),
                        ("pref_sp", [128, NCH * 6144]), ("t_sp", [128, NKI * TCORE])):
        SP[name] = nc.dram_tensor(name, shape, F32).ap()
    SP["mx_sp"] = nc.dram_tensor("mx_sp", [1, 128], F32).ap()
    for name, shape in (("ag_st_in", [128, 6144]), ("ag_st_out", [256, 6144]),
                        ("ag_m1_in", [1, 8]), ("ag_m1_out", [8, 8]),
                        ("ag_m2_in", [1, 8]), ("ag_m2_out", [8, 8])):
        SP[name] = nc.dram_tensor(name, shape, F32)

    if debug and stage == "F":
        T["dbg_q"] = nc.dram_tensor("dbg_q", [128, 5632], F32, kind="ExternalOutput").ap()
        T["dbg_sc"] = nc.dram_tensor("dbg_sc", [128, 16], F32, kind="ExternalOutput").ap()
        T["dbg_chain"] = nc.dram_tensor("dbg_chain", [128, 32], F32, kind="ExternalOutput").ap()
        T["dbg_xq"] = nc.dram_tensor("dbg_xq", [128, 16384], BF16, kind="ExternalOutput").ap()
        T["dbg_t"] = nc.dram_tensor("dbg_t", [128, 2048], F32, kind="ExternalOutput").ap()
        T["dbg_h2"] = nc.dram_tensor("dbg_h2", [128, 11264], BF16, kind="ExternalOutput").ap()
        T["dbg_y2"] = nc.dram_tensor("dbg_y2", [128, 4096], F32, kind="ExternalOutput").ap()
    if debug:
        T["dbg_state"] = nc.dram_tensor("dbg_state", [128, 6144], F32, kind="ExternalOutput").ap()
        T["dbg_k"] = nc.dram_tensor("dbg_k", [128, 2176], F32, kind="ExternalOutput").ap()
        T["dbg_h"] = nc.dram_tensor("dbg_h", [128, 4096], F32, kind="ExternalOutput").ap()
        if stage in ("Q", "F"):
            T["dbg_y1"] = nc.dram_tensor("dbg_y1", [128, 8 * TCORE], F32, kind="ExternalOutput").ap()
            T["dbg_cat"] = nc.dram_tensor("dbg_cat", [128, 4096], F32, kind="ExternalOutput").ap()
            T["dbg_yh"] = nc.dram_tensor("dbg_yh", [128, 3072], F32, kind="ExternalOutput").ap()
            T["dbg_ff"] = nc.dram_tensor("dbg_ff", [128, 1280], F32, kind="ExternalOutput").ap()
            T["dbg_su"] = nc.dram_tensor("dbg_su", [128, 12288], F32, kind="ExternalOutput").ap()

    with ExitStack() as es:
        C = Ctx()
        C.p, C.nc, C.T, C.SP, C.stage, C.debug = p, nc, T, SP, stage, debug
        pers = es.enter_context(nc.sbuf_tensor("pers", [128, 1426], F32))
        pers16 = es.enter_context(nc.sbuf_tensor("pers16", [128, 128], BF16))
        st_t = es.enter_context(nc.sbuf_tensor("st_t", [128, 6144], F32))
        C.pb = [es.enter_context(nc.psum_tensor(f"pb{i}", [128, 512], F32)) for i in range(7)]
        C.pbh = es.enter_context(nc.psum_tensor("pbh", [128, 1024], BF16))
        sems = {}

        PA = Arena(pers, 1426)
        C.MD = PA.alloc(128, "m_diag")
        C.MP = PA.alloc(128, "m_prev")
        C.MF = PA.alloc(128, "m_first")
        C.MT = PA.alloc(128, "m_tril")
        C.s_id = PA.alloc(128, "ident")
        C.s_wqf = [PA.alloc(128, "wqf_lo"), PA.alloc(128, "wqf_hi")]
        C.s_wkf = [PA.alloc(128, "wkf_lo"), PA.alloc(128, "wkf_hi")]
        C.s_normw = PA.alloc(8, "normw")
        C.s_flag = PA.alloc(2, "flag")
        C.s_ones = PA.alloc(128, "allones")
        C.s_scal = PA.alloc(16, "scalars")
        C.s_clipx = PA.alloc(2, "clipx")
        C.s_clipdn = PA.alloc(2, "clipdn")
        C.s_id16 = pers16[:]
        C.s_state = st_t[:]
        C.s_const = PA.alloc(16, "consts")
        C.consts = {}

        def constv(val):
            val = float(val)
            if val not in C.consts:
                i = len(C.consts)
                assert i < 16
                C.consts[val] = C.s_const[:, i:i + 1]
            return C.consts[val]
        C.constv = constv

        emit_init(C)
        with nc.sbuf_tensor("akq", [128, 40000], F32) as akq:
            C.A = Arena(akq, 40000)
            emit_phase_k(C)
            if stage != "K":
                C.A = Arena(akq, 40000)
                C.A.off = 0
                emit_phase_q(C)
        if stage == "F":
            emit_phase_f(C)
        emit_finish(C)

        for s in set(list(p.cnt.keys()) + ["bar"]):
            sems[s] = es.enter_context(nc.semaphore(s))
        block = es.enter_context(nc.Block())
        for engname in ENGS:
            getattr(block, engname)(lambda eng, _n=engname: p.run(_n, eng, sems))

    return nc


def emit_init(C):
    p, T = C.p, C.T
    for dst, key in ((C.MD, "m_diag"), (C.MP, "m_prev"), (C.MF, "m_first"),
                     (C.MT, "m_tril"), (C.s_id, "ident"), (C.s_id16, "ident16"),
                     (C.s_wqf[0], "wqf_lo"), (C.s_wqf[1], "wqf_hi"),
                     (C.s_wkf[0], "wkf_lo"), (C.s_wkf[1], "wkf_hi"),
                     (C.s_normw, "normw"), (C.s_flag[:, 0:1], "lin_flag"),
                     (C.s_clipx[:, 0:2], "clip_x"), (C.s_clipdn[:, 0:2], "clip_dn")):
        p.dma("sync", dst, T[key])
    p.memset("vector", C.s_ones, 1.0)
    p.memset("vector", C.s_state, 0.0)
    for v in (0.0, EPS_NORM, EPS_LN, -1.0, 256.5, 1.0):
        p.memset("vector", C.constv(v), v)
    for e in ("vector", "tensor", "scalar"):
        p.wait(e, "dq")
        p.wait(e, "ms")


def sblk(C, h, kk):
    return C.s_state[:, (3 * h + kk) * 128:(3 * h + kk) * 128 + 128]


def x_ap(C, tfirst, ntok):
    return dap(C.T["x_t"], tfirst * 128,
               [[NTT * 128, 128], [128 * NTT * 128, 8], [1, ntok]])


def emit_phase_k(C):
    p, A, pb, T, SP = C.p, C.A, C.pb, C.T, C.SP

    s_wk = A.alloc(8 * 1024, "wk")
    s_wv = A.alloc(8 * 1024, "wv")
    for c in range(8):
        p.dma("sync", s_wk[:, c * 1024:(c + 1) * 1024], T["wk"][c * 128:(c + 1) * 128, :], sem="dwk")
        p.dma("sync", s_wv[:, c * 1024:(c + 1) * 1024], T["wv"][c * 128:(c + 1) * 128, :], sem="dwk")
    wk_ld = p.c("dwk")

    s_x = [A.alloc(8 * 512, "xg0"), A.alloc(8 * 512, "xg1")]
    s_h = A.alloc(8 * 512, "hg")
    s_hsq = A.alloc(8 * 512, "hsq")
    s_rms = A.alloc(512, "rms")
    s_ktmp = [A.alloc(512, "ktmp0"), A.alloc(512, "ktmp1")]
    s_vtile = [A.alloc(1024, "vt0"), A.alloc(1024, "vt1")]
    s_kftm = A.alloc(2 * 256, "kftm")
    s_phik2 = [[A.alloc(384, "phik00"), A.alloc(384, "phik01")],
               [A.alloc(384, "phik10"), A.alloc(384, "phik11")]]
    s_vb2 = [[A.alloc(128, "vb00"), A.alloc(128, "vb01")],
             [A.alloc(128, "vb10"), A.alloc(128, "vb11")]]

    for hp in (0, 1):
        for b in (0, 1):
            p.memset("vector", s_phik2[hp][b], 0.0)
            p.memset("vector", s_phik2[hp][b][:, 32:33], 1.0)
            p.memset("vector", s_vb2[hp][b][:, 64:128], 1.0)
    p.wait("vector", "ms")

    kgroups = [(0, 1)] + [(1 + 4 * g, 4) for g in range(4)]

    def xg_sb(buf, ntok):
        return s_x[buf][:, 0:8 * ntok].rearrange("p (c t) -> p c t", c=8)

    x_done = [0, 0]
    p.dma("sync", xg_sb(0, 128), x_ap(C, 0, 128), sem="dx0")
    x_done[0] = p.c("dx0")

    h_spill_done = 0
    prev_kv_done = 0
    h_done_prev = 0
    state_add_done = 0
    ktmp_free = [0, 0]
    ktmp_spill = [0, 0]
    vtile_spill = [0, 0]
    b23_free = 0
    kf_evac_all = 0

    for gi, (tfirst, ntile) in enumerate(kgroups):
        nt = ntile * 128
        buf = gi % 2
        if gi + 1 < len(kgroups):
            nf, nn = kgroups[gi + 1]
            p.wait("sync", "vv", h_done_prev)
            p.dma("sync", xg_sb(1 - buf, nn * 128), x_ap(C, nf, nn * 128), sem=f"dx{1 - buf}")
            x_done[1 - buf] = p.c(f"dx{1 - buf}")
        xa = s_x[buf]
        p.wait("scalar", f"dx{buf}", x_done[buf])
        p.wait("scalar", "tt", prev_kv_done)
        p.op("scalar", lambda e, xa=xa, nt=nt: e.activation(
            out=s_hsq[:, 0:8 * nt], in_=xa[:, 0:8 * nt], func=AF.Square,
            bias=C.constv(0.0), scale=1.0), inc="ss")
        sq_done = p.c("ss")
        p.wait("tensor", "ss", sq_done)
        for c in range(8):
            p.op("tensor", lambda e, c=c, nt=nt: e.matmul(
                pb[6][:, 0:nt], C.s_ones, s_hsq[:, c * nt:c * nt + nt],
                start=(c == 0), stop=(c == 7)), inc="tt" if c == 7 else None)
        mm_rms = p.c("tt")
        p.wait("scalar", "tt", mm_rms)
        p.op("scalar", lambda e, nt=nt: e.activation(
            out=s_rms[:, 0:nt], in_=pb[6][:, 0:nt], func=AF.Ln,
            bias=C.constv(EPS_NORM), scale=1.0 / D), inc="ss")
        p.op("scalar", lambda e, nt=nt: e.activation(
            out=s_rms[:, 0:nt], in_=s_rms[:, 0:nt], func=AF.Exp,
            bias=C.constv(0.0), scale=-0.5), inc="ss")
        rms_done = p.c("ss")
        p.wait("vector", "ss", rms_done)
        p.wait("vector", "dhs", h_spill_done)
        p.wait("vector", "tt", prev_kv_done)
        for c in range(8):
            p.op("vector", lambda e, c=c, xa=xa, nt=nt: e.scalar_tensor_tensor(
                out=s_h[:, c * nt:c * nt + nt], in0=xa[:, c * nt:c * nt + nt],
                scalar=C.s_normw[:, c:c + 1], in1=s_rms[:, 0:nt],
                op0=ALU.mult, op1=ALU.mult), inc="vv" if c == 7 else None)
        h_done = p.c("vv")
        h_done_prev = h_done
        if tfirst > 0:
            p.wait("sync", "vv", h_done)
            h_spill_done = p.dma("sync",
                SP["h_sp"][:, (gi - 1) * 4096:(gi - 1) * 4096 + 8 * nt],
                s_h[:, 0:8 * nt], sem="dhs")

        if PARANOID:
            p.sync3("tensor", "vector", "scalar")
        p.wait("tensor", "dwk", wk_ld)
        p.wait("tensor", "vv", h_done)
        kf_mm_last = 0
        for m in range(8):
            if PARANOID:
                p.sync3("tensor", "vector")
            bank = m % 2
            p.wait("tensor", "vv", ktmp_free[bank])
            for c in range(8):
                p.op("tensor", lambda e, m=m, c=c, bank=bank, nt=nt: e.matmul(
                    pb[bank][:, 0:nt],
                    s_wk[:, c * 1024 + m * 128: c * 1024 + m * 128 + 128],
                    s_h[:, c * nt:c * nt + nt], start=(c == 0), stop=(c == 7)),
                    inc="tt" if c == 7 else None)
            kmm = p.c("tt")
            p.wait("vector", "tt", kmm)
            p.wait("vector", f"dks{bank}", ktmp_spill[bank])
            p.op("vector", lambda e, bank=bank, nt=nt: e.tensor_copy(
                out=s_ktmp[bank][:, 0:nt], in_=pb[bank][:, 0:nt]), inc="vv")
            ktmp_free[bank] = p.c("vv")
            p.wait("sync", "vv", ktmp_free[bank])
            ktmp_spill[bank] = p.dma("sync",
                SP["k_sp"][:, m * NTT * 128 + tfirst * 128:
                           m * NTT * 128 + (tfirst + ntile) * 128],
                s_ktmp[bank][:, 0:nt], sem=f"dks{bank}")
            if tfirst > 0:
                p.wait("tensor", "vv", max(ktmp_free[bank], kf_evac_all))
                for tl in range(ntile):
                    for par in (0, 1):
                        h = 2 * m + par
                        kb = 4 + tl % 2
                        p.op("tensor", lambda e, tl=tl, par=par, h=h, kb=kb, bank=bank: e.matmul(
                            pb[kb][:, h * 16 + (tl // 2) * 256: h * 16 + (tl // 2) * 256 + 16],
                            s_ktmp[bank][:, tl * 128:(tl + 1) * 128],
                            C.s_wkf[par][:, 0:16], start=True, stop=True),
                            inc="tt" if (tl == ntile - 1 and par == 1) else None)
                kf_mm_last = p.c("tt")

        for tl in range(ntile):
            if PARANOID:
                p.sync3("tensor", "vector", "scalar")
            vbuf = tl % 2
            p.wait("tensor", "vv", b23_free)
            for hf in range(2):
                bank = 2 + hf
                for c in range(8):
                    p.op("tensor", lambda e, tl=tl, hf=hf, c=c, bank=bank, nt=nt: e.matmul(
                        pb[bank][:, 0:512],
                        s_h[:, c * nt + tl * 128: c * nt + tl * 128 + 128],
                        s_wv[:, c * 1024 + hf * 512: c * 1024 + (hf + 1) * 512],
                        start=(c == 0), stop=(c == 7)), inc="tt" if c == 7 else None)
            vmm = p.c("tt")
            p.wait("vector", "tt", vmm)
            p.wait("vector", f"dvs{vbuf}", vtile_spill[vbuf])
            for hf in range(2):
                p.op("vector", lambda e, hf=hf, vbuf=vbuf: e.tensor_copy(
                    out=s_vtile[vbuf][:, hf * 512:(hf + 1) * 512],
                    in_=pb[2 + hf][:, 0:512]), inc="vv" if hf == 1 else None)
            v_evac = p.c("vv")
            b23_free = v_evac
            p.wait("sync", "vv", v_evac)
            vtile_spill[vbuf] = p.dma("sync",
                SP["v_sp"][:, (tfirst + tl) * 1024:(tfirst + tl + 1) * 1024],
                s_vtile[vbuf], sem=f"dvs{vbuf}")

            if tfirst > 0 and tl % 2 == 1:
                cglob = (tfirst - 1) // 2 + tl // 2
                p.wait("vector", "tt", kf_mm_last)
                for b in (0, 1):
                    kb = 4 + (tl - 1 + b) % 2
                    colb = ((tl - 1 + b) // 2) * 256
                    p.op("vector", lambda e, b=b, kb=kb, colb=colb: e.tensor_copy(
                        out=s_kftm[:, b * 256:(b + 1) * 256],
                        in_=pb[kb][:, colb:colb + 256]), inc="vv" if b == 1 else None)
                kf_evac_all = p.c("vv")
                p.wait("sync", "vv", state_add_done if cglob > 0 else 0)
                pref_done = p.dma("sync",
                    SP["pref_sp"][:, cglob * 6144:(cglob + 1) * 6144], C.s_state, sem="dps")
                upd_aps = [
                    [pb[0][:, 0:128], pb[1][:, 0:128], pb[6][:, 0:128]],
                    [pb[4][:, 0:128], pb[5][:, 0:128], pb[6][:, 128:256]],
                ]
                phik_done = [0, 0]
                upd_mm2 = [0, 0]
                add_done2 = [0, 0]
                for hstep in range(H + 1):
                    if hstep < H:
                        h = hstep
                        hp = h % 2
                        p.wait("vector", "tt", upd_mm2[hp])
                        for b in (0, 1):
                            vtl = (tl - 1 + b) % 2
                            p.op("vector", lambda e, b=b, h=h, vtl=vtl, hp=hp: e.tensor_copy(
                                out=s_vb2[hp][b][:, 0:64],
                                in_=s_vtile[vtl][:, h * 64:(h + 1) * 64]), inc=None)
                            p.op("vector", lambda e, b=b, h=h, hp=hp: e.tensor_copy(
                                out=s_phik2[hp][b][:, 0:16],
                                in_=s_kftm[:, b * 256 + h * 16: b * 256 + h * 16 + 16]),
                                inc=None)
                            tns = s_kftm
                            pstride = tns.ap[0][0]
                            in0 = bass.AP(tns.tensor, tns.offset + b * 256 + h * 16,
                                          [[pstride, 128], [0, 16], [1, 16]])
                            in1 = bass.AP(tns.tensor, tns.offset + b * 256 + h * 16,
                                          [[pstride, 128], [1, 16], [0, 16]])
                            outo = s_phik2[hp][b][:, 128:384].rearrange("p (j i) -> p j i", j=16)
                            p.op("vector", lambda e, in0=in0, in1=in1, outo=outo: e.tensor_tensor(
                                out=outo, in0=in0, in1=in1, op=ALU.mult), inc="vv")
                        phik_done[hp] = p.c("vv")
                    if hstep >= 1:
                        h = hstep - 1
                        hp = h % 2
                        p.wait("tensor", "vv", phik_done[hp])
                        p.wait("tensor", "vv", add_done2[hp])
                        for kk in range(3):
                            for b in (0, 1):
                                p.op("tensor", lambda e, kk=kk, b=b, hp=hp: e.matmul(
                                    upd_aps[hp][kk],
                                    s_phik2[hp][b][:, kk * 128:(kk + 1) * 128],
                                    s_vb2[hp][b], start=(b == 0), stop=(b == 1)),
                                    inc="tt" if b == 1 else None)
                        upd_mm2[hp] = p.c("tt")
                        upd_mm_prev = upd_mm2[hp]
                        p.wait("vector", "tt", upd_mm2[hp])
                        p.wait("vector", "dps", pref_done)
                        for kk in range(3):
                            p.op("vector", lambda e, h=h, kk=kk, hp=hp: e.tensor_tensor(
                                out=sblk(C, h, kk), in0=sblk(C, h, kk),
                                in1=upd_aps[hp][kk], op=ALU.add),
                                inc="vv" if kk == 2 else None)
                        add_done2[hp] = p.c("vv")
                state_add_done = p.c("vv")
                ktmp_free[0] = max(ktmp_free[0], state_add_done)
                ktmp_free[1] = max(ktmp_free[1], state_add_done)
        prev_kv_done = p.c("tt")

    C.state_add_done = state_add_done
    C.k_tt_done = p.c("tt")

    p.wait("sync", "vv", state_add_done)
    ag_in_done = p.dma("sync", SP["ag_st_in"].ap(), C.s_state, sem="dag")
    p.wait("gpsimd", "dag", ag_in_done)
    p.op("gpsimd", lambda e: e.collective_compute(
        "AllGather", ALU.bypass, replica_groups=[[0, 1], [2, 3], [4, 5], [6, 7]],
        ins=[SP["ag_st_in"].ap()], outs=[SP["ag_st_out"].ap()]), inc="cc")
    C.ag_state_done = p.c("cc")

    if C.debug:
        p.wait("sync", "vv", state_add_done)
        p.dma("sync", T["dbg_state"], C.s_state, sem="dqo")
        p.dma("sync", T["dbg_k"], SP["k_sp"][:, 0:2176], sem="dqo")
        p.dma("sync", T["dbg_h"], SP["h_sp"][:, 0:4096], sem="dqo")


def emit_phase_q(C):
    p, A, pb, T, SP = C.p, C.A, C.pb, C.T, C.SP

    def fence_mm(bank, inc="tt"):
        p.op("tensor", lambda e, bank=bank: e.matmul(
            pb[bank][:, 504:506], C.s_ones, C.s_ones[:, 0:2],
            start=True, stop=True), inc=inc)

    A2 = Arena(A.h, A.words)
    s_hg = A2.alloc(4096, "hg")
    s_qg = A2.alloc(4096, "qg")
    s_vg = A2.alloc(5120, "vg")
    s_su = A2.alloc(6144, "su")
    s_cat = A2.alloc(4096, "cat")
    s_yh = A2.alloc(2048, "yh")
    s_xc = A2.alloc(2048, "xc")
    s_wqc = [A2.alloc(1024, "wqc0"), A2.alloc(1024, "wqc1")]
    s_woc = [A2.alloc(1024, "woc0"), A2.alloc(1024, "woc1")]
    s_ksl = [A2.alloc(640, "ksl0"), A2.alloc(640, "ksl1")]
    s_kslf = A2.alloc(640, "kslf")
    s_kff = A2.alloc(256, "kff")
    s_qff = A2.alloc(256, "qff")
    s_qftm = A2.alloc(32, "qftm")
    s_opqtm = A2.alloc(512, "opqtm")
    s_opqf = A2.alloc(512, "opqf")
    s_vbw = A2.alloc(3 * 128, "vbw")
    s_E = A2.alloc(512, "E")
    s_sq = A2.alloc(384, "sq")
    s_A = A2.alloc(384, "A")
    s_zr = A2.alloc(256, "zr")
    s_zt = A2.alloc(256, "zt")
    s_mpd = A2.alloc(256, "mpd")
    s_mfd = A2.alloc(256, "mfd")
    s_ysq = A2.alloc(256, "ysq")
    s_mb = A2.alloc(256, "mb")
    s_t2 = A2.alloc(256, "t2")
    s_u = A2.alloc(256, "u")
    s_var = A2.alloc(256, "var")
    s_r = A2.alloc(256, "r")
    s_m1 = A2.alloc(2, "m1")
    s_red = A2.alloc(2, "red")

    p.memset("vector", s_kff[:, 0:256], 0.0)
    p.memset("vector", s_qff[:, 0:256], 0.0)
    p.wait("vector", "ms")
    onesrow = bass.AP(C.s_ones.tensor, C.s_ones.offset + 32 * C.s_ones.ap[0][0],
                      [[C.s_ones.ap[0][0], 1], [0, 256]])
    p.op("vector", lambda e: e.tensor_copy(out=s_kff[32:33, 0:256], in_=onesrow), inc=None)
    p.op("vector", lambda e: e.tensor_copy(out=s_qff[32:33, 0:256], in_=onesrow), inc="vv")
    for i in range(3):
        p.memset("vector", s_vbw[:, i * 128 + 64:i * 128 + 128], 1.0)
    p.memset("vector", s_m1, 0.0)
    p.wait("vector", "ms")
    p.op("vector", lambda e: e.tensor_copy(out=s_mpd[:, 0:128], in_=C.MP), inc=None)
    p.op("vector", lambda e: e.tensor_copy(out=s_mpd[:, 128:256], in_=C.MD), inc=None)
    p.op("vector", lambda e: e.tensor_copy(out=s_mfd[:, 0:128], in_=C.MF), inc=None)
    p.op("vector", lambda e: e.tensor_copy(out=s_mfd[:, 128:256], in_=C.MD), inc="vv")

    ksl_ld = [0, 0]
    qg_free = 0
    hg_users = 0
    vg_users = 0
    cat_users = 0
    kffqff_users = 0
    vbw_users = 0
    opq_users = 0
    su_users = 0
    opq_evac = 0
    e_consumed = 0
    sq_consumed = 0
    tr_done_prev = 0
    qftm_cons = 0
    b6_read = 0
    e_read = 0
    sq_read = 0
    yh_spill = 0
    wout_done = 0
    yh_built_all = 0

    hg_users_prev = 0
    vg_users_prev = 0
    su_ready = 0
    for g in range(NG):
        if g == 0:
            p.wait("sync", "tt", C.k_tt_done)
            p.wait("sync", "vv", C.state_add_done)
            for s in ("dhs", "dvs0", "dvs1", "dks0", "dks1"):
                p.wait("sync", s, p.c(s))
        p.wait("sync", "tt", hg_users_prev)
        hg_ld = p.dma("sync", s_hg, SP["h_sp"][:, g * 4096:(g + 1) * 4096], sem="dhg")
        p.wait("sync", "vv", vg_users_prev)
        vg_ld = p.dma("sync", s_vg, SP["v_sp"][:, 4 * g * 1024:(4 * g + 5) * 1024], sem="dvg")

        p.wait("tensor", "dhg", hg_ld)
        p.wait("tensor", "tt", qg_free)
        if g == 0:
            p.wait("tensor", "vv", C.state_add_done)
        wq_ld = [0, 0]
        wqc_users = [0, 0]
        for wave in range(2):
            for c in range(8):
                wb = c % 2
                p.wait("sync", "tt", wqc_users[wb])
                wq_ld[wb] = p.dma("sync", s_wqc[wb][:, 0:512],
                                  T["wq"][c * 128:(c + 1) * 128, wave * 512:(wave + 1) * 512],
                                  sem=f"dwq{wb}")
                p.wait("tensor", f"dwq{wb}", wq_ld[wb])
                for mo in range(4):
                    p.op("tensor", lambda e, c=c, mo=mo, wb=wb: e.matmul(
                        pb[mo][:, 0:512], s_wqc[wb][:, mo * 128:(mo + 1) * 128],
                        s_hg[:, c * 512:(c + 1) * 512], start=(c == 0), stop=(c == 7)),
                        inc="tt" if mo == 3 else None)
                wqc_users[wb] = p.c("tt")
            qmm = p.c("tt")
            p.wait("scalar", "tt", qmm)
            for mo in range(4):
                m = wave * 4 + mo
                p.op("scalar", lambda e, mo=mo, m=m: e.activation(
                    out=s_qg[:, m * 512:(m + 1) * 512], in_=pb[mo][:, 0:512],
                    func=AF.Identity, bias=C.constv(0.0), scale=1.0),
                    inc="ss" if mo == 3 else None)
            qevac = p.c("ss")
            p.wait("tensor", "ss", qevac)
        q_ready = p.c("ss")

        if g == 0:
            s_soff = A.h[:, 19456:25600]
            s_pref = [C.s_state[:, 0:6144], A.h[:, 13312:19456]]
            p.wait("sync", "cc", C.ag_state_done)
            p.wait("sync", "dps", p.c("dps"))
            soff_ld = p.dma("sync", s_soff,
                            dap(SP["ag_st_out"].ap(), 0, [[6144, 128], [1, 6144]]), sem="dso")
            p.wait("vector", "dso", soff_ld)
            p.op("vector", lambda e, s_soff=s_soff: e.tensor_scalar(
                out=s_soff, in0=s_soff, scalar1=C.s_flag[:, 0:1], scalar2=None,
                op0=ALU.mult), inc="vv")
            soff_ok = p.c("vv")
            pref_ld = [0, 0]
            pref_add = [0, 0]
            pref_st = [0, 0]
            for c2 in range(NCH):
                b = c2 % 2
                p.wait("sync", "vv", pref_add[b])
                p.wait("sync", "dps", pref_st[b])
                pref_ld[b] = p.dma("sync", s_pref[b],
                                   SP["pref_sp"][:, c2 * 6144:(c2 + 1) * 6144], sem=f"dpl{b}")
                p.wait("vector", f"dpl{b}", pref_ld[b])
                p.wait("vector", "vv", soff_ok)
                p.op("vector", lambda e, b=b, s_soff=s_soff, s_pref=s_pref: e.tensor_tensor(
                    out=s_pref[b], in0=s_pref[b], in1=s_soff, op=ALU.add), inc="vv")
                pref_add[b] = p.c("vv")
                p.wait("sync", "vv", pref_add[b])
                p.dma("sync", SP["pref_sp"][:, c2 * 6144:(c2 + 1) * 6144], s_pref[b], sem="dps")
                pref_st[b] = p.c("dps")
            su_ready = p.c("dps")

        for cc in range(2):
            c = 2 * g + cc
            p.wait("sync", "dps", su_ready)
            p.wait("sync", "tt", su_users)
            su_ld = p.dma("sync", s_su, SP["pref_sp"][:, c * 6144:(c + 1) * 6144], sem="dsu")

            ksl_ok = 0
            for h in range(H):
                if PARANOID:
                    p.sync3("tensor", "vector", "scalar")
                par = h % 2
                m = h // 2
                if par == 0:
                    p.wait("sync", "vv", ksl_ok)
                    kslf_ld = p.dma("sync", s_kslf[:, 0:640],
                                    SP["k_sp"][:, m * (NTT * 128) + 4 * g * 128:
                                               m * (NTT * 128) + 4 * g * 128 + 640], sem="dkf")
                    if g == 0 and cc == 0 and h == 0:
                        p.memset("vector", s_ksl[0][:, 0:640], 0.0)
                        p.memset("vector", s_ksl[1][:, 0:640], 0.0)
                        p.wait("vector", "ms")
                        p.wait("tensor", "ms")
                    p.wait("vector", "dkf", kslf_ld)
                    p.wait("vector", "tt", kffqff_users)
                    p.op("vector", lambda e: e.tensor_copy(
                        out=s_ksl[0][0:64, 0:640], in_=s_kslf[0:64, 0:640]), inc=None)
                    p.op("vector", lambda e: e.tensor_copy(
                        out=s_ksl[1][64:128, 0:640], in_=s_kslf[64:128, 0:640]), inc="vv")
                    ksl_ok = p.c("vv")
                p.wait("vector", "tt", vbw_users)
                p.wait("vector", "dvg", vg_ld)
                for i in range(3):
                    tloc = 2 * cc + i
                    p.op("vector", lambda e, i=i, tloc=tloc, h=h: e.tensor_copy(
                        out=s_vbw[:, i * 128:i * 128 + 64],
                        in_=s_vg[:, tloc * 1024 + h * 64:tloc * 1024 + h * 64 + 64]),
                        inc="vv" if i == 2 else None)
                vbw_ok = p.c("vv")
                p.wait("tensor", "vv", ksl_ok)
                p.wait("tensor", "ss", b6_read)
                p.wait("tensor", "ss", q_ready)
                p.op("tensor", lambda e, par=par, cc=cc: e.matmul(
                    pb[6][:, 0:256], C.s_wkf[par],
                    s_ksl[par][:, (2 * cc + 1) * 128:(2 * cc + 3) * 128],
                    start=True, stop=True), inc=None)
                p.op("tensor", lambda e, par=par, m=m, cc=cc: e.matmul(
                    pb[6][:, 256:512], C.s_wqf[par],
                    s_qg[:, m * 512 + cc * 256:m * 512 + cc * 256 + 256],
                    start=True, stop=True), inc="tt")
                fqmm = p.c("tt")
                p.wait("scalar", "tt", fqmm)
                p.wait("scalar", "tt", kffqff_users)
                p.op("scalar", lambda e: e.activation(
                    out=s_kff[0:16, 0:256], in_=pb[6][0:16, 0:256],
                    func=AF.Identity, bias=C.constv(0.0)[0:16], scale=1.0), inc=None)
                p.op("scalar", lambda e: e.activation(
                    out=s_qff[0:16, 0:256], in_=pb[6][0:16, 256:512],
                    func=AF.Identity, bias=C.constv(0.0)[0:16], scale=1.0), inc="ss")
                b6_read = p.c("ss")
                ffq_ok = b6_read
                for t2_ in range(2):
                    p.op("tensor", lambda e, t2_=t2_, m=m, cc=cc, par=par: e.matmul(
                        pb[3][:, t2_ * 16:t2_ * 16 + 16],
                        s_qg[:, m * 512 + (2 * cc + t2_) * 128:m * 512 + (2 * cc + t2_) * 128 + 128],
                        C.s_wqf[par][:, 0:16], start=True, stop=True),
                        inc="tt" if t2_ == 1 else None)
                qftm_mm = p.c("tt")
                p.wait("scalar", "tt", qftm_mm)
                p.wait("scalar", "vv", qftm_cons)
                p.op("scalar", lambda e: e.activation(
                    out=s_qftm[:, 0:32], in_=pb[3][:, 0:32],
                    func=AF.Identity, bias=C.constv(0.0), scale=1.0), inc="ss")
                qftm_ok = p.c("ss")
                tq0, tq1 = 2 * cc, 2 * cc + 1
                if EMIT_WIN:
                    p.wait("tensor", "ss", e_read)
                    p.op("tensor", lambda e, tq0=tq0, par=par, m=m: e.matmul(
                        pb[0][:, 0:128],
                        s_ksl[par][:, tq0 * 128:(tq0 + 1) * 128],
                        s_qg[:, m * 512 + tq0 * 128:m * 512 + tq0 * 128 + 128],
                        start=True, stop=True), inc=None)
                    p.op("tensor", lambda e, tq0=tq0, tq1=tq1, par=par, m=m: e.matmul(
                        pb[0][:, 128:384],
                        s_ksl[par][:, tq1 * 128:(tq1 + 1) * 128],
                        s_qg[:, m * 512 + tq0 * 128:m * 512 + tq0 * 128 + 256],
                        start=True, stop=True), inc=None)
                    p.op("tensor", lambda e, tq1=tq1, par=par, m=m: e.matmul(
                        pb[0][:, 384:512],
                        s_ksl[par][:, (tq1 + 1) * 128:(tq1 + 2) * 128],
                        s_qg[:, m * 512 + tq1 * 128:m * 512 + tq1 * 128 + 128],
                        start=True, stop=True), inc="tt")
                    smm = p.c("tt")
                    p.wait("scalar", "tt", smm)
                    p.wait("scalar", "vv", e_consumed)
                    p.op("scalar", lambda e: e.activation(
                        out=s_E[:, 0:512], in_=pb[0][:, 0:512],
                        func=AF.Exp, bias=C.constv(0.0), scale=0.125), inc="ss")
                    e_read = p.c("ss")
                p.wait("vector", "ss", qftm_ok)
                p.wait("vector", "tt", tr_done_prev)
                pstride = s_qftm.ap[0][0]
                for t2_ in range(2):
                    in0 = bass.AP(s_qftm.tensor, s_qftm.offset + t2_ * 16,
                                  [[pstride, 128], [0, 16], [1, 16]])
                    in1 = bass.AP(s_qftm.tensor, s_qftm.offset + t2_ * 16,
                                  [[pstride, 128], [1, 16], [0, 16]])
                    outo = s_opqtm[:, t2_ * 256:(t2_ + 1) * 256].rearrange(
                        "p (j i) -> p j i", j=16)
                    p.op("vector", lambda e, in0=in0, in1=in1, outo=outo: e.scalar_tensor_tensor(
                        out=outo, in0=in0, scalar=0.5, in1=in1,
                        op0=ALU.mult, op1=ALU.mult), inc="vv" if t2_ == 1 else None)
                opqtm_ok = p.c("vv")
                qftm_cons = opqtm_ok
                if EMIT_WIN:
                    p.wait("vector", "ss", e_read)
                    mh = s_mfd if (g == 0 and cc == 0) else s_mpd
                    p.op("vector", lambda e, mh=mh: e.tensor_tensor(
                        out=s_E[:, 0:256], in0=s_E[:, 0:256], in1=mh, op=ALU.mult), inc=None)
                    p.op("vector", lambda e: e.tensor_tensor(
                        out=s_E[:, 256:512], in0=s_E[:, 256:512], in1=s_mpd, op=ALU.mult),
                        inc="vv")
                    e_ok = p.c("vv")
                    e_consumed = e_ok
                p.wait("tensor", "vv", opqtm_ok)
                p.wait("tensor", "ss", opq_evac)
                for i in range(4):
                    p.op("tensor", lambda e, i=i: e.matmul(
                        pb[4][:, i * 128:(i + 1) * 128], s_opqtm[:, i * 128:(i + 1) * 128],
                        C.s_id, start=True, stop=True), inc="tt" if i == 3 else None)
                tr_done = p.c("tt")
                tr_done_prev = tr_done
                p.wait("scalar", "tt", tr_done)
                p.wait("scalar", "tt", opq_users)
                p.op("scalar", lambda e: e.activation(
                    out=s_opqf[:, 0:512], in_=pb[4][:, 0:512],
                    func=AF.Identity, bias=C.constv(0.0), scale=1.0), inc="ss")
                opq_evac = p.c("ss")
                if C.debug and g == 0 and cc == 0 and h == 0:
                    p.wait("sync", "vv", opq_evac)
                    ffd = p.dma("sync", T["dbg_ff"][:, 0:256], s_qff[:, 0:256], sem="dqo")
                    ffd = p.dma("sync", T["dbg_ff"][:, 256:512], s_kff[:, 0:256], sem="dqo")
                    ffd = p.dma("sync", T["dbg_ff"][:, 512:1024], s_opqf[:, 0:512], sem="dqo")
                    ffd = p.dma("sync", T["dbg_ff"][:, 1024:1056], s_qftm[:, 0:32], sem="dqo")
                    ffd = p.dma("sync", T["dbg_ff"][:, 1056:1184], s_opqtm[:, 0:128], sem="dqo")
                    p.wait("vector", "dqo", ffd)
                    p.wait("tensor", "dqo", ffd)
                    p.wait("scalar", "dqo", ffd)

                if EMIT_LIN:
                    p.wait("tensor", "ss", sq_read)
                    p.wait("tensor", "ss", ffq_ok)
                    p.op("tensor", lambda e: e.matmul(
                        pb[1][:, 0:256], s_kff[:, 0:128], s_qff[:, 0:256],
                        start=True, stop=True), inc=None)
                    p.op("tensor", lambda e: e.matmul(
                        pb[1][:, 256:384], s_kff[:, 128:256], s_qff[:, 128:256],
                        start=True, stop=True), inc="tt")
                    stmm = p.c("tt")
                    p.wait("scalar", "tt", stmm)
                    p.wait("scalar", "vv", sq_consumed)
                    p.op("scalar", lambda e: e.activation(
                        out=s_sq[:, 0:384], in_=pb[1][:, 0:384],
                        func=AF.Square, bias=C.constv(-1.0), scale=1.0), inc="ss")
                    sq_read = p.c("ss")
                    p.wait("vector", "ss", sq_read)
                    p.op("vector", lambda e: e.scalar_tensor_tensor(
                        out=s_A[:, 0:384], in0=s_sq[:, 0:384],
                        scalar=0.5, in1=pb[1][:, 0:384], op0=ALU.mult, op1=ALU.add),
                        inc=None)
                    pa = s_A.ap[0][0]
                    a02 = bass.AP(s_A.tensor, s_A.offset, [[pa, 128], [256, 2], [1, 128]])
                    mt2 = bass.AP(C.MT.tensor, C.MT.offset, [[C.MT.ap[0][0], 128], [0, 2], [1, 128]])
                    p.op("vector", lambda e, a02=a02, mt2=mt2: e.tensor_tensor(
                        out=a02, in0=a02, in1=mt2, op=ALU.mult), inc="vv")
                    a_ok = p.c("vv")
                    sq_consumed = a_ok
                if EMIT_WIN:
                    p.wait("tensor", "vv", max(e_ok, vbw_ok))
                    for j2 in range(2):
                        p.op("tensor", lambda e, j2=j2: e.matmul(
                            pb[2][:, 0:128], s_vbw[:, j2 * 128:(j2 + 1) * 128],
                            s_E[:, j2 * 128:(j2 + 1) * 128],
                            start=(j2 == 0), stop=(j2 == 1)), inc=None)
                    for j2 in range(2):
                        p.op("tensor", lambda e, j2=j2: e.matmul(
                            pb[2][:, 128:256], s_vbw[:, (1 + j2) * 128:(2 + j2) * 128],
                            s_E[:, (2 + j2) * 128:(3 + j2) * 128],
                            start=(j2 == 0), stop=(j2 == 1)), inc="tt" if j2 == 1 else None)
                    avmm = p.c("tt")
                    p.wait("vector", "tt", avmm)
                    p.op("vector", lambda e: e.reciprocal(
                        s_zr[0:64, 0:256], pb[2][64:128, 0:256]), inc=None)
                    catslot = (8 + m) * 256
                    p.op("vector", lambda e, catslot=catslot, par=par: e.tensor_tensor(
                        out=s_cat[64 * par:64 * par + 64, catslot:catslot + 256],
                        in0=pb[2][0:64, 0:256], in1=s_zr[0:64, 0:256], op=ALU.mult),
                        inc="vv")
                    win_ok = p.c("vv")

                if EMIT_LIN:
                    p.wait("tensor", "vv", a_ok)
                    p.wait("tensor", "ss", opq_evac)
                    p.wait("tensor", "dsu", su_ld)
                    ymms0 = [
                        (s_vbw[:, 128:256],                          s_A[:, 0:128]),
                        (s_su[:, (3 * h) * 128:(3 * h + 1) * 128],   s_qff[:, 0:128]),
                        (s_su[:, (3 * h + 1) * 128:(3 * h + 2) * 128], s_opqf[:, 0:128]),
                        (s_su[:, (3 * h + 2) * 128:(3 * h + 3) * 128], s_opqf[:, 128:256]),
                    ]
                    for i, (stat, mov) in enumerate(ymms0):
                        p.op("tensor", lambda e, stat=stat, mov=mov, st=(i == 0),
                             sp=(i == len(ymms0) - 1): e.matmul(
                            pb[5][:, 0:128], stat, mov, start=st, stop=sp), inc=None)
                    ymms1 = [
                        (s_vbw[:, 128:256],                          s_A[:, 128:256]),
                        (s_vbw[:, 256:384],                          s_A[:, 256:384]),
                        (s_su[:, (3 * h) * 128:(3 * h + 1) * 128],   s_qff[:, 128:256]),
                        (s_su[:, (3 * h + 1) * 128:(3 * h + 2) * 128], s_opqf[:, 256:384]),
                        (s_su[:, (3 * h + 2) * 128:(3 * h + 3) * 128], s_opqf[:, 384:512]),
                    ]
                    for i, (stat, mov) in enumerate(ymms1):
                        p.op("tensor", lambda e, stat=stat, mov=mov, st=(i == 0),
                             sp=(i == len(ymms1) - 1): e.matmul(
                            pb[5][:, 128:256], stat, mov, start=st, stop=sp),
                            inc="tt" if i == len(ymms1) - 1 else None)
                    ymm = p.c("tt")
                    p.wait("vector", "tt", ymm)
                    p.op("vector", lambda e: e.tensor_scalar(
                        out=s_zt[0:64, 0:256], in0=pb[5][64:128, 0:256],
                        scalar1=EPS_LIN, scalar2=None, op0=ALU.add), inc=None)
                    p.op("vector", lambda e: e.reciprocal(
                        s_zr[0:64, 0:256], s_zt[0:64, 0:256]), inc=None)
                    catslot2 = m * 256
                    p.op("vector", lambda e, catslot2=catslot2, par=par: e.tensor_tensor(
                        out=s_cat[64 * par:64 * par + 64, catslot2:catslot2 + 256],
                        in0=pb[5][0:64, 0:256], in1=s_zr[0:64, 0:256], op=ALU.mult),
                        inc="vv")
                    lin_ok = p.c("vv")
                vbw_users = p.c("tt")
                kffqff_users = p.c("tt")
                su_users = p.c("tt")
                opq_users = p.c("tt")
            cat_ok = p.c("vv")
            if C.debug and g == 0 and cc == 0:
                p.wait("sync", "vv", cat_ok)
                catdump = p.dma("sync", T["dbg_cat"], s_cat, sem="dqo")
                p.wait("vector", "dqo", catdump)

            if PARANOID:
                p.sync3("tensor", "vector", "scalar")
            p.wait("sync", "vv", yh_built_all)
            xc_ld = p.dma("sync", s_xc[:, 0:2048].rearrange("p (c t) -> p c t", c=8),
                          dap(T["x_t"], 128 + g * 512 + cc * 256,
                              [[NTT * 128, 128], [128 * NTT * 128, 8], [1, 256]]), sem="dxc")
            p.wait("tensor", "vv", cat_ok)
            p.wait("tensor", "ss", max(sq_read, e_read))
            wo_ld = [0, 0]
            woc_users = [0, 0]
            for wave in range(2):
                for kc in range(16):
                    wb = kc % 2
                    if PARANOID:
                        p.sync3("tensor", "vector")
                    p.wait("sync", "tt", woc_users[wb])
                    wo_ld[wb] = p.dma("sync", s_woc[wb][:, 0:512],
                                      T["wout"][kc * 128:(kc + 1) * 128, wave * 512:(wave + 1) * 512],
                                      sem=f"dwo{wb}")
                    p.wait("tensor", f"dwo{wb}", wo_ld[wb])
                    for mo in range(4):
                        p.op("tensor", lambda e, kc=kc, mo=mo, wb=wb: e.matmul(
                            pb[mo][:, 0:256], s_woc[wb][:, mo * 128:(mo + 1) * 128],
                            s_cat[:, kc * 256:(kc + 1) * 256],
                            start=(kc == 0), stop=(kc == 15)),
                            inc="tt" if mo == 3 else None)
                    woc_users[wb] = p.c("tt")
                womm = p.c("tt")
                p.wait("vector", "tt", womm)
                p.wait("vector", "dxc", xc_ld)
                p.wait("vector", "dy1", yh_spill)
                for mo in range(4):
                    mg = wave * 4 + mo
                    p.op("vector", lambda e, mo=mo, mg=mg: e.scalar_tensor_tensor(
                        out=s_yh[:, mg * 256:(mg + 1) * 256],
                        in0=s_xc[:, mg * 256:(mg + 1) * 256], scalar=2.0,
                        in1=pb[mo][:, 0:256], op0=ALU.mult, op1=ALU.add),
                        inc="vv" if mo == 3 else None)
                yh_built = p.c("vv")
                p.wait("tensor", "vv", yh_built)
            wout_done = p.c("tt")
            cat_users = wout_done
            yh_built_all = yh_built

            s2mm = 0
            p.wait("scalar", "vv", yh_built)
            for mg in range(8):
                p.wait("scalar", "tt", s2mm)
                p.op("scalar", lambda e, mg=mg: e.activation(
                    out=s_ysq[:, 0:256], in_=s_yh[:, mg * 256:(mg + 1) * 256],
                    func=AF.Square, bias=C.constv(0.0), scale=1.0), inc="ss")
                sqd = p.c("ss")
                p.op("tensor", lambda e, mg=mg: e.matmul(
                    pb[4][:, 0:256], C.s_ones, s_yh[:, mg * 256:(mg + 1) * 256],
                    start=(mg == 0), stop=(mg == 7)), inc=None)
                p.wait("tensor", "ss", sqd)
                p.op("tensor", lambda e, mg=mg: e.matmul(
                    pb[5][:, 0:256], C.s_ones, s_ysq[:, 0:256],
                    start=(mg == 0), stop=(mg == 7)), inc="tt")
                s2mm = p.c("tt")
            statmm = p.c("tt")
            if C.debug and g == 0 and cc == 0:
                p.wait("sync", "vv", p.c("vv"))
                p.wait("sync", "tt", statmm)
                dbgd = p.dma("sync", T["dbg_yh"][:, 0:2048], s_yh[:, 0:2048], sem="dqo")
                p.wait("vector", "dqo", dbgd)
            p.wait("vector", "tt", statmm)
            p.op("vector", lambda e: e.tensor_scalar(
                out=s_mb[:, 0:256], in0=pb[4][:, 0:256], scalar1=1.0 / D, scalar2=None,
                op0=ALU.mult), inc=None)
            p.op("vector", lambda e: e.tensor_scalar(
                out=s_t2[:, 0:256], in0=pb[5][:, 0:256], scalar1=1.0 / D, scalar2=None,
                op0=ALU.mult), inc=None)
            p.op("vector", lambda e: e.tensor_tensor(
                out=s_u[:, 0:256], in0=s_mb[:, 0:256], in1=s_mb[:, 0:256], op=ALU.mult),
                inc=None)
            p.op("vector", lambda e: e.scalar_tensor_tensor(
                out=s_var[:, 0:256], in0=s_t2[:, 0:256], scalar=EPS_LN, in1=s_u[:, 0:256],
                op0=ALU.add, op1=ALU.subtract), inc="vv")
            var_ok = p.c("vv")
            p.wait("scalar", "vv", var_ok)
            p.op("scalar", lambda e: e.activation(
                out=s_r[:, 0:256], in_=s_var[:, 0:256], func=AF.Ln,
                bias=C.constv(0.0), scale=1.0), inc=None)
            p.op("scalar", lambda e: e.activation(
                out=s_r[:, 0:256], in_=s_r[:, 0:256], func=AF.Exp,
                bias=C.constv(0.0), scale=-0.5), inc="ss")
            r_ok = p.c("ss")
            if C.debug and g == 0 and cc == 0:
                p.wait("sync", "ss", r_ok)
                p.dma("sync", T["dbg_yh"][:, 2048:2304], s_mb[:, 0:256])
                p.dma("sync", T["dbg_yh"][:, 2304:2560], s_r[:, 0:256])
                p.dma("sync", T["dbg_yh"][:, 2560:2816], s_var[:, 0:256])
            p.wait("vector", "ss", r_ok)
            p.wait("vector", "dy1", yh_spill)
            for mg in range(8):
                p.op("vector", lambda e, mg=mg: e.tensor_tensor(
                    out=s_ysq[:, 0:256], in0=s_yh[:, mg * 256:(mg + 1) * 256],
                    in1=s_mb[:, 0:256], op=ALU.subtract), inc=None)
                p.op("vector", lambda e, mg=mg: e.tensor_tensor(
                    out=s_yh[:, mg * 256:(mg + 1) * 256], in0=s_ysq[:, 0:256],
                    in1=s_r[:, 0:256], op=ALU.mult), inc=None)
                p.op("vector", lambda e, mg=mg: e.tensor_reduce(
                    out=s_red[:, 0:1], in_=s_yh[:, mg * 256:(mg + 1) * 256],
                    axis=mybir.AxisListType.X, op=ALU.max, apply_absolute_value=True),
                    inc=None)
                p.op("vector", lambda e: e.tensor_tensor(
                    out=s_m1[:, 0:1], in0=s_m1[:, 0:1], in1=s_red[:, 0:1], op=ALU.max),
                    inc="vv" if mg == 7 else None)
            y1_ok = p.c("vv")
            p.wait("sync", "vv", y1_ok)
            yh_spill = p.dma("sync",
                dap(SP["y1_sp"], g * 512 + cc * 256, [[8 * TCORE, 128], [TCORE, 8], [1, 256]]),
                s_yh[:, 0:2048].rearrange("p (c t) -> p c t", c=8), sem="dy1")
        qg_free = p.c("tt")
        hg_users_prev = p.c("tt")
        vg_users_prev = p.c("vv")

    p.wait("sync", "vv", p.c("vv"))
    mxs = p.dma("sync", dap(SP["mx_sp"], 0, [[1, 128], [1, 1]]), s_m1[:, 0:1], sem="dqo")
    p.wait("sync", "dqo", mxs)
    mxl = p.dma("sync", s_mb[0:1, 0:128], SP["mx_sp"][0:1, 0:128], sem="dqo")
    p.wait("vector", "dqo", mxl)
    p.op("vector", lambda e: e.tensor_reduce(
        out=s_red[0:1, 0:1], in_=s_mb[0:1, 0:128], axis=mybir.AxisListType.X,
        op=ALU.max), inc=None)
    if C.debug:
        p.op("vector", lambda e: e.tensor_copy(out=s_mb[0:1, 128:256], in_=s_mb[0:1, 0:128]), inc="vv")
        dchk = p.c("vv")
        p.wait("sync", "vv", dchk)
        p.dma("sync", T["dbg_t"][0:1, 0:128], s_mb[0:1, 128:256], sem="dqo")
    pst = s_red.ap[0][0]
    p.op("vector", lambda e: e.tensor_copy(
        out=s_mb[0:1, 0:8],
        in_=bass.AP(s_red.tensor, s_red.offset, [[pst, 1], [0, 8]])), inc="vv")
    m1rep = p.c("vv")
    p.wait("sync", "vv", m1rep)
    agm_in = p.dma("sync", SP["ag_m1_in"].ap(), s_mb[0:1, 0:8], sem="dqo")
    p.wait("sync", "dqo", agm_in)
    p.wait("gpsimd", "dqo", agm_in)
    p.wait("gpsimd", "cc", C.ag_state_done)
    p.op("gpsimd", lambda e: e.collective_compute(
        "AllGather", ALU.bypass, replica_groups=[list(range(8))],
        ins=[SP["ag_m1_in"].ap()], outs=[SP["ag_m1_out"].ap()]), inc="cc")
    agm1 = p.c("cc")
    C.agm1 = agm1
    C.gmax1_ok = p.c("vv")
    if C.debug:
        p.op("vector", lambda e: e.tensor_copy(out=s_mb[:, 16:17], in_=s_m1[:, 0:1]), inc="vv")
        chain_ok = p.c("vv")
        p.wait("sync", "vv", chain_ok)
        p.wait("sync", "cc", agm1)
        dd = p.dma("sync", T["dbg_chain"][:, 0:8], s_mb[:, 16:24], sem="dqo")
        dd = p.dma("sync", T["dbg_chain"][:, 8:16], s_mb[:, 0:8], sem="dqo")
        dd = p.dma("sync", dap(T["dbg_chain"], 16, [[32, 1], [1, 8]]),
                   dap(SP["ag_m1_in"].ap(), 0, [[8, 1], [1, 8]]), sem="dqo")
        dd = p.dma("sync", dap(T["dbg_chain"], 24, [[32, 8], [1, 8]]),
                   dap(SP["ag_m1_out"].ap(), 0, [[8, 8], [1, 8]]), sem="dqo")
        p.wait("vector", "dqo", dd)

    C.yh_spill_last = yh_spill
    if C.debug:
        p.wait("sync", "vv", C.gmax1_ok)
        p.wait("sync", "dy1", yh_spill)
        p.dma("sync", T["dbg_su"][:, 0:6144], SP["pref_sp"][:, 4 * 6144:5 * 6144], sem="dqo")
        p.dma("sync", T["dbg_su"][:, 6144:12288],
              dap(SP["ag_st_out"].ap(), 0, [[6144, 128], [1, 6144]]), sem="dqo")
        p.dma("sync", T["dbg_y1"], SP["y1_sp"], sem="dqo")


def emit_phase_f(C):
    C.s_clipx = C.s_clipx[:, 0:2]
    C.s_clipdn = C.s_clipdn[:, 0:2]
    C.p.barrier()
    emit_phase_f1(C)
    emit_phase_f2(C)


def emit_phase_f1(C):
    p, pb, T, SP, nc = C.p, C.pb, C.T, C.SP, C.nc
    QM = 127.0
    CLIPLO = 128.0
    CLIPHI = 383.99996948242188

    with ExitStack() as es:
        af = Arena(es.enter_context(nc.sbuf_tensor("af1", [128, 23040], F32)), 23040)
        ah = Arena(es.enter_context(nc.sbuf_tensor("af1h", [128, 36608], BF16)), 36608)
        ai = Arena(es.enter_context(nc.sbuf_tensor("ai1", [128, 2944], I32)), 2944)

        s_g64 = af.alloc(64, "g64")
        s_sx = af.alloc(2, "sx")
        s_invx = af.alloc(2, "invx")
        s_tmp1 = af.alloc(4, "tmp1")
        s_mpad = af.alloc(128, "mpad")
        s_z128 = af.alloc(2, "z128")
        p.memset("vector", s_z128, 0.0)
        ci2 = ai.alloc(2048, "ci32")

        def repmax_scalar(ag_out, sem_done, out_col):
            ld = p.dma("sync", s_g64[0:1, 0:64], dap(ag_out.ap(), 0, [[64, 1], [1, 64]]),
                       sem="dqo")
            p.wait("vector", "dqo", ld)
            p.wait("vector", "ms", p.c("ms"))
            p.op("vector", lambda e: e.tensor_reduce(
                out=s_z128[0:1, 0:1], in_=s_g64[0:1, 0:64], axis=mybir.AxisListType.X,
                op=ALU.max), inc=None)
            p.op("vector", lambda e: e.tensor_reduce(
                out=s_z128[0:1, 1:2], in_=s_g64[0:1, 0:64], axis=mybir.AxisListType.X,
                op=ALU.max), inc="vv")
            rok = p.c("vv")
            p.wait("tensor", "vv", rok)
            p.op("tensor", lambda e: e.matmul(pb[6][:, 0:2], C.s_ones, s_z128[:, 0:2],
                                              start=True, stop=True), inc="tt")
            rok = p.c("tt")
            p.wait("vector", "tt", rok)
            p.op("vector", lambda e, out_col=out_col: e.tensor_copy(
                out=out_col, in_=pb[6][:, 0:2]), inc="vv")
            return p.c("vv")

        def sig_scale(clip_ap, gmax_col, s_col, inv_col):
            p.op("scalar", lambda e: e.activation(out=s_col, in_=clip_ap, func=AF.Exp,
                                                  bias=C.constv(0.0), scale=-1.0), inc="ss")
            p.wait("vector", "ss")
            p.op("vector", lambda e: e.tensor_scalar(out=s_col, in0=s_col,
                                                     scalar1=1.0, scalar2=None, op0=ALU.add), inc=None)
            p.op("vector", lambda e: e.reciprocal(s_col, s_col), inc=None)
            p.op("vector", lambda e: e.tensor_tensor(out=s_col, in0=s_col,
                                                     in1=gmax_col, op=ALU.mult), inc=None)
            p.op("vector", lambda e: e.tensor_scalar(out=s_col, in0=s_col,
                                                     scalar1=1.0 / QM, scalar2=None, op0=ALU.mult), inc=None)
            p.op("vector", lambda e: e.reciprocal(inv_col, s_col), inc="vv")
            return p.c("vv")

        def quant_to_int16(u_ap, cu_ap, out16_ap, n, clip, ci=None):
            civ = ci if ci is not None else ci2[:, 0:n]
            if clip:
                p.op("vector", lambda e: e.tensor_scalar(
                    out=cu_ap, in0=u_ap, scalar1=CLIPLO, scalar2=CLIPHI,
                    op0=ALU.max, op1=ALU.min), inc=None)
                cua = cu_ap
            else:
                cua = u_ap
            p.op("vector", lambda e, cua=cua, civ=civ: e.tensor_copy(out=civ, in_=cua), inc=None)
            dd = cu_ap if not clip else u_ap
            p.op("vector", lambda e, cua=cua, dd=dd, civ=civ: e.tensor_tensor(
                out=dd, in0=civ, in1=cua, op=ALU.is_gt), inc=None)
            p.op("vector", lambda e, dd=dd, civ=civ: e.scalar_tensor_tensor(
                out=out16_ap, in0=civ, scalar=-256.0, in1=dd,
                op0=ALU.add, op1=ALU.subtract), inc="vv")
            return p.c("vv")

        p.wait("sync", "cc", C.agm1)
        g1_ok = repmax_scalar(SP["ag_m1_out"], None, s_tmp1[:, 0:2])
        p.wait("scalar", "vv", g1_ok)
        sx_ok = sig_scale(C.s_clipx, s_tmp1[:, 0:2], s_sx[:, 0:2], s_invx[:, 0:2])
        if C.debug:
            s_dbgsc = af.alloc(16, "dbgsc")
            p.op("vector", lambda e: e.tensor_copy(out=s_dbgsc[:, 0:1], in_=s_tmp1[:, 0:1]), inc=None)
            p.op("vector", lambda e: e.tensor_copy(out=s_dbgsc[:, 1:2], in_=s_sx[:, 0:1]), inc=None)
            p.op("vector", lambda e: e.tensor_copy(out=s_dbgsc[:, 2:3], in_=s_invx[:, 0:1]), inc=None)
            p.op("vector", lambda e: e.tensor_copy(out=s_dbgsc[:, 3:5], in_=C.s_clipx), inc="vv")
            dsc = p.c("vv")
            p.wait("sync", "vv", dsc)
            p.dma("sync", T["dbg_sc"][:, 0:8], s_dbgsc[:, 0:8], sem="dqo")

        s_y1g = af.alloc(4096, "y1g")
        s_u = af.alloc(2048, "uq")
        s_cu = af.alloc(2048, "cuq")
        s_xq = ah.alloc(16384, "xq")
        xqp = s_xq.ap[0][0]
        y1q_ld = 0
        uq_cons = 0
        for tg in range(NG):
            p.wait("sync", "ss", p.c("ss"))
            y1q_ld = p.dma("sync", s_y1g,
                           dap(SP["y1_sp"], tg * 512, [[8 * TCORE, 128], [TCORE, 8], [1, 512]]),
                           sem="dyq")
            p.wait("scalar", "dyq", y1q_ld)
            p.wait("scalar", "vv", sx_ok)
            for hb in range(2):
                sl = slice(hb * 2048, (hb + 1) * 2048)
                p.wait("scalar", "vv", uq_cons)
                p.op("scalar", lambda e, sl=sl: e.activation(
                    out=s_u[:, 0:2048], in_=s_y1g[:, sl], func=AF.Identity,
                    bias=C.constv(256.5), scale=s_invx[:, 0:1]), inc="ss")
                p.wait("vector", "ss")
                out16 = bass.AP(s_xq.tensor, s_xq.offset + (hb * 4) * 2048 + tg * 512,
                                [[xqp, 128], [2048, 4], [1, 512]])
                quant_to_int16(s_u[:, 0:2048].rearrange("p (c t) -> p c t", c=4),
                               s_cu[:, 0:2048].rearrange("p (c t) -> p c t", c=4),
                               out16, 2048, clip=True,
                               ci=ci2[:, 0:2048].rearrange("p (c t) -> p c t", c=4))
                uq_cons = p.c("vv")
        xq_ok = p.c("vv")
        if C.debug:
            p.op("vector", lambda e: e.tensor_copy(out=s_g64[:, 8:9], in_=s_sx[:, 0:1]), inc=None)
            p.op("vector", lambda e: e.tensor_copy(out=s_g64[:, 9:10], in_=s_invx[:, 0:1]), inc=None)
            p.op("vector", lambda e: e.tensor_copy(out=s_g64[:, 10:11], in_=s_tmp1[:, 0:1]), inc="vv")
            dbgsc = p.c("vv")
            p.wait("sync", "vv", max(xq_ok, dbgsc))
            dd = p.dma("sync", T["dbg_xq"], s_xq[:, 0:16384], sem="dqo")
            dd = p.dma("sync", T["dbg_sc"][:, 0:8], s_g64[:, 8:16], sem="dqo")
            p.wait("vector", "dqo", dd)
            p.wait("scalar", "dqo", dd)

        s_wg = [af.alloc(1024, "wg0"), af.alloc(1024, "wg1")]
        s_wu = [af.alloc(1024, "wu0"), af.alloc(1024, "wu1")]
        s_wqu2 = [af.alloc(1024, "wqua"), af.alloc(1024, "wqub")]
        s_wqd2 = [af.alloc(1024, "wqda"), af.alloc(1024, "wqdb")]
        s_stash = af.alloc(2048, "sgstash")
        s_up2 = [af.alloc(512, "upbuf0"), af.alloc(512, "upbuf1")]
        s_t2 = [af.alloc(512, "tbuf0"), af.alloc(512, "tbuf1")]
        s_tmax = af.alloc(2, "tmax")
        s_sgu = af.alloc(16, "sgu")
        s_tsc = af.alloc(16, "tsc")
        w162 = [ah.alloc(1024, "w16a"), ah.alloc(1024, "w16b")]
        wgT2 = [ah.alloc(1024, "wgT0"), ah.alloc(1024, "wgT1")]
        wuT2 = [ah.alloc(1024, "wuT0"), ah.alloc(1024, "wuT1")]

        p.memset("vector", s_tmax, 0.0)
        p.wait("vector", "ms")

        wg_ld = [0, 0]
        wg_users = [0, 0]
        wg_users_ss = [0, 0]
        bank_ss = [0, 0, 0, 0]
        t_sp_done = [0, 0]
        w16_users = [0, 0]
        wT_users = [0, 0]
        wT_ok = [0, 0]
        wb45_evac = 0
        i1_done = [0, 0, 0, 0]
        sgu_cons = [0, 0, 0, 0]
        stash_users = 0
        up_cons = [0, 0]

        for step in range(NKI + 1):
            if step < NKI:
                ki = step
                wb = ki % 2
                p.wait("sync", "vv", wg_users[wb])
                p.wait("sync", "ss", wg_users_ss[wb])
                p.dma("sync", s_wg[wb], T["w_gate"][ki * 128:(ki + 1) * 128, :], sem=f"dwg{wb}")
                p.dma("sync", s_wu[wb], T["w_up"][ki * 128:(ki + 1) * 128, :], sem=f"dwg{wb}")
                wg_ld[wb] = p.c(f"dwg{wb}")
                p.wait("vector", f"dwg{wb}", wg_ld[wb])
                for wi, wsrc, wTdst in ((0, s_wg[wb], wgT2[wb]), (1, s_wu[wb], wuT2[wb])):
                    q4 = wb * 2 + wi
                    tsc = s_tsc[:, q4 * 4:q4 * 4 + 4]
                    sgs = (ki % 4) * 4 + 2 * wi
                    p.wait("vector", "ss", i1_done[q4])
                    p.op("vector", lambda e, wsrc=wsrc, tsc=tsc: e.tensor_reduce(
                        out=tsc[:, 0:1], in_=wsrc, axis=mybir.AxisListType.X, op=ALU.max,
                        apply_absolute_value=True), inc=None)
                    p.op("vector", lambda e, wsrc=wsrc, tsc=tsc: e.tensor_reduce(
                        out=tsc[:, 1:2], in_=wsrc, axis=mybir.AxisListType.X, op=ALU.max,
                        apply_absolute_value=True), inc=None)
                    p.wait("vector", "ss", sgu_cons[ki % 4])
                    p.op("vector", lambda e, tsc=tsc, sgs=sgs: e.tensor_scalar(
                        out=s_sgu[:, sgs:sgs + 2], in0=tsc[:, 0:2], scalar1=1.0 / QM,
                        scalar2=None, op0=ALU.mult), inc=None)
                    p.op("vector", lambda e, sgs=sgs: e.tensor_tensor(
                        out=s_sgu[:, sgs:sgs + 2], in0=s_sgu[:, sgs:sgs + 2],
                        in1=s_sx[:, 0:2], op=ALU.mult), inc=None)
                    p.op("vector", lambda e, tsc=tsc: e.reciprocal(tsc[:, 0:2], tsc[:, 0:2]),
                         inc=None)
                    p.op("vector", lambda e, tsc=tsc: e.tensor_scalar(
                        out=tsc[:, 0:2], in0=tsc[:, 0:2], scalar1=QM, scalar2=None,
                        op0=ALU.mult), inc="vv")
                    sc_ok = p.c("vv")
                    p.wait("scalar", "vv", sc_ok)
                    p.op("scalar", lambda e, wsrc=wsrc, wb=wb, tsc=tsc: e.activation(
                        out=s_wqu2[wb][:, 0:1024], in_=wsrc, func=AF.Identity,
                        bias=C.constv(256.5), scale=tsc[:, 0:1]), inc="ss")
                    i1_done[q4] = p.c("ss")
                    p.wait("vector", "ss", i1_done[q4])
                    p.wait("vector", "tt", w16_users[wb])
                    w_ok = quant_to_int16(s_wqu2[wb][:, 0:1024], s_wqd2[wb][:, 0:1024],
                                          w162[wb][:, 0:1024], 1024, clip=False)
                    p.wait("tensor", "vv", max(w_ok, wb45_evac))
                    for i in range(8):
                        kb = 4 + i // 4
                        p.op("tensor", lambda e, i=i, kb=kb, wb=wb: e.matmul(
                            pb[kb][:, (i % 4) * 128:(i % 4) * 128 + 128],
                            w162[wb][:, i * 128:(i + 1) * 128], C.s_id16,
                            start=True, stop=True), inc="tt" if i == 7 else None)
                    w16_users[wb] = p.c("tt")
                    p.wait("vector", "tt", w16_users[wb])
                    p.wait("vector", "tt", wT_users[wb])
                    for kb in (4, 5):
                        p.op("vector", lambda e, kb=kb, wTdst=wTdst: e.tensor_copy(
                            out=wTdst[:, (kb - 4) * 512:(kb - 3) * 512], in_=pb[kb][:, 0:512]),
                            inc="vv" if kb == 5 else None)
                    wT_ok[wb] = p.c("vv")
                    wb45_evac = wT_ok[wb]
                wg_users[wb] = p.c("vv")
                wg_users_ss[wb] = p.c("ss")

            if step >= 1:
                ki = step - 1
                wb = ki % 2
                p.wait("tensor", "vv", max(wT_ok[wb], xq_ok))
                for wi in (0, 1):
                    wTdst = wgT2[wb] if wi == 0 else wuT2[wb]
                    sgs = (ki % 4) * 4 + 2 * wi
                    for tg in range(NG):
                        bank = (wi * 2) + (tg % 2)
                        p.wait("tensor", "ss", bank_ss[bank])
                        for hc in range(8):
                            p.op("tensor", lambda e, hc=hc, tg=tg, bank=bank, wTdst=wTdst: e.matmul(
                                pb[bank][:, 0:512], wTdst[:, hc * 128:(hc + 1) * 128],
                                s_xq[:, hc * 2048 + tg * 512: hc * 2048 + tg * 512 + 512],
                                start=(hc == 0), stop=(hc == 7)), inc="tt" if hc == 7 else None)
                        gmm = p.c("tt")
                        p.wait("scalar", "tt", gmm)
                        if wi == 0:
                            p.wait("scalar", "vv", stash_users)
                            p.op("scalar", lambda e, tg=tg, bank=bank, sgs=sgs: e.activation(
                                out=s_stash[:, tg * 512:(tg + 1) * 512], in_=pb[bank][:, 0:512],
                                func=AF.Silu, bias=C.constv(0.0), scale=s_sgu[:, sgs:sgs + 1]),
                                inc="ss")
                            bank_ss[bank] = p.c("ss")
                        else:
                            tb = (ki * NG + tg) % 2
                            p.wait("scalar", "vv", up_cons[tb])
                            p.op("scalar", lambda e, tg=tg, bank=bank, sgs=sgs, tb=tb: e.activation(
                                out=s_up2[tb][:, 0:512], in_=pb[bank][:, 0:512],
                                func=AF.Identity, bias=C.constv(0.0),
                                scale=s_sgu[:, sgs:sgs + 1]), inc="ss")
                            up_ok = p.c("ss")
                            bank_ss[bank] = up_ok
                            p.wait("vector", "ss", up_ok)
                            p.wait("vector", f"dtw{tb}", t_sp_done[tb])
                            p.op("vector", lambda e, tg=tg, tb=tb: e.tensor_tensor(
                                out=s_t2[tb][:, 0:512], in0=s_stash[:, tg * 512:(tg + 1) * 512],
                                in1=s_up2[tb][:, 0:512], op=ALU.mult), inc=None)
                            p.op("vector", lambda e, tb=tb: e.tensor_reduce(
                                out=s_tmp1[:, 3:4], in_=s_t2[tb][:, 0:512],
                                axis=mybir.AxisListType.X,
                                op=ALU.max, apply_absolute_value=True), inc=None)
                            p.op("vector", lambda e: e.tensor_tensor(
                                out=s_tmax[:, 0:1], in0=s_tmax[:, 0:1], in1=s_tmp1[:, 3:4],
                                op=ALU.max), inc="vv")
                            t_ok = p.c("vv")
                            up_cons[tb] = t_ok
                            if tg == NG - 1:
                                stash_users = t_ok
                            p.wait("sync", "vv", t_ok)
                            t_sp_done[tb] = p.dma("sync",
                                SP["t_sp"][:, tg * NKI * 512 + ki * 512:
                                           tg * NKI * 512 + ki * 512 + 512],
                                s_t2[tb][:, 0:512], sem=f"dtw{tb}")
                wT_users[wb] = p.c("tt")
                sgu_cons[ki % 4] = p.c("ss")

        p.wait("sync", "vv", p.c("vv"))
        mxs = p.dma("sync", dap(SP["mx_sp"], 0, [[1, 128], [1, 1]]), s_tmax[:, 0:1], sem="dqo")
        p.wait("sync", "dqo", mxs)
        mxl = p.dma("sync", s_mpad[0:1, 0:128], SP["mx_sp"][0:1, 0:128], sem="dqo")
        p.wait("vector", "dqo", mxl)
        p.op("vector", lambda e: e.tensor_reduce(
            out=s_tmp1[0:1, 0:1], in_=s_mpad[0:1, 0:128], axis=mybir.AxisListType.X,
            op=ALU.max), inc=None)
        pst = s_tmp1.ap[0][0]
        p.op("vector", lambda e: e.tensor_copy(
            out=s_g64[0:1, 0:8],
            in_=bass.AP(s_tmp1.tensor, s_tmp1.offset, [[pst, 1], [0, 8]])), inc="vv")
        m2rep = p.c("vv")
        p.wait("sync", "vv", m2rep)
        agm2_in = p.dma("sync", SP["ag_m2_in"].ap(), s_g64[0:1, 0:8], sem="dqo")
        p.wait("sync", "dqo", agm2_in)
        p.wait("gpsimd", "dqo", agm2_in)
        p.op("gpsimd", lambda e: e.collective_compute(
            "AllGather", ALU.bypass, replica_groups=[list(range(8))],
            ins=[SP["ag_m2_in"].ap()], outs=[SP["ag_m2_out"].ap()]), inc="cc")
        agm2 = p.c("cc")
        p.wait("sync", "cc", agm2)
        p.wait("sync", "dtw0", p.c("dtw0"))
        p.wait("sync", "dtw1", p.c("dtw1"))
        if C.debug:
            p.dma("sync", T["dbg_t"], SP["t_sp"][:, 0:2048], sem="dqo")
        g2_ok = repmax_scalar(SP["ag_m2_out"], None, s_tmp1[:, 0:2])
        s_s2 = af.alloc(4, "s2")
        s_inv2 = af.alloc(4, "inv2")
        p.wait("scalar", "vv", g2_ok)
        s2_ok = sig_scale(C.s_clipdn, s_tmp1[:, 0:2], s_s2[:, 0:2], s_inv2[:, 0:2])
        p.op("vector", lambda e: e.tensor_copy(out=C.s_scal[:, 0:2], in_=s_s2[:, 0:2]), inc=None)
        p.op("vector", lambda e: e.tensor_copy(out=C.s_scal[:, 2:4], in_=s_inv2[:, 0:2]), inc="vv")
        if C.debug:
            scok = p.c("vv")
            p.wait("sync", "vv", scok)
            dd = p.dma("sync", T["dbg_sc"][:, 8:10], C.s_scal[:, 0:2], sem="dqo")
            p.wait("vector", "dqo", dd)
        p.barrier()


def emit_phase_f2(C):
    p, pb, T, SP, nc = C.p, C.pb, C.T, C.SP, C.nc
    QM = 127.0
    CLIPLO = 128.0
    CLIPHI = 383.99996948242188

    with ExitStack() as es:
        af = Arena(es.enter_context(nc.sbuf_tensor("af2", [128, 23040], F32)), 23040)
        ah = Arena(es.enter_context(nc.sbuf_tensor("af2h", [128, 36608], BF16)), 36608)
        ai = Arena(es.enter_context(nc.sbuf_tensor("ai2", [128, 2944], I32)), 2944)
        ci2 = ai.alloc(2944, "ci32b")
        s_s2 = C.s_scal[:, 0:1]
        s_inv2 = C.s_scal[:, 2:3]

        if C.debug:
            dd = p.dma("sync", T["dbg_sc"][:, 10:14], C.s_scal[:, 0:4], sem="dqo")
        h2 = ah.alloc(11264, "h2")
        wdT = ah.alloc(22528, "wdT")
        w16d = ah.alloc(2816, "w16d")
        s_wd = af.alloc(2816, "wd")
        s_u = af.alloc(2816, "ud")
        s_cu = af.alloc(2816, "cud")
        s_sdo = af.alloc(32, "sdo")
        s_tslab = af.alloc(2816, "tslab")
        s_y1tg = af.alloc(4096, "y1tg")
        s_y2 = af.alloc(4096, "y2")
        s_mb = af.alloc(512, "mb2")
        s_t2s = af.alloc(512, "t22")
        s_uu = af.alloc(512, "uu2")
        s_var = af.alloc(512, "var2")
        s_r = af.alloc(512, "r2")
        s_ysq = af.alloc(512, "ysq2")
        s_tmp1 = af.alloc(4, "tmp1b")

        def quant_to_int16b(u_ap, cu_ap, out16_ap, n, clip):
            if clip:
                p.op("vector", lambda e: e.tensor_scalar(
                    out=cu_ap, in0=u_ap, scalar1=CLIPLO, scalar2=CLIPHI,
                    op0=ALU.max, op1=ALU.min), inc=None)
                cua = cu_ap
            else:
                cua = u_ap
            p.op("vector", lambda e, cua=cua: e.tensor_copy(out=ci2[:, 0:n], in_=cua), inc=None)
            dd = cu_ap if not clip else u_ap
            p.op("vector", lambda e, cua=cua, dd=dd: e.tensor_tensor(
                out=dd, in0=ci2[:, 0:n], in1=cua, op=ALU.is_gt), inc=None)
            p.op("vector", lambda e, dd=dd: e.scalar_tensor_tensor(
                out=out16_ap, in0=ci2[:, 0:n], scalar=-256.0, in1=dd,
                op0=ALU.add, op1=ALU.subtract), inc="vv")
            return p.c("vv")

        wd_ld = 0
        w16d_users = 0
        for hc in range(8):
            p.wait("sync", "ss", p.c("ss"))
            wd_ld = p.dma("sync", s_wd, T["w_down"][hc * 128:(hc + 1) * 128, :], sem="dwd")
            p.wait("vector", "dwd", wd_ld)
            p.op("vector", lambda e: e.tensor_reduce(
                out=s_tmp1[:, 2:3], in_=s_wd, axis=mybir.AxisListType.X, op=ALU.max,
                apply_absolute_value=True), inc=None)
            p.op("vector", lambda e: e.tensor_reduce(
                out=s_tmp1[:, 3:4], in_=s_wd, axis=mybir.AxisListType.X, op=ALU.max,
                apply_absolute_value=True), inc=None)
            p.op("vector", lambda e, hc=hc: e.tensor_scalar(
                out=s_tmp1[:, 0:2], in0=s_tmp1[:, 2:4], scalar1=1.0 / QM,
                scalar2=None, op0=ALU.mult), inc=None)
            p.op("vector", lambda e, hc=hc: e.scalar_tensor_tensor(
                out=s_sdo[:, 2 * hc:2 * hc + 2],
                in0=s_tmp1[:, 0:2], scalar=1.0, in1=C.s_scal[:, 0:2],
                op0=ALU.mult, op1=ALU.mult), inc=None)
            p.op("vector", lambda e: e.reciprocal(s_tmp1[:, 2:4], s_tmp1[:, 2:4]), inc=None)
            p.op("vector", lambda e: e.tensor_scalar(
                out=s_tmp1[:, 2:4], in0=s_tmp1[:, 2:4], scalar1=QM, scalar2=None,
                op0=ALU.mult), inc="vv")
            iv_ok = p.c("vv")
            p.wait("scalar", "vv", iv_ok)
            p.op("scalar", lambda e: e.activation(
                out=s_u[:, 0:2816], in_=s_wd, func=AF.Identity,
                bias=C.constv(256.5), scale=s_tmp1[:, 2:3]), inc="ss")
            p.wait("vector", "ss")
            p.wait("vector", "tt", w16d_users)
            w_ok = quant_to_int16b(s_u[:, 0:2816], s_cu[:, 0:2816], w16d[:, 0:2816],
                                   2816, clip=False)
            p.wait("tensor", "vv", w_ok)
            done = 0
            for wv in range(3):
                ntile = 8 if wv < 2 else 6
                for i in range(ntile):
                    ti = wv * 8 + i
                    kb = 4 + i // 4
                    p.op("tensor", lambda e, ti=ti, i=i, kb=kb: e.matmul(
                        pb[kb][:, (i % 4) * 128:(i % 4) * 128 + 128],
                        w16d[:, ti * 128:(ti + 1) * 128], C.s_id16,
                        start=True, stop=True), inc="tt" if i == ntile - 1 else None)
                trw = p.c("tt")
                p.wait("vector", "tt", trw)
                for kb in (4, 5):
                    nn = min(512, (ntile - (kb - 4) * 4) * 128)
                    if nn <= 0:
                        continue
                    p.op("vector", lambda e, kb=kb, nn=nn, wv=wv, hc=hc: e.tensor_copy(
                        out=wdT[:, hc * 2816 + wv * 1024 + (kb - 4) * 512:
                                hc * 2816 + wv * 1024 + (kb - 4) * 512 + nn],
                        in_=pb[kb][:, 0:nn]), inc="vv")
                evw = p.c("vv")
                p.wait("tensor", "vv", evw)
            w16d_users = p.c("tt")
        wdT_ok = p.c("vv")

        s_tslab2 = [s_tslab, s_wd]
        i1_cnt = [p.c("ss"), p.c("ss")]
        t_ld2 = [0, 0]
        h2_cons = 0
        y2_out_done = 0
        qk = 0
        for tg in range(NG):
            p.wait("sync", "vv", p.c("vv"))
            y1tg_ld = p.dma("sync", s_y1tg,
                            dap(SP["y1_sp"], tg * 512, [[8 * TCORE, 128], [TCORE, 8], [1, 512]]),
                            sem="dyr")
            for sb in range(4):
                gp = (tg * 4 + sb) % 2
                p.wait("sync", "ss", i1_cnt[gp])
                t_ld2[gp] = p.dma("sync", s_tslab2[gp],
                             SP["t_sp"][:, tg * NKI * 512 + sb * 2816:
                                        tg * NKI * 512 + (sb + 1) * 2816], sem=f"dtr{gp}")
                p.wait("scalar", f"dtr{gp}", t_ld2[gp])
                p.wait("scalar", "vv", qk)
                p.op("scalar", lambda e, gp=gp: e.activation(
                    out=s_u[:, 0:2816], in_=s_tslab2[gp], func=AF.Identity,
                    bias=C.constv(256.5), scale=s_inv2), inc="ss")
                i1_cnt[gp] = p.c("ss")
                p.wait("vector", "ss")
                p.wait("vector", "tt", h2_cons)
                qk = quant_to_int16b(s_u[:, 0:2816], s_cu[:, 0:2816],
                                h2[:, sb * 2816:(sb + 1) * 2816], 2816, clip=True)
                if C.debug and tg == 0 and sb == 0:
                    p.wait("sync", "vv", qk)
                    dd = p.dma("sync", T["dbg_q"][:, 0:2816], s_u[:, 0:2816], sem="dqo")
                    dd = p.dma("sync", T["dbg_q"][:, 2816:5632], s_cu[:, 0:2816], sem="dqo")
                    p.wait("scalar", "dqo", dd)
                    p.wait("vector", "dqo", dd)
            h2_ok = p.c("vv")
            if C.debug and tg == 0:
                p.wait("sync", "vv", h2_ok)
                dd = p.dma("sync", T["dbg_h2"], h2[:, 0:11264], sem="dqo")
                p.wait("vector", "dqo", dd)
            p.wait("tensor", "vv", max(h2_ok, wdT_ok))
            bank_vv = [0, 0]
            for hc in range(8):
                bank = hc % 2
                p.wait("tensor", "vv", bank_vv[bank])
                for ki in range(NKI):
                    p.op("tensor", lambda e, hc=hc, ki=ki, bank=bank: e.matmul(
                        pb[bank][:, 0:512], wdT[:, (hc * 22 + ki) * 128:(hc * 22 + ki + 1) * 128],
                        h2[:, ki * 512:(ki + 1) * 512],
                        start=(ki == 0), stop=(ki == NKI - 1)), inc="tt" if ki == NKI - 1 else None)
                dmm = p.c("tt")
                p.wait("vector", "tt", dmm)
                p.wait("vector", "dyr", y1tg_ld)
                p.op("vector", lambda e, hc=hc, bank=bank: e.scalar_tensor_tensor(
                    out=s_y2[:, hc * 512:(hc + 1) * 512], in0=pb[bank][:, 0:512],
                    scalar=s_sdo[:, 2 * hc:2 * hc + 1], in1=s_y1tg[:, hc * 512:(hc + 1) * 512],
                    op0=ALU.mult, op1=ALU.add), inc="vv")
                bank_vv[bank] = p.c("vv")
            y2_ok = p.c("vv")
            if C.debug and tg == 0:
                p.wait("sync", "vv", y2_ok)
                dd = p.dma("sync", T["dbg_y2"], s_y2[:, 0:4096], sem="dqo")
                p.wait("scalar", "dqo", dd)
                p.wait("vector", "dqo", dd)
            h2_cons = p.c("tt")
            s2mm = 0
            p.wait("scalar", "vv", y2_ok)
            for mg in range(8):
                p.wait("scalar", "tt", s2mm)
                p.op("scalar", lambda e, mg=mg: e.activation(
                    out=s_ysq[:, 0:512], in_=s_y2[:, mg * 512:(mg + 1) * 512],
                    func=AF.Square, bias=C.constv(0.0), scale=1.0), inc="ss")
                sqd = p.c("ss")
                p.wait("tensor", "vv", y2_ok)
                p.op("tensor", lambda e, mg=mg: e.matmul(
                    pb[4][:, 0:512], C.s_ones, s_y2[:, mg * 512:(mg + 1) * 512],
                    start=(mg == 0), stop=(mg == 7)), inc=None)
                p.wait("tensor", "ss", sqd)
                p.op("tensor", lambda e, mg=mg: e.matmul(
                    pb[5][:, 0:512], C.s_ones, s_ysq[:, 0:512],
                    start=(mg == 0), stop=(mg == 7)), inc="tt")
                s2mm = p.c("tt")
            statmm = p.c("tt")
            p.wait("vector", "tt", statmm)
            p.op("vector", lambda e: e.tensor_scalar(
                out=s_mb[:, 0:512], in0=pb[4][:, 0:512], scalar1=1.0 / D, scalar2=None,
                op0=ALU.mult), inc=None)
            p.op("vector", lambda e: e.tensor_scalar(
                out=s_t2s[:, 0:512], in0=pb[5][:, 0:512], scalar1=1.0 / D, scalar2=None,
                op0=ALU.mult), inc=None)
            p.op("vector", lambda e: e.tensor_tensor(
                out=s_uu[:, 0:512], in0=s_mb[:, 0:512], in1=s_mb[:, 0:512], op=ALU.mult),
                inc=None)
            p.op("vector", lambda e: e.scalar_tensor_tensor(
                out=s_var[:, 0:512], in0=s_t2s[:, 0:512], scalar=EPS_LN, in1=s_uu[:, 0:512],
                op0=ALU.add, op1=ALU.subtract), inc="vv")
            var_ok = p.c("vv")
            p.wait("scalar", "vv", var_ok)
            p.op("scalar", lambda e: e.activation(
                out=s_r[:, 0:512], in_=s_var[:, 0:512], func=AF.Ln,
                bias=C.constv(0.0), scale=1.0), inc=None)
            p.op("scalar", lambda e: e.activation(
                out=s_r[:, 0:512], in_=s_r[:, 0:512], func=AF.Exp,
                bias=C.constv(0.0), scale=-0.5), inc="ss")
            r_ok = p.c("ss")
            p.wait("vector", "ss", r_ok)
            p.wait("vector", "dout", y2_out_done)
            for mg in range(8):
                p.op("vector", lambda e, mg=mg: e.tensor_tensor(
                    out=s_ysq[:, 0:512], in0=s_y2[:, mg * 512:(mg + 1) * 512],
                    in1=s_mb[:, 0:512], op=ALU.subtract), inc=None)
                p.op("vector", lambda e, mg=mg: e.tensor_tensor(
                    out=s_y2[:, mg * 512:(mg + 1) * 512], in0=s_ysq[:, 0:512],
                    in1=s_r[:, 0:512], op=ALU.mult), inc="vv" if mg == 7 else None)
            y2n_ok = p.c("vv")
            p.wait("sync", "vv", y2n_ok)
            y2_out_done = p.dma("sync",
                dap(T["out_t"], tg * 512, [[TCORE, 128], [128 * TCORE, 8], [1, 512]]),
                s_y2[:, 0:4096].rearrange("p (c t) -> p c t", c=8), sem="dout")
    C.out_done = p.c("dout")


def emit_finish(C):
    p, T = C.p, C.T
    p.barrier()
    if C.stage == "F":
        p.wait("sync", "dout", C.out_done)
    else:
        fin = p.dma("sync", T["out_t"][0:128, 0:128], C.s_state[:, 0:128])
        p.wait("sync", "dq", fin)


def _stage_inputs(L0_attn_x, L0_attn_norm_w, L0_attn_Wq, L0_attn_Wk, L0_attn_Wv,
                  L0_attn_Wqf, L0_attn_Wkf, L0_attn_Wout, L0_ffn_clip_x,
                  L0_ffn_clip_down, L0_ffn_w_gate, L0_ffn_w_up, L0_ffn_w_down):
    f32 = np.float32
    x = np.asarray(L0_attn_x, f32)

    def fpad(w, hi):
        wp = np.zeros((128, 128), f32)
        if hi:
            wp[64:128, 0:16] = np.asarray(w, f32)
        else:
            wp[0:64, 0:16] = np.asarray(w, f32)
        return wp

    r = np.arange(128)
    m_diag = ((r[:, None] <= r[None, :]) & (r[:, None] >= r[None, :] - 64)).astype(f32)
    m_prev = (r[:, None] >= r[None, :] + 64).astype(f32)
    m_tril = (r[:, None] <= r[None, :]).astype(f32)
    ident = np.eye(128, dtype=f32)

    import ml_dtypes
    ident16 = np.eye(128, dtype=ml_dtypes.bfloat16)

    common = dict(
        wq=np.ascontiguousarray(np.asarray(L0_attn_Wq, f32)),
        wk=np.ascontiguousarray(np.asarray(L0_attn_Wk, f32)),
        wv=np.ascontiguousarray(np.asarray(L0_attn_Wv, f32)),
        wqf_lo=fpad(L0_attn_Wqf, 0), wqf_hi=fpad(L0_attn_Wqf, 1),
        wkf_lo=fpad(L0_attn_Wkf, 0), wkf_hi=fpad(L0_attn_Wkf, 1),
        wout=np.ascontiguousarray(np.asarray(L0_attn_Wout, f32)),
        normw=np.ascontiguousarray(np.asarray(L0_attn_norm_w, f32).reshape(8, 128).T),
        m_diag=m_diag, m_prev=m_prev, m_tril=m_tril, ident=ident, ident16=ident16,
        clip_x=np.full((128, 2), np.asarray(L0_ffn_clip_x, f32)[0], f32),
        clip_dn=np.full((128, 2), np.asarray(L0_ffn_clip_down, f32)[0], f32),
        w_gate=np.ascontiguousarray(np.asarray(L0_ffn_w_gate, f32)),
        w_up=np.ascontiguousarray(np.asarray(L0_ffn_w_up, f32)),
        w_down=np.ascontiguousarray(np.asarray(L0_ffn_w_down, f32)),
    )

    in_maps = []
    for core in range(8):
        b, half = core // 2, core % 2
        t0 = half * TCORE
        xs = np.zeros((NTT * 128, D), f32)
        if half == 1:
            xs[0:128] = x[b, t0 - 128:t0]
        xs[128:] = x[b, t0:t0 + TCORE]
        m = dict(common)
        m["x_t"] = np.ascontiguousarray(xs.T)
        m["m_first"] = m_prev if half == 1 else np.zeros_like(m_prev)
        m["lin_flag"] = np.full((128, 1), f32(half))
        in_maps.append(m)
    return in_maps


def kernel(**inputs):
    in_maps = _stage_inputs(**inputs)
    nc = build_program(stage="F", debug=False)
    res = run_bass_kernel_spmd(nc, in_maps, core_ids=list(range(8)))
    out = np.empty((4, 4096, D), np.float32)
    for core in range(8):
        b, half = core // 2, core % 2
        out[b, half * TCORE:(half + 1) * TCORE] = np.asarray(res.results[core]["out_t"]).T
    return out


# version 16
# speedup vs baseline: 1.5807x; 1.0042x over previous
import numpy as np
from contextlib import ExitStack

import concourse.bass as bass
import concourse.mybir as mybir
from concourse.bass_utils import run_bass_kernel_spmd

F32 = mybir.dt.float32
BF16 = mybir.dt.bfloat16
I32 = mybir.dt.int32
AF = mybir.ActivationFunctionType
ALU = mybir.AluOpType

D = 1024
H = 16
HD = 64
FT = 16
TCORE = 2048
NTT = 17
NG = 4
NCH = 8
DI = 2816
NKI = 22
EPS_NORM = 1e-6
EPS_LIN = 1e-6
EPS_LN = 1e-5

ENGS = ("sync", "vector", "scalar", "tensor", "gpsimd")
PARANOID = False
EMIT_WIN = True
EMIT_LIN = True
LIN_INTRA = True
LIN_INTER = True
INTER_KK = (0, 1, 2)
INTER_SU_ONES = False


class Prog:
    def __init__(self):
        self.ops = []
        self.cnt = {}

    def c(self, sem):
        return self.cnt.get(sem, 0)

    def bump(self, sem, n):
        self.cnt[sem] = self.c(sem) + n
        return self.cnt[sem]

    def wait(self, eng, sem, val=None):
        if val is None:
            val = self.c(sem)
        if val > 0:
            self.ops.append((eng, "wait", (sem, val)))

    def op(self, eng, fn, inc=None, inc_n=1):
        if eng == "vector":
            assert inc in (None, "vv")
            self.wait(eng, "vv")
            inc, inc_n = "vv", 1
        elif eng == "scalar":
            assert inc in (None, "ss")
            self.wait(eng, "ss")
            inc, inc_n = "ss", 1
        tgt = self.bump(inc, inc_n) if inc is not None else None
        self.ops.append((eng, "op", (fn, inc, inc_n)))
        return tgt

    def dma(self, eng, out, in_, sem="dq"):
        tgt = self.bump(sem, 16)
        self.ops.append((eng, "dma", (out, in_, sem)))
        if sem == "dqo":
            self.ops.append((eng, "wait", (sem, tgt)))
        return tgt

    def memset(self, eng, ap, const, sem="ms"):
        tgt = self.bump(sem, 16)
        self.ops.append((eng, "memset", (ap, const, sem)))
        return tgt

    def sync3(self, *engs):
        sems = [s for s in self.cnt if s not in ("bar", "cc")]
        for e in engs:
            for s in sems:
                self.wait(e, s)
        for s in ("tt", "vv", "ss"):
            self.wait("sync", s)

    def barrier(self):
        tgt = self.bump("bar", len(ENGS))
        for e in ENGS:
            self.ops.append((e, "barinc", None))
        for e in ENGS:
            self.ops.append((e, "wait", ("bar", tgt)))

    def run(self, engname, eng, sems):
        lastw = {}
        for e, kind, data in self.ops:
            if e != engname:
                continue
            if kind == "wait":
                sem, val = data
                if lastw.get(sem, -1) >= val:
                    continue
                eng.wait_ge(sems[sem], val)
                lastw[sem] = val
            elif kind == "op":
                fn, inc, inc_n = data
                inst = fn(eng)
                if inc is not None:
                    inst.then_inc(sems[inc], inc_n)
            elif kind == "dma":
                out, in_, sem = data
                eng.dma_start(out=out, in_=in_).then_inc(sems[sem], 16)
            elif kind == "memset":
                ap, const, sem = data
                eng.memset(ap, const).then_inc(sems[sem], 16)
            elif kind == "barinc":
                eng.sem_inc(sems["bar"], 1)


class Arena:
    def __init__(self, handle, words):
        self.h = handle
        self.words = words
        self.off = 0
        self.peak = 0

    def alloc(self, n, name=""):
        n = (n + 1) & ~1
        assert self.off + n <= self.words, (
            f"SBUF arena overflow at {name}: {self.off}+{n}>{self.words}")
        ap = self.h[:, self.off:self.off + n]
        self.off += n
        self.peak = max(self.peak, self.off)
        return ap


def dap(t, offset, dims):
    return bass.AP(t.tensor, offset, dims)


class Ctx:
    pass


def build_program(stage="F", debug=False):
    nc = bass.Bass("TRN2", target_bir_lowering=False, debug=False)
    p = Prog()
    T = {}

    def din(name, shape, dt=F32):
        T[name] = nc.dram_tensor(name, shape, dt, kind="ExternalInput").ap()

    din("x_t", [D, NTT * 128])
    din("wq", [D, D])
    din("wk", [D, D])
    din("wv", [D, D])
    din("wqf_lo", [128, 128])
    din("wqf_hi", [128, 128])
    din("wkf_lo", [128, 128])
    din("wkf_hi", [128, 128])
    din("wout", [2048, D])
    din("normw", [128, 8])
    din("m_diag", [128, 128])
    din("m_prev", [128, 128])
    din("m_first", [128, 128])
    din("m_tril", [128, 128])
    din("ident", [128, 128])
    din("ident16", [128, 128], BF16)
    din("lin_flag", [128, 1])
    din("clip_x", [128, 2])
    din("clip_dn", [128, 2])
    din("w_gate", [DI, D])
    din("w_up", [DI, D])
    din("w_down", [D, DI])

    T["out_t"] = nc.dram_tensor("out_t", [D, TCORE], F32, kind="ExternalOutput").ap()

    SP = {}
    for name, shape in (("h_sp", [128, 4 * 4096]), ("k_sp", [128, 8 * NTT * 128]),
                        ("v_sp", [128, NTT * 1024]), ("y1_sp", [128, 8 * TCORE]),
                        ("pref_sp", [128, NCH * 6144]), ("t_sp", [128, NKI * TCORE])):
        SP[name] = nc.dram_tensor(name, shape, F32).ap()
    SP["mx_sp"] = nc.dram_tensor("mx_sp", [1, 128], F32).ap()
    for name, shape in (("ag_st_in", [128, 6144]), ("ag_st_out", [256, 6144]),
                        ("ag_m1_in", [1, 8]), ("ag_m1_out", [8, 8]),
                        ("ag_m2_in", [1, 8]), ("ag_m2_out", [8, 8])):
        SP[name] = nc.dram_tensor(name, shape, F32)

    if debug and stage == "F":
        T["dbg_q"] = nc.dram_tensor("dbg_q", [128, 5632], F32, kind="ExternalOutput").ap()
        T["dbg_sc"] = nc.dram_tensor("dbg_sc", [128, 16], F32, kind="ExternalOutput").ap()
        T["dbg_chain"] = nc.dram_tensor("dbg_chain", [128, 32], F32, kind="ExternalOutput").ap()
        T["dbg_xq"] = nc.dram_tensor("dbg_xq", [128, 16384], BF16, kind="ExternalOutput").ap()
        T["dbg_t"] = nc.dram_tensor("dbg_t", [128, 2048], F32, kind="ExternalOutput").ap()
        T["dbg_h2"] = nc.dram_tensor("dbg_h2", [128, 11264], BF16, kind="ExternalOutput").ap()
        T["dbg_y2"] = nc.dram_tensor("dbg_y2", [128, 4096], F32, kind="ExternalOutput").ap()
    if debug:
        T["dbg_state"] = nc.dram_tensor("dbg_state", [128, 6144], F32, kind="ExternalOutput").ap()
        T["dbg_k"] = nc.dram_tensor("dbg_k", [128, 2176], F32, kind="ExternalOutput").ap()
        T["dbg_h"] = nc.dram_tensor("dbg_h", [128, 4096], F32, kind="ExternalOutput").ap()
        if stage in ("Q", "F"):
            T["dbg_y1"] = nc.dram_tensor("dbg_y1", [128, 8 * TCORE], F32, kind="ExternalOutput").ap()
            T["dbg_cat"] = nc.dram_tensor("dbg_cat", [128, 4096], F32, kind="ExternalOutput").ap()
            T["dbg_yh"] = nc.dram_tensor("dbg_yh", [128, 3072], F32, kind="ExternalOutput").ap()
            T["dbg_ff"] = nc.dram_tensor("dbg_ff", [128, 1280], F32, kind="ExternalOutput").ap()
            T["dbg_su"] = nc.dram_tensor("dbg_su", [128, 12288], F32, kind="ExternalOutput").ap()

    with ExitStack() as es:
        C = Ctx()
        C.p, C.nc, C.T, C.SP, C.stage, C.debug = p, nc, T, SP, stage, debug
        pers = es.enter_context(nc.sbuf_tensor("pers", [128, 1426], F32))
        pers16 = es.enter_context(nc.sbuf_tensor("pers16", [128, 128], BF16))
        st_t = es.enter_context(nc.sbuf_tensor("st_t", [128, 6144], F32))
        C.pb = [es.enter_context(nc.psum_tensor(f"pb{i}", [128, 512], F32)) for i in range(7)]
        C.pbh = es.enter_context(nc.psum_tensor("pbh", [128, 1024], BF16))
        sems = {}

        PA = Arena(pers, 1426)
        C.MD = PA.alloc(128, "m_diag")
        C.MP = PA.alloc(128, "m_prev")
        C.MF = PA.alloc(128, "m_first")
        C.MT = PA.alloc(128, "m_tril")
        C.s_id = PA.alloc(128, "ident")
        C.s_wqf = [PA.alloc(128, "wqf_lo"), PA.alloc(128, "wqf_hi")]
        C.s_wkf = [PA.alloc(128, "wkf_lo"), PA.alloc(128, "wkf_hi")]
        C.s_normw = PA.alloc(8, "normw")
        C.s_flag = PA.alloc(2, "flag")
        C.s_ones = PA.alloc(128, "allones")
        C.s_scal = PA.alloc(16, "scalars")
        C.s_clipx = PA.alloc(2, "clipx")
        C.s_clipdn = PA.alloc(2, "clipdn")
        C.s_id16 = pers16[:]
        C.s_state = st_t[:]
        C.s_const = PA.alloc(16, "consts")
        C.consts = {}

        def constv(val):
            val = float(val)
            if val not in C.consts:
                i = len(C.consts)
                assert i < 16
                C.consts[val] = C.s_const[:, i:i + 1]
            return C.consts[val]
        C.constv = constv

        emit_init(C)
        with nc.sbuf_tensor("akq", [128, 40000], F32) as akq:
            C.A = Arena(akq, 40000)
            emit_phase_k(C)
            if stage != "K":
                C.A = Arena(akq, 40000)
                C.A.off = 0
                emit_phase_q(C)
        if stage == "F":
            emit_phase_f(C)
        emit_finish(C)

        for s in set(list(p.cnt.keys()) + ["bar"]):
            sems[s] = es.enter_context(nc.semaphore(s))
        block = es.enter_context(nc.Block())
        for engname in ENGS:
            getattr(block, engname)(lambda eng, _n=engname: p.run(_n, eng, sems))

    return nc


def emit_init(C):
    p, T = C.p, C.T
    for dst, key in ((C.MD, "m_diag"), (C.MP, "m_prev"), (C.MF, "m_first"),
                     (C.MT, "m_tril"), (C.s_id, "ident"), (C.s_id16, "ident16"),
                     (C.s_wqf[0], "wqf_lo"), (C.s_wqf[1], "wqf_hi"),
                     (C.s_wkf[0], "wkf_lo"), (C.s_wkf[1], "wkf_hi"),
                     (C.s_normw, "normw"), (C.s_flag[:, 0:1], "lin_flag"),
                     (C.s_clipx[:, 0:2], "clip_x"), (C.s_clipdn[:, 0:2], "clip_dn")):
        p.dma("sync", dst, T[key])
    p.memset("vector", C.s_ones, 1.0)
    p.memset("vector", C.s_state, 0.0)
    for v in (0.0, EPS_NORM, EPS_LN, -1.0, 256.5, 1.0):
        p.memset("vector", C.constv(v), v)
    for e in ("vector", "tensor", "scalar"):
        p.wait(e, "dq")
        p.wait(e, "ms")


def sblk(C, h, kk):
    return C.s_state[:, (3 * h + kk) * 128:(3 * h + kk) * 128 + 128]


def x_ap(C, tfirst, ntok):
    return dap(C.T["x_t"], tfirst * 128,
               [[NTT * 128, 128], [128 * NTT * 128, 8], [1, ntok]])


def emit_phase_k(C):
    p, A, pb, T, SP = C.p, C.A, C.pb, C.T, C.SP

    s_wk = A.alloc(8 * 1024, "wk")
    s_wv = A.alloc(8 * 1024, "wv")
    for c in range(8):
        p.dma("sync", s_wk[:, c * 1024:(c + 1) * 1024], T["wk"][c * 128:(c + 1) * 128, :], sem="dwk")
        p.dma("sync", s_wv[:, c * 1024:(c + 1) * 1024], T["wv"][c * 128:(c + 1) * 128, :], sem="dwk")
    wk_ld = p.c("dwk")

    s_x = [A.alloc(8 * 512, "xg0"), A.alloc(8 * 512, "xg1")]
    s_h = A.alloc(8 * 512, "hg")
    s_hsq = A.alloc(8 * 512, "hsq")
    s_rms = A.alloc(512, "rms")
    s_ktmp = [A.alloc(512, "ktmp0"), A.alloc(512, "ktmp1")]
    s_vtile = [A.alloc(1024, "vt0"), A.alloc(1024, "vt1")]
    s_kftm = A.alloc(2 * 256, "kftm")
    s_phik2 = [[A.alloc(384, "phik00"), A.alloc(384, "phik01")],
               [A.alloc(384, "phik10"), A.alloc(384, "phik11")]]
    s_vb2 = [[A.alloc(128, "vb00"), A.alloc(128, "vb01")],
             [A.alloc(128, "vb10"), A.alloc(128, "vb11")]]

    for hp in (0, 1):
        for b in (0, 1):
            p.memset("vector", s_phik2[hp][b], 0.0)
            p.memset("vector", s_phik2[hp][b][:, 32:33], 1.0)
            p.memset("vector", s_vb2[hp][b][:, 64:128], 1.0)
    p.wait("vector", "ms")

    kgroups = [(0, 1)] + [(1 + 4 * g, 4) for g in range(4)]

    def xg_sb(buf, ntok):
        return s_x[buf][:, 0:8 * ntok].rearrange("p (c t) -> p c t", c=8)

    x_done = [0, 0]
    p.dma("sync", xg_sb(0, 128), x_ap(C, 0, 128), sem="dx0")
    x_done[0] = p.c("dx0")

    h_spill_done = 0
    prev_kv_done = 0
    h_done_prev = 0
    state_add_done = 0
    ktmp_free = [0, 0]
    ktmp_spill = [0, 0]
    vtile_spill = [0, 0]
    b23_free = 0
    kf_evac_all = 0

    for gi, (tfirst, ntile) in enumerate(kgroups):
        nt = ntile * 128
        buf = gi % 2
        if gi + 1 < len(kgroups):
            nf, nn = kgroups[gi + 1]
            p.wait("sync", "vv", h_done_prev)
            p.dma("sync", xg_sb(1 - buf, nn * 128), x_ap(C, nf, nn * 128), sem=f"dx{1 - buf}")
            x_done[1 - buf] = p.c(f"dx{1 - buf}")
        xa = s_x[buf]
        p.wait("scalar", f"dx{buf}", x_done[buf])
        p.wait("scalar", "tt", prev_kv_done)
        p.op("scalar", lambda e, xa=xa, nt=nt: e.activation(
            out=s_hsq[:, 0:8 * nt], in_=xa[:, 0:8 * nt], func=AF.Square,
            bias=C.constv(0.0), scale=1.0), inc="ss")
        sq_done = p.c("ss")
        p.wait("tensor", "ss", sq_done)
        for c in range(8):
            p.op("tensor", lambda e, c=c, nt=nt: e.matmul(
                pb[6][:, 0:nt], C.s_ones, s_hsq[:, c * nt:c * nt + nt],
                start=(c == 0), stop=(c == 7)), inc="tt" if c == 7 else None)
        mm_rms = p.c("tt")
        p.wait("scalar", "tt", mm_rms)
        p.op("scalar", lambda e, nt=nt: e.activation(
            out=s_rms[:, 0:nt], in_=pb[6][:, 0:nt], func=AF.Ln,
            bias=C.constv(EPS_NORM), scale=1.0 / D), inc="ss")
        p.op("scalar", lambda e, nt=nt: e.activation(
            out=s_rms[:, 0:nt], in_=s_rms[:, 0:nt], func=AF.Exp,
            bias=C.constv(0.0), scale=-0.5), inc="ss")
        rms_done = p.c("ss")
        p.wait("vector", "ss", rms_done)
        p.wait("vector", "dhs", h_spill_done)
        p.wait("vector", "tt", prev_kv_done)
        for c in range(8):
            p.op("vector", lambda e, c=c, xa=xa, nt=nt: e.scalar_tensor_tensor(
                out=s_h[:, c * nt:c * nt + nt], in0=xa[:, c * nt:c * nt + nt],
                scalar=C.s_normw[:, c:c + 1], in1=s_rms[:, 0:nt],
                op0=ALU.mult, op1=ALU.mult), inc="vv" if c == 7 else None)
        h_done = p.c("vv")
        h_done_prev = h_done
        if tfirst > 0:
            p.wait("sync", "vv", h_done)
            h_spill_done = p.dma("sync",
                SP["h_sp"][:, (gi - 1) * 4096:(gi - 1) * 4096 + 8 * nt],
                s_h[:, 0:8 * nt], sem="dhs")

        if PARANOID:
            p.sync3("tensor", "vector", "scalar")
        p.wait("tensor", "dwk", wk_ld)
        p.wait("tensor", "vv", h_done)
        kf_mm_last = 0
        for m in range(8):
            if PARANOID:
                p.sync3("tensor", "vector")
            bank = m % 2
            p.wait("tensor", "vv", ktmp_free[bank])
            for c in range(8):
                p.op("tensor", lambda e, m=m, c=c, bank=bank, nt=nt: e.matmul(
                    pb[bank][:, 0:nt],
                    s_wk[:, c * 1024 + m * 128: c * 1024 + m * 128 + 128],
                    s_h[:, c * nt:c * nt + nt], start=(c == 0), stop=(c == 7)),
                    inc="tt" if c == 7 else None)
            kmm = p.c("tt")
            p.wait("vector", "tt", kmm)
            p.wait("vector", f"dks{bank}", ktmp_spill[bank])
            p.op("vector", lambda e, bank=bank, nt=nt: e.tensor_copy(
                out=s_ktmp[bank][:, 0:nt], in_=pb[bank][:, 0:nt]), inc="vv")
            ktmp_free[bank] = p.c("vv")
            p.wait("sync", "vv", ktmp_free[bank])
            ktmp_spill[bank] = p.dma("sync",
                SP["k_sp"][:, m * NTT * 128 + tfirst * 128:
                           m * NTT * 128 + (tfirst + ntile) * 128],
                s_ktmp[bank][:, 0:nt], sem=f"dks{bank}")
            if tfirst > 0:
                p.wait("tensor", "vv", max(ktmp_free[bank], kf_evac_all))
                for tl in range(ntile):
                    for par in (0, 1):
                        h = 2 * m + par
                        kb = 4 + tl % 2
                        p.op("tensor", lambda e, tl=tl, par=par, h=h, kb=kb, bank=bank: e.matmul(
                            pb[kb][:, h * 16 + (tl // 2) * 256: h * 16 + (tl // 2) * 256 + 16],
                            s_ktmp[bank][:, tl * 128:(tl + 1) * 128],
                            C.s_wkf[par][:, 0:16], start=True, stop=True),
                            inc="tt" if (tl == ntile - 1 and par == 1) else None)
                kf_mm_last = p.c("tt")

        for tl in range(ntile):
            if PARANOID:
                p.sync3("tensor", "vector", "scalar")
            vbuf = tl % 2
            p.wait("tensor", "vv", b23_free)
            for hf in range(2):
                bank = 2 + hf
                for c in range(8):
                    p.op("tensor", lambda e, tl=tl, hf=hf, c=c, bank=bank, nt=nt: e.matmul(
                        pb[bank][:, 0:512],
                        s_h[:, c * nt + tl * 128: c * nt + tl * 128 + 128],
                        s_wv[:, c * 1024 + hf * 512: c * 1024 + (hf + 1) * 512],
                        start=(c == 0), stop=(c == 7)), inc="tt" if c == 7 else None)
            vmm = p.c("tt")
            p.wait("vector", "tt", vmm)
            p.wait("vector", f"dvs{vbuf}", vtile_spill[vbuf])
            for hf in range(2):
                p.op("vector", lambda e, hf=hf, vbuf=vbuf: e.tensor_copy(
                    out=s_vtile[vbuf][:, hf * 512:(hf + 1) * 512],
                    in_=pb[2 + hf][:, 0:512]), inc="vv" if hf == 1 else None)
            v_evac = p.c("vv")
            b23_free = v_evac
            p.wait("sync", "vv", v_evac)
            vtile_spill[vbuf] = p.dma("sync",
                SP["v_sp"][:, (tfirst + tl) * 1024:(tfirst + tl + 1) * 1024],
                s_vtile[vbuf], sem=f"dvs{vbuf}")

            if tfirst > 0 and tl % 2 == 1:
                cglob = (tfirst - 1) // 2 + tl // 2
                p.wait("vector", "tt", kf_mm_last)
                for b in (0, 1):
                    kb = 4 + (tl - 1 + b) % 2
                    colb = ((tl - 1 + b) // 2) * 256
                    p.op("vector", lambda e, b=b, kb=kb, colb=colb: e.tensor_copy(
                        out=s_kftm[:, b * 256:(b + 1) * 256],
                        in_=pb[kb][:, colb:colb + 256]), inc="vv" if b == 1 else None)
                kf_evac_all = p.c("vv")
                p.wait("sync", "vv", state_add_done if cglob > 0 else 0)
                pref_done = p.dma("sync",
                    SP["pref_sp"][:, cglob * 6144:(cglob + 1) * 6144], C.s_state, sem="dps")
                upd_aps = [
                    [pb[0][:, 0:128], pb[1][:, 0:128], pb[6][:, 0:128]],
                    [pb[4][:, 0:128], pb[5][:, 0:128], pb[6][:, 128:256]],
                ]
                phik_done = [0, 0]
                upd_mm2 = [0, 0]
                add_done2 = [0, 0]
                for hstep in range(H + 1):
                    if hstep < H:
                        h = hstep
                        hp = h % 2
                        p.wait("vector", "tt", upd_mm2[hp])
                        for b in (0, 1):
                            vtl = (tl - 1 + b) % 2
                            p.op("vector", lambda e, b=b, h=h, vtl=vtl, hp=hp: e.tensor_copy(
                                out=s_vb2[hp][b][:, 0:64],
                                in_=s_vtile[vtl][:, h * 64:(h + 1) * 64]), inc=None)
                            p.op("vector", lambda e, b=b, h=h, hp=hp: e.tensor_copy(
                                out=s_phik2[hp][b][:, 0:16],
                                in_=s_kftm[:, b * 256 + h * 16: b * 256 + h * 16 + 16]),
                                inc=None)
                            tns = s_kftm
                            pstride = tns.ap[0][0]
                            in0 = bass.AP(tns.tensor, tns.offset + b * 256 + h * 16,
                                          [[pstride, 128], [0, 16], [1, 16]])
                            in1 = bass.AP(tns.tensor, tns.offset + b * 256 + h * 16,
                                          [[pstride, 128], [1, 16], [0, 16]])
                            outo = s_phik2[hp][b][:, 128:384].rearrange("p (j i) -> p j i", j=16)
                            p.op("vector", lambda e, in0=in0, in1=in1, outo=outo: e.tensor_tensor(
                                out=outo, in0=in0, in1=in1, op=ALU.mult), inc="vv")
                        phik_done[hp] = p.c("vv")
                    if hstep >= 1:
                        h = hstep - 1
                        hp = h % 2
                        p.wait("tensor", "vv", phik_done[hp])
                        p.wait("tensor", "vv", add_done2[hp])
                        for kk in range(3):
                            for b in (0, 1):
                                p.op("tensor", lambda e, kk=kk, b=b, hp=hp: e.matmul(
                                    upd_aps[hp][kk],
                                    s_phik2[hp][b][:, kk * 128:(kk + 1) * 128],
                                    s_vb2[hp][b], start=(b == 0), stop=(b == 1)),
                                    inc="tt" if b == 1 else None)
                        upd_mm2[hp] = p.c("tt")
                        upd_mm_prev = upd_mm2[hp]
                        p.wait("vector", "tt", upd_mm2[hp])
                        p.wait("vector", "dps", pref_done)
                        for kk in range(3):
                            p.op("vector", lambda e, h=h, kk=kk, hp=hp: e.tensor_tensor(
                                out=sblk(C, h, kk), in0=sblk(C, h, kk),
                                in1=upd_aps[hp][kk], op=ALU.add),
                                inc="vv" if kk == 2 else None)
                        add_done2[hp] = p.c("vv")
                state_add_done = p.c("vv")
                ktmp_free[0] = max(ktmp_free[0], state_add_done)
                ktmp_free[1] = max(ktmp_free[1], state_add_done)
        prev_kv_done = p.c("tt")

    C.state_add_done = state_add_done
    C.k_tt_done = p.c("tt")

    p.wait("sync", "vv", state_add_done)
    ag_in_done = p.dma("sync", SP["ag_st_in"].ap(), C.s_state, sem="dag")
    p.wait("gpsimd", "dag", ag_in_done)
    p.op("gpsimd", lambda e: e.collective_compute(
        "AllGather", ALU.bypass, replica_groups=[[0, 1], [2, 3], [4, 5], [6, 7]],
        ins=[SP["ag_st_in"].ap()], outs=[SP["ag_st_out"].ap()]), inc="cc")
    C.ag_state_done = p.c("cc")

    if C.debug:
        p.wait("sync", "vv", state_add_done)
        p.dma("sync", T["dbg_state"], C.s_state, sem="dqo")
        p.dma("sync", T["dbg_k"], SP["k_sp"][:, 0:2176], sem="dqo")
        p.dma("sync", T["dbg_h"], SP["h_sp"][:, 0:4096], sem="dqo")


def emit_phase_q(C):
    p, A, pb, T, SP = C.p, C.A, C.pb, C.T, C.SP

    def fence_mm(bank, inc="tt"):
        p.op("tensor", lambda e, bank=bank: e.matmul(
            pb[bank][:, 504:506], C.s_ones, C.s_ones[:, 0:2],
            start=True, stop=True), inc=inc)

    A2 = Arena(A.h, A.words)
    s_hg = A2.alloc(4096, "hg")
    s_qg = A2.alloc(4096, "qg")
    s_vg = A2.alloc(5120, "vg")
    s_su = A2.alloc(6144, "su")
    s_cat = A2.alloc(4096, "cat")
    s_yh = A2.alloc(2048, "yh")
    s_xc = A2.alloc(2048, "xc")
    s_wqc = [A2.alloc(1024, "wqc0"), A2.alloc(1024, "wqc1")]
    s_woc = [A2.alloc(1024, "woc0"), A2.alloc(1024, "woc1")]
    s_ksl = [A2.alloc(640, "ksl0"), A2.alloc(640, "ksl1")]
    s_kslf = A2.alloc(640, "kslf")
    s_kff = A2.alloc(256, "kff")
    s_qff = A2.alloc(256, "qff")
    s_qftm = A2.alloc(32, "qftm")
    s_opqtm = A2.alloc(512, "opqtm")
    s_opqf = A2.alloc(512, "opqf")
    s_vbw = A2.alloc(3 * 128, "vbw")
    s_E = A2.alloc(512, "E")
    s_sq = A2.alloc(384, "sq")
    s_A = A2.alloc(384, "A")
    s_zr = A2.alloc(256, "zr")
    s_zt = A2.alloc(256, "zt")
    s_mpd = A2.alloc(256, "mpd")
    s_mfd = A2.alloc(256, "mfd")
    s_ysq = A2.alloc(256, "ysq")
    s_mb = A2.alloc(256, "mb")
    s_t2 = A2.alloc(256, "t2")
    s_u = A2.alloc(256, "u")
    s_var = A2.alloc(256, "var")
    s_r = A2.alloc(256, "r")
    s_m1 = A2.alloc(2, "m1")
    s_red = A2.alloc(2, "red")

    p.memset("vector", s_kff[:, 0:256], 0.0)
    p.memset("vector", s_qff[:, 0:256], 0.0)
    p.wait("vector", "ms")
    onesrow = bass.AP(C.s_ones.tensor, C.s_ones.offset + 32 * C.s_ones.ap[0][0],
                      [[C.s_ones.ap[0][0], 1], [0, 256]])
    p.op("vector", lambda e: e.tensor_copy(out=s_kff[32:33, 0:256], in_=onesrow), inc=None)
    p.op("vector", lambda e: e.tensor_copy(out=s_qff[32:33, 0:256], in_=onesrow), inc="vv")
    for i in range(3):
        p.memset("vector", s_vbw[:, i * 128 + 64:i * 128 + 128], 1.0)
    p.memset("vector", s_m1, 0.0)
    p.wait("vector", "ms")
    p.op("vector", lambda e: e.tensor_copy(out=s_mpd[:, 0:128], in_=C.MP), inc=None)
    p.op("vector", lambda e: e.tensor_copy(out=s_mpd[:, 128:256], in_=C.MD), inc=None)
    p.op("vector", lambda e: e.tensor_copy(out=s_mfd[:, 0:128], in_=C.MF), inc=None)
    p.op("vector", lambda e: e.tensor_copy(out=s_mfd[:, 128:256], in_=C.MD), inc="vv")

    ksl_ld = [0, 0]
    qg_free = 0
    hg_users = 0
    vg_users = 0
    cat_users = 0
    kffqff_users = 0
    vbw_users = 0
    opq_users = 0
    su_users = 0
    opq_evac = 0
    e_consumed = 0
    sq_consumed = 0
    tr_done_prev = 0
    qftm_cons = 0
    b6_read = 0
    e_read = 0
    sq_read = 0
    yh_spill = 0
    wout_done = 0
    yh_built_all = 0

    hg_users_prev = 0
    vg_users_prev = 0
    su_ready = 0
    for g in range(NG):
        if g == 0:
            p.wait("sync", "tt", C.k_tt_done)
            p.wait("sync", "vv", C.state_add_done)
            for s in ("dhs", "dvs0", "dvs1", "dks0", "dks1"):
                p.wait("sync", s, p.c(s))
        p.wait("sync", "tt", hg_users_prev)
        hg_ld = p.dma("sync", s_hg, SP["h_sp"][:, g * 4096:(g + 1) * 4096], sem="dhg")
        p.wait("sync", "vv", vg_users_prev)
        vg_ld = p.dma("sync", s_vg, SP["v_sp"][:, 4 * g * 1024:(4 * g + 5) * 1024], sem="dvg")

        p.wait("tensor", "dhg", hg_ld)
        p.wait("tensor", "tt", qg_free)
        if g == 0:
            p.wait("tensor", "vv", C.state_add_done)
        wq_ld = [0, 0]
        wqc_users = [0, 0]
        for wave in range(2):
            for c in range(8):
                wb = c % 2
                p.wait("sync", "tt", wqc_users[wb])
                wq_ld[wb] = p.dma("sync", s_wqc[wb][:, 0:512],
                                  T["wq"][c * 128:(c + 1) * 128, wave * 512:(wave + 1) * 512],
                                  sem=f"dwq{wb}")
                p.wait("tensor", f"dwq{wb}", wq_ld[wb])
                for mo in range(4):
                    p.op("tensor", lambda e, c=c, mo=mo, wb=wb: e.matmul(
                        pb[mo][:, 0:512], s_wqc[wb][:, mo * 128:(mo + 1) * 128],
                        s_hg[:, c * 512:(c + 1) * 512], start=(c == 0), stop=(c == 7)),
                        inc="tt" if mo == 3 else None)
                wqc_users[wb] = p.c("tt")
            qmm = p.c("tt")
            p.wait("scalar", "tt", qmm)
            for mo in range(4):
                m = wave * 4 + mo
                p.op("scalar", lambda e, mo=mo, m=m: e.activation(
                    out=s_qg[:, m * 512:(m + 1) * 512], in_=pb[mo][:, 0:512],
                    func=AF.Identity, bias=C.constv(0.0), scale=1.0),
                    inc="ss" if mo == 3 else None)
            qevac = p.c("ss")
            p.wait("tensor", "ss", qevac)
        q_ready = p.c("ss")

        if g == 0:
            s_soff = A.h[:, 19456:25600]
            s_pref = [C.s_state[:, 0:6144], A.h[:, 13312:19456]]
            p.wait("sync", "cc", C.ag_state_done)
            p.wait("sync", "dps", p.c("dps"))
            soff_ld = p.dma("sync", s_soff,
                            dap(SP["ag_st_out"].ap(), 0, [[6144, 128], [1, 6144]]), sem="dso")
            p.wait("vector", "dso", soff_ld)
            p.op("vector", lambda e, s_soff=s_soff: e.tensor_scalar(
                out=s_soff, in0=s_soff, scalar1=C.s_flag[:, 0:1], scalar2=None,
                op0=ALU.mult), inc="vv")
            soff_ok = p.c("vv")
            pref_ld = [0, 0]
            pref_add = [0, 0]
            pref_st = [0, 0]
            for c2 in range(NCH):
                b = c2 % 2
                p.wait("sync", "vv", pref_add[b])
                p.wait("sync", "dps", pref_st[b])
                pref_ld[b] = p.dma("sync", s_pref[b],
                                   SP["pref_sp"][:, c2 * 6144:(c2 + 1) * 6144], sem=f"dpl{b}")
                p.wait("vector", f"dpl{b}", pref_ld[b])
                p.wait("vector", "vv", soff_ok)
                p.op("vector", lambda e, b=b, s_soff=s_soff, s_pref=s_pref: e.tensor_tensor(
                    out=s_pref[b], in0=s_pref[b], in1=s_soff, op=ALU.add), inc="vv")
                pref_add[b] = p.c("vv")
                p.wait("sync", "vv", pref_add[b])
                p.dma("sync", SP["pref_sp"][:, c2 * 6144:(c2 + 1) * 6144], s_pref[b], sem="dps")
                pref_st[b] = p.c("dps")
            su_ready = p.c("dps")

        for cc in range(2):
            c = 2 * g + cc
            p.wait("sync", "dps", su_ready)
            p.wait("sync", "tt", su_users)
            su_ld = p.dma("sync", s_su, SP["pref_sp"][:, c * 6144:(c + 1) * 6144], sem="dsu")

            ksl_ok = 0
            for h in range(H):
                if PARANOID:
                    p.sync3("tensor", "vector", "scalar")
                par = h % 2
                m = h // 2
                if par == 0:
                    p.wait("sync", "vv", ksl_ok)
                    kslf_ld = p.dma("sync", s_kslf[:, 0:640],
                                    SP["k_sp"][:, m * (NTT * 128) + 4 * g * 128:
                                               m * (NTT * 128) + 4 * g * 128 + 640], sem="dkf")
                    if g == 0 and cc == 0 and h == 0:
                        p.memset("vector", s_ksl[0][:, 0:640], 0.0)
                        p.memset("vector", s_ksl[1][:, 0:640], 0.0)
                        p.wait("vector", "ms")
                        p.wait("tensor", "ms")
                    p.wait("vector", "dkf", kslf_ld)
                    p.wait("vector", "tt", kffqff_users)
                    p.op("vector", lambda e: e.tensor_copy(
                        out=s_ksl[0][0:64, 0:640], in_=s_kslf[0:64, 0:640]), inc=None)
                    p.op("vector", lambda e: e.tensor_copy(
                        out=s_ksl[1][64:128, 0:640], in_=s_kslf[64:128, 0:640]), inc="vv")
                    ksl_ok = p.c("vv")
                p.wait("vector", "tt", vbw_users)
                p.wait("vector", "dvg", vg_ld)
                for i in range(3):
                    tloc = 2 * cc + i
                    p.op("vector", lambda e, i=i, tloc=tloc, h=h: e.tensor_copy(
                        out=s_vbw[:, i * 128:i * 128 + 64],
                        in_=s_vg[:, tloc * 1024 + h * 64:tloc * 1024 + h * 64 + 64]),
                        inc="vv" if i == 2 else None)
                vbw_ok = p.c("vv")
                p.wait("tensor", "vv", ksl_ok)
                p.wait("tensor", "ss", b6_read)
                p.wait("tensor", "ss", q_ready)
                p.op("tensor", lambda e, par=par, cc=cc: e.matmul(
                    pb[6][:, 0:256], C.s_wkf[par],
                    s_ksl[par][:, (2 * cc + 1) * 128:(2 * cc + 3) * 128],
                    start=True, stop=True), inc=None)
                p.op("tensor", lambda e, par=par, m=m, cc=cc: e.matmul(
                    pb[6][:, 256:512], C.s_wqf[par],
                    s_qg[:, m * 512 + cc * 256:m * 512 + cc * 256 + 256],
                    start=True, stop=True), inc="tt")
                fqmm = p.c("tt")
                p.wait("scalar", "tt", fqmm)
                p.wait("scalar", "tt", kffqff_users)
                p.op("scalar", lambda e: e.activation(
                    out=s_kff[0:16, 0:256], in_=pb[6][0:16, 0:256],
                    func=AF.Identity, bias=C.constv(0.0)[0:16], scale=1.0), inc=None)
                p.op("scalar", lambda e: e.activation(
                    out=s_qff[0:16, 0:256], in_=pb[6][0:16, 256:512],
                    func=AF.Identity, bias=C.constv(0.0)[0:16], scale=1.0), inc="ss")
                b6_read = p.c("ss")
                ffq_ok = b6_read
                for t2_ in range(2):
                    p.op("tensor", lambda e, t2_=t2_, m=m, cc=cc, par=par: e.matmul(
                        pb[3][:, t2_ * 16:t2_ * 16 + 16],
                        s_qg[:, m * 512 + (2 * cc + t2_) * 128:m * 512 + (2 * cc + t2_) * 128 + 128],
                        C.s_wqf[par][:, 0:16], start=True, stop=True),
                        inc="tt" if t2_ == 1 else None)
                qftm_mm = p.c("tt")
                p.wait("scalar", "tt", qftm_mm)
                p.wait("scalar", "vv", qftm_cons)
                p.op("scalar", lambda e: e.activation(
                    out=s_qftm[:, 0:32], in_=pb[3][:, 0:32],
                    func=AF.Identity, bias=C.constv(0.0), scale=1.0), inc="ss")
                qftm_ok = p.c("ss")
                tq0, tq1 = 2 * cc, 2 * cc + 1
                if EMIT_WIN:
                    p.wait("tensor", "ss", e_read)
                    p.op("tensor", lambda e, tq0=tq0, par=par, m=m: e.matmul(
                        pb[0][:, 0:128],
                        s_ksl[par][:, tq0 * 128:(tq0 + 1) * 128],
                        s_qg[:, m * 512 + tq0 * 128:m * 512 + tq0 * 128 + 128],
                        start=True, stop=True), inc=None)
                    p.op("tensor", lambda e, tq0=tq0, tq1=tq1, par=par, m=m: e.matmul(
                        pb[0][:, 128:384],
                        s_ksl[par][:, tq1 * 128:(tq1 + 1) * 128],
                        s_qg[:, m * 512 + tq0 * 128:m * 512 + tq0 * 128 + 256],
                        start=True, stop=True), inc=None)
                    p.op("tensor", lambda e, tq1=tq1, par=par, m=m: e.matmul(
                        pb[0][:, 384:512],
                        s_ksl[par][:, (tq1 + 1) * 128:(tq1 + 2) * 128],
                        s_qg[:, m * 512 + tq1 * 128:m * 512 + tq1 * 128 + 128],
                        start=True, stop=True), inc="tt")
                    smm = p.c("tt")
                    p.wait("scalar", "tt", smm)
                    p.wait("scalar", "vv", e_consumed)
                    p.op("scalar", lambda e: e.activation(
                        out=s_E[:, 0:512], in_=pb[0][:, 0:512],
                        func=AF.Exp, bias=C.constv(0.0), scale=0.125), inc="ss")
                    e_read = p.c("ss")
                p.wait("vector", "ss", qftm_ok)
                p.wait("vector", "tt", tr_done_prev)
                pstride = s_qftm.ap[0][0]
                for t2_ in range(2):
                    in0 = bass.AP(s_qftm.tensor, s_qftm.offset + t2_ * 16,
                                  [[pstride, 128], [0, 16], [1, 16]])
                    in1 = bass.AP(s_qftm.tensor, s_qftm.offset + t2_ * 16,
                                  [[pstride, 128], [1, 16], [0, 16]])
                    outo = s_opqtm[:, t2_ * 256:(t2_ + 1) * 256].rearrange(
                        "p (j i) -> p j i", j=16)
                    p.op("vector", lambda e, in0=in0, in1=in1, outo=outo: e.scalar_tensor_tensor(
                        out=outo, in0=in0, scalar=0.5, in1=in1,
                        op0=ALU.mult, op1=ALU.mult), inc="vv" if t2_ == 1 else None)
                opqtm_ok = p.c("vv")
                qftm_cons = opqtm_ok
                if EMIT_WIN:
                    p.wait("vector", "ss", e_read)
                    mh = s_mfd if (g == 0 and cc == 0) else s_mpd
                    p.op("vector", lambda e, mh=mh: e.tensor_tensor(
                        out=s_E[:, 0:256], in0=s_E[:, 0:256], in1=mh, op=ALU.mult), inc=None)
                    p.op("vector", lambda e: e.tensor_tensor(
                        out=s_E[:, 256:512], in0=s_E[:, 256:512], in1=s_mpd, op=ALU.mult),
                        inc="vv")
                    e_ok = p.c("vv")
                    e_consumed = e_ok
                p.wait("tensor", "vv", opqtm_ok)
                p.wait("tensor", "ss", opq_evac)
                for i in range(4):
                    p.op("tensor", lambda e, i=i: e.matmul(
                        pb[4][:, i * 128:(i + 1) * 128], s_opqtm[:, i * 128:(i + 1) * 128],
                        C.s_id, start=True, stop=True), inc="tt" if i == 3 else None)
                tr_done = p.c("tt")
                tr_done_prev = tr_done
                p.wait("scalar", "tt", tr_done)
                p.wait("scalar", "tt", opq_users)
                p.op("scalar", lambda e: e.activation(
                    out=s_opqf[:, 0:512], in_=pb[4][:, 0:512],
                    func=AF.Identity, bias=C.constv(0.0), scale=1.0), inc="ss")
                opq_evac = p.c("ss")
                if C.debug and g == 0 and cc == 0 and h == 0:
                    p.wait("sync", "vv", opq_evac)
                    ffd = p.dma("sync", T["dbg_ff"][:, 0:256], s_qff[:, 0:256], sem="dqo")
                    ffd = p.dma("sync", T["dbg_ff"][:, 256:512], s_kff[:, 0:256], sem="dqo")
                    ffd = p.dma("sync", T["dbg_ff"][:, 512:1024], s_opqf[:, 0:512], sem="dqo")
                    ffd = p.dma("sync", T["dbg_ff"][:, 1024:1056], s_qftm[:, 0:32], sem="dqo")
                    ffd = p.dma("sync", T["dbg_ff"][:, 1056:1184], s_opqtm[:, 0:128], sem="dqo")
                    p.wait("vector", "dqo", ffd)
                    p.wait("tensor", "dqo", ffd)
                    p.wait("scalar", "dqo", ffd)

                if EMIT_LIN:
                    p.wait("tensor", "ss", sq_read)
                    p.wait("tensor", "ss", ffq_ok)
                    p.op("tensor", lambda e: e.matmul(
                        pb[1][:, 0:256], s_kff[:, 0:128], s_qff[:, 0:256],
                        start=True, stop=True), inc=None)
                    p.op("tensor", lambda e: e.matmul(
                        pb[1][:, 256:384], s_kff[:, 128:256], s_qff[:, 128:256],
                        start=True, stop=True), inc="tt")
                    stmm = p.c("tt")
                    p.wait("scalar", "tt", stmm)
                    p.wait("scalar", "vv", sq_consumed)
                    p.op("scalar", lambda e: e.activation(
                        out=s_sq[:, 0:384], in_=pb[1][:, 0:384],
                        func=AF.Square, bias=C.constv(-1.0), scale=1.0), inc="ss")
                    sq_read = p.c("ss")
                    p.wait("vector", "ss", sq_read)
                    p.op("vector", lambda e: e.scalar_tensor_tensor(
                        out=s_A[:, 0:384], in0=s_sq[:, 0:384],
                        scalar=0.5, in1=pb[1][:, 0:384], op0=ALU.mult, op1=ALU.add),
                        inc=None)
                    pa = s_A.ap[0][0]
                    a02 = bass.AP(s_A.tensor, s_A.offset, [[pa, 128], [256, 2], [1, 128]])
                    mt2 = bass.AP(C.MT.tensor, C.MT.offset, [[C.MT.ap[0][0], 128], [0, 2], [1, 128]])
                    p.op("vector", lambda e, a02=a02, mt2=mt2: e.tensor_tensor(
                        out=a02, in0=a02, in1=mt2, op=ALU.mult), inc="vv")
                    a_ok = p.c("vv")
                    sq_consumed = a_ok
                if EMIT_WIN:
                    p.wait("tensor", "vv", max(e_ok, vbw_ok))
                    for j2 in range(2):
                        p.op("tensor", lambda e, j2=j2: e.matmul(
                            pb[2][:, 0:128], s_vbw[:, j2 * 128:(j2 + 1) * 128],
                            s_E[:, j2 * 128:(j2 + 1) * 128],
                            start=(j2 == 0), stop=(j2 == 1)), inc=None)
                    for j2 in range(2):
                        p.op("tensor", lambda e, j2=j2: e.matmul(
                            pb[2][:, 128:256], s_vbw[:, (1 + j2) * 128:(2 + j2) * 128],
                            s_E[:, (2 + j2) * 128:(3 + j2) * 128],
                            start=(j2 == 0), stop=(j2 == 1)), inc="tt" if j2 == 1 else None)
                    avmm = p.c("tt")
                    p.wait("vector", "tt", avmm)
                    p.op("vector", lambda e: e.reciprocal(
                        s_zr[0:64, 0:256], pb[2][64:128, 0:256]), inc=None)
                    catslot = (8 + m) * 256
                    p.op("vector", lambda e, catslot=catslot, par=par: e.tensor_tensor(
                        out=s_cat[64 * par:64 * par + 64, catslot:catslot + 256],
                        in0=pb[2][0:64, 0:256], in1=s_zr[0:64, 0:256], op=ALU.mult),
                        inc="vv")
                    win_ok = p.c("vv")

                if EMIT_LIN:
                    p.wait("tensor", "vv", a_ok)
                    p.wait("tensor", "ss", opq_evac)
                    p.wait("tensor", "dsu", su_ld)
                    ymms0 = [
                        (s_vbw[:, 128:256],                          s_A[:, 0:128]),
                        (s_su[:, (3 * h) * 128:(3 * h + 1) * 128],   s_qff[:, 0:128]),
                        (s_su[:, (3 * h + 1) * 128:(3 * h + 2) * 128], s_opqf[:, 0:128]),
                        (s_su[:, (3 * h + 2) * 128:(3 * h + 3) * 128], s_opqf[:, 128:256]),
                    ]
                    for i, (stat, mov) in enumerate(ymms0):
                        p.op("tensor", lambda e, stat=stat, mov=mov, st=(i == 0),
                             sp=(i == len(ymms0) - 1): e.matmul(
                            pb[5][:, 0:128], stat, mov, start=st, stop=sp), inc=None)
                    ymms1 = [
                        (s_vbw[:, 128:256],                          s_A[:, 128:256]),
                        (s_vbw[:, 256:384],                          s_A[:, 256:384]),
                        (s_su[:, (3 * h) * 128:(3 * h + 1) * 128],   s_qff[:, 128:256]),
                        (s_su[:, (3 * h + 1) * 128:(3 * h + 2) * 128], s_opqf[:, 256:384]),
                        (s_su[:, (3 * h + 2) * 128:(3 * h + 3) * 128], s_opqf[:, 384:512]),
                    ]
                    for i, (stat, mov) in enumerate(ymms1):
                        p.op("tensor", lambda e, stat=stat, mov=mov, st=(i == 0),
                             sp=(i == len(ymms1) - 1): e.matmul(
                            pb[5][:, 128:256], stat, mov, start=st, stop=sp),
                            inc="tt" if i == len(ymms1) - 1 else None)
                    ymm = p.c("tt")
                    p.wait("vector", "tt", ymm)
                    p.op("vector", lambda e: e.tensor_scalar(
                        out=s_zt[0:64, 0:256], in0=pb[5][64:128, 0:256],
                        scalar1=EPS_LIN, scalar2=None, op0=ALU.add), inc=None)
                    p.op("vector", lambda e: e.reciprocal(
                        s_zr[0:64, 0:256], s_zt[0:64, 0:256]), inc=None)
                    catslot2 = m * 256
                    p.op("vector", lambda e, catslot2=catslot2, par=par: e.tensor_tensor(
                        out=s_cat[64 * par:64 * par + 64, catslot2:catslot2 + 256],
                        in0=pb[5][0:64, 0:256], in1=s_zr[0:64, 0:256], op=ALU.mult),
                        inc="vv")
                    lin_ok = p.c("vv")
                vbw_users = p.c("tt")
                kffqff_users = p.c("tt")
                su_users = p.c("tt")
                opq_users = p.c("tt")
            cat_ok = p.c("vv")
            if C.debug and g == 0 and cc == 0:
                p.wait("sync", "vv", cat_ok)
                catdump = p.dma("sync", T["dbg_cat"], s_cat, sem="dqo")
                p.wait("vector", "dqo", catdump)

            if PARANOID:
                p.sync3("tensor", "vector", "scalar")
            p.wait("sync", "vv", yh_built_all)
            xc_ld = p.dma("sync", s_xc[:, 0:2048].rearrange("p (c t) -> p c t", c=8),
                          dap(T["x_t"], 128 + g * 512 + cc * 256,
                              [[NTT * 128, 128], [128 * NTT * 128, 8], [1, 256]]), sem="dxc")
            p.wait("tensor", "vv", cat_ok)
            p.wait("tensor", "ss", max(sq_read, e_read))
            wo_ld = [0, 0]
            woc_users = [0, 0]
            for wave in range(2):
                for kc in range(16):
                    wb = kc % 2
                    if PARANOID:
                        p.sync3("tensor", "vector")
                    p.wait("sync", "tt", woc_users[wb])
                    wo_ld[wb] = p.dma("sync", s_woc[wb][:, 0:512],
                                      T["wout"][kc * 128:(kc + 1) * 128, wave * 512:(wave + 1) * 512],
                                      sem=f"dwo{wb}")
                    p.wait("tensor", f"dwo{wb}", wo_ld[wb])
                    for mo in range(4):
                        p.op("tensor", lambda e, kc=kc, mo=mo, wb=wb: e.matmul(
                            pb[mo][:, 0:256], s_woc[wb][:, mo * 128:(mo + 1) * 128],
                            s_cat[:, kc * 256:(kc + 1) * 256],
                            start=(kc == 0), stop=(kc == 15)),
                            inc="tt" if mo == 3 else None)
                    woc_users[wb] = p.c("tt")
                womm = p.c("tt")
                p.wait("vector", "tt", womm)
                p.wait("vector", "dxc", xc_ld)
                p.wait("vector", "dy1", yh_spill)
                for mo in range(4):
                    mg = wave * 4 + mo
                    p.op("vector", lambda e, mo=mo, mg=mg: e.scalar_tensor_tensor(
                        out=s_yh[:, mg * 256:(mg + 1) * 256],
                        in0=s_xc[:, mg * 256:(mg + 1) * 256], scalar=2.0,
                        in1=pb[mo][:, 0:256], op0=ALU.mult, op1=ALU.add),
                        inc="vv" if mo == 3 else None)
                yh_built = p.c("vv")
                p.wait("tensor", "vv", yh_built)
            wout_done = p.c("tt")
            cat_users = wout_done
            yh_built_all = yh_built

            s2mm = 0
            p.wait("scalar", "vv", yh_built)
            for mg in range(8):
                p.wait("scalar", "tt", s2mm)
                p.op("scalar", lambda e, mg=mg: e.activation(
                    out=s_ysq[:, 0:256], in_=s_yh[:, mg * 256:(mg + 1) * 256],
                    func=AF.Square, bias=C.constv(0.0), scale=1.0), inc="ss")
                sqd = p.c("ss")
                p.op("tensor", lambda e, mg=mg: e.matmul(
                    pb[4][:, 0:256], C.s_ones, s_yh[:, mg * 256:(mg + 1) * 256],
                    start=(mg == 0), stop=(mg == 7)), inc=None)
                p.wait("tensor", "ss", sqd)
                p.op("tensor", lambda e, mg=mg: e.matmul(
                    pb[5][:, 0:256], C.s_ones, s_ysq[:, 0:256],
                    start=(mg == 0), stop=(mg == 7)), inc="tt")
                s2mm = p.c("tt")
            statmm = p.c("tt")
            if C.debug and g == 0 and cc == 0:
                p.wait("sync", "vv", p.c("vv"))
                p.wait("sync", "tt", statmm)
                dbgd = p.dma("sync", T["dbg_yh"][:, 0:2048], s_yh[:, 0:2048], sem="dqo")
                p.wait("vector", "dqo", dbgd)
            p.wait("vector", "tt", statmm)
            p.op("vector", lambda e: e.tensor_scalar(
                out=s_mb[:, 0:256], in0=pb[4][:, 0:256], scalar1=1.0 / D, scalar2=None,
                op0=ALU.mult), inc=None)
            p.op("vector", lambda e: e.tensor_scalar(
                out=s_t2[:, 0:256], in0=pb[5][:, 0:256], scalar1=1.0 / D, scalar2=None,
                op0=ALU.mult), inc=None)
            p.op("vector", lambda e: e.tensor_tensor(
                out=s_u[:, 0:256], in0=s_mb[:, 0:256], in1=s_mb[:, 0:256], op=ALU.mult),
                inc=None)
            p.op("vector", lambda e: e.scalar_tensor_tensor(
                out=s_var[:, 0:256], in0=s_t2[:, 0:256], scalar=EPS_LN, in1=s_u[:, 0:256],
                op0=ALU.add, op1=ALU.subtract), inc="vv")
            var_ok = p.c("vv")
            p.wait("scalar", "vv", var_ok)
            p.op("scalar", lambda e: e.activation(
                out=s_r[:, 0:256], in_=s_var[:, 0:256], func=AF.Ln,
                bias=C.constv(0.0), scale=1.0), inc=None)
            p.op("scalar", lambda e: e.activation(
                out=s_r[:, 0:256], in_=s_r[:, 0:256], func=AF.Exp,
                bias=C.constv(0.0), scale=-0.5), inc="ss")
            r_ok = p.c("ss")
            if C.debug and g == 0 and cc == 0:
                p.wait("sync", "ss", r_ok)
                p.dma("sync", T["dbg_yh"][:, 2048:2304], s_mb[:, 0:256])
                p.dma("sync", T["dbg_yh"][:, 2304:2560], s_r[:, 0:256])
                p.dma("sync", T["dbg_yh"][:, 2560:2816], s_var[:, 0:256])
            p.wait("vector", "ss", r_ok)
            p.wait("vector", "dy1", yh_spill)
            for mg in range(8):
                p.op("vector", lambda e, mg=mg: e.tensor_tensor(
                    out=s_ysq[:, 0:256], in0=s_yh[:, mg * 256:(mg + 1) * 256],
                    in1=s_mb[:, 0:256], op=ALU.subtract), inc=None)
                p.op("vector", lambda e, mg=mg: e.tensor_tensor(
                    out=s_yh[:, mg * 256:(mg + 1) * 256], in0=s_ysq[:, 0:256],
                    in1=s_r[:, 0:256], op=ALU.mult), inc=None)
                p.op("vector", lambda e, mg=mg: e.tensor_reduce(
                    out=s_red[:, 0:1], in_=s_yh[:, mg * 256:(mg + 1) * 256],
                    axis=mybir.AxisListType.X, op=ALU.max, apply_absolute_value=True),
                    inc=None)
                p.op("vector", lambda e: e.tensor_tensor(
                    out=s_m1[:, 0:1], in0=s_m1[:, 0:1], in1=s_red[:, 0:1], op=ALU.max),
                    inc="vv" if mg == 7 else None)
            y1_ok = p.c("vv")
            p.wait("sync", "vv", y1_ok)
            yh_spill = p.dma("sync",
                dap(SP["y1_sp"], g * 512 + cc * 256, [[8 * TCORE, 128], [TCORE, 8], [1, 256]]),
                s_yh[:, 0:2048].rearrange("p (c t) -> p c t", c=8), sem="dy1")
        qg_free = p.c("tt")
        hg_users_prev = p.c("tt")
        vg_users_prev = p.c("vv")

    p.wait("sync", "vv", p.c("vv"))
    mxs = p.dma("sync", dap(SP["mx_sp"], 0, [[1, 128], [1, 1]]), s_m1[:, 0:1], sem="dqo")
    p.wait("sync", "dqo", mxs)
    mxl = p.dma("sync", s_mb[0:1, 0:128], SP["mx_sp"][0:1, 0:128], sem="dqo")
    p.wait("vector", "dqo", mxl)
    p.op("vector", lambda e: e.tensor_reduce(
        out=s_red[0:1, 0:1], in_=s_mb[0:1, 0:128], axis=mybir.AxisListType.X,
        op=ALU.max), inc=None)
    if C.debug:
        p.op("vector", lambda e: e.tensor_copy(out=s_mb[0:1, 128:256], in_=s_mb[0:1, 0:128]), inc="vv")
        dchk = p.c("vv")
        p.wait("sync", "vv", dchk)
        p.dma("sync", T["dbg_t"][0:1, 0:128], s_mb[0:1, 128:256], sem="dqo")
    pst = s_red.ap[0][0]
    p.op("vector", lambda e: e.tensor_copy(
        out=s_mb[0:1, 0:8],
        in_=bass.AP(s_red.tensor, s_red.offset, [[pst, 1], [0, 8]])), inc="vv")
    m1rep = p.c("vv")
    p.wait("sync", "vv", m1rep)
    agm_in = p.dma("sync", SP["ag_m1_in"].ap(), s_mb[0:1, 0:8], sem="dqo")
    p.wait("sync", "dqo", agm_in)
    p.wait("gpsimd", "dqo", agm_in)
    p.wait("gpsimd", "cc", C.ag_state_done)
    p.op("gpsimd", lambda e: e.collective_compute(
        "AllGather", ALU.bypass, replica_groups=[list(range(8))],
        ins=[SP["ag_m1_in"].ap()], outs=[SP["ag_m1_out"].ap()]), inc="cc")
    agm1 = p.c("cc")
    C.agm1 = agm1
    C.gmax1_ok = p.c("vv")
    if C.debug:
        p.op("vector", lambda e: e.tensor_copy(out=s_mb[:, 16:17], in_=s_m1[:, 0:1]), inc="vv")
        chain_ok = p.c("vv")
        p.wait("sync", "vv", chain_ok)
        p.wait("sync", "cc", agm1)
        dd = p.dma("sync", T["dbg_chain"][:, 0:8], s_mb[:, 16:24], sem="dqo")
        dd = p.dma("sync", T["dbg_chain"][:, 8:16], s_mb[:, 0:8], sem="dqo")
        dd = p.dma("sync", dap(T["dbg_chain"], 16, [[32, 1], [1, 8]]),
                   dap(SP["ag_m1_in"].ap(), 0, [[8, 1], [1, 8]]), sem="dqo")
        dd = p.dma("sync", dap(T["dbg_chain"], 24, [[32, 8], [1, 8]]),
                   dap(SP["ag_m1_out"].ap(), 0, [[8, 8], [1, 8]]), sem="dqo")
        p.wait("vector", "dqo", dd)

    C.yh_spill_last = yh_spill
    if C.debug:
        p.wait("sync", "vv", C.gmax1_ok)
        p.wait("sync", "dy1", yh_spill)
        p.dma("sync", T["dbg_su"][:, 0:6144], SP["pref_sp"][:, 4 * 6144:5 * 6144], sem="dqo")
        p.dma("sync", T["dbg_su"][:, 6144:12288],
              dap(SP["ag_st_out"].ap(), 0, [[6144, 128], [1, 6144]]), sem="dqo")
        p.dma("sync", T["dbg_y1"], SP["y1_sp"], sem="dqo")


def emit_phase_f(C):
    C.s_clipx = C.s_clipx[:, 0:2]
    C.s_clipdn = C.s_clipdn[:, 0:2]
    C.p.barrier()
    emit_phase_f1(C)
    emit_phase_f2(C)


def emit_phase_f1(C):
    p, pb, T, SP, nc = C.p, C.pb, C.T, C.SP, C.nc
    QM = 127.0
    CLIPLO = 128.0
    CLIPHI = 383.99996948242188

    with ExitStack() as es:
        af = Arena(es.enter_context(nc.sbuf_tensor("af1", [128, 23040], F32)), 23040)
        ah = Arena(es.enter_context(nc.sbuf_tensor("af1h", [128, 36608], BF16)), 36608)
        ai = Arena(es.enter_context(nc.sbuf_tensor("ai1", [128, 2944], I32)), 2944)

        s_g64 = af.alloc(64, "g64")
        s_sx = af.alloc(2, "sx")
        s_invx = af.alloc(2, "invx")
        s_tmp1 = af.alloc(4, "tmp1")
        s_mpad = af.alloc(128, "mpad")
        s_z128 = af.alloc(2, "z128")
        p.memset("vector", s_z128, 0.0)
        ci2 = ai.alloc(2048, "ci32")

        def repmax_scalar(ag_out, sem_done, out_col):
            ld = p.dma("sync", s_g64[0:1, 0:64], dap(ag_out.ap(), 0, [[64, 1], [1, 64]]),
                       sem="dqo")
            p.wait("vector", "dqo", ld)
            p.wait("vector", "ms", p.c("ms"))
            p.op("vector", lambda e: e.tensor_reduce(
                out=s_z128[0:1, 0:1], in_=s_g64[0:1, 0:64], axis=mybir.AxisListType.X,
                op=ALU.max), inc=None)
            p.op("vector", lambda e: e.tensor_reduce(
                out=s_z128[0:1, 1:2], in_=s_g64[0:1, 0:64], axis=mybir.AxisListType.X,
                op=ALU.max), inc="vv")
            rok = p.c("vv")
            p.wait("tensor", "vv", rok)
            p.op("tensor", lambda e: e.matmul(pb[6][:, 0:2], C.s_ones, s_z128[:, 0:2],
                                              start=True, stop=True), inc="tt")
            rok = p.c("tt")
            p.wait("vector", "tt", rok)
            p.op("vector", lambda e, out_col=out_col: e.tensor_copy(
                out=out_col, in_=pb[6][:, 0:2]), inc="vv")
            return p.c("vv")

        def sig_scale(clip_ap, gmax_col, s_col, inv_col):
            p.op("scalar", lambda e: e.activation(out=s_col, in_=clip_ap, func=AF.Exp,
                                                  bias=C.constv(0.0), scale=-1.0), inc="ss")
            p.wait("vector", "ss")
            p.op("vector", lambda e: e.tensor_scalar(out=s_col, in0=s_col,
                                                     scalar1=1.0, scalar2=None, op0=ALU.add), inc=None)
            p.op("vector", lambda e: e.reciprocal(s_col, s_col), inc=None)
            p.op("vector", lambda e: e.tensor_tensor(out=s_col, in0=s_col,
                                                     in1=gmax_col, op=ALU.mult), inc=None)
            p.op("vector", lambda e: e.tensor_scalar(out=s_col, in0=s_col,
                                                     scalar1=1.0 / QM, scalar2=None, op0=ALU.mult), inc=None)
            p.op("vector", lambda e: e.reciprocal(inv_col, s_col), inc="vv")
            return p.c("vv")

        def quant_to_int16(u_ap, cu_ap, out16_ap, n, clip, ci=None):
            civ = ci if ci is not None else ci2[:, 0:n]
            if clip:
                p.op("vector", lambda e: e.tensor_scalar(
                    out=cu_ap, in0=u_ap, scalar1=CLIPLO, scalar2=CLIPHI,
                    op0=ALU.max, op1=ALU.min), inc=None)
                cua = cu_ap
            else:
                cua = u_ap
            p.op("vector", lambda e, cua=cua, civ=civ: e.tensor_copy(out=civ, in_=cua), inc=None)
            dd = cu_ap if not clip else u_ap
            p.op("vector", lambda e, cua=cua, dd=dd, civ=civ: e.tensor_tensor(
                out=dd, in0=civ, in1=cua, op=ALU.is_gt), inc=None)
            p.op("vector", lambda e, dd=dd, civ=civ: e.scalar_tensor_tensor(
                out=out16_ap, in0=civ, scalar=-256.0, in1=dd,
                op0=ALU.add, op1=ALU.subtract), inc="vv")
            return p.c("vv")

        p.wait("sync", "cc", C.agm1)
        g1_ok = repmax_scalar(SP["ag_m1_out"], None, s_tmp1[:, 0:2])
        p.wait("scalar", "vv", g1_ok)
        sx_ok = sig_scale(C.s_clipx, s_tmp1[:, 0:2], s_sx[:, 0:2], s_invx[:, 0:2])
        if C.debug:
            s_dbgsc = af.alloc(16, "dbgsc")
            p.op("vector", lambda e: e.tensor_copy(out=s_dbgsc[:, 0:1], in_=s_tmp1[:, 0:1]), inc=None)
            p.op("vector", lambda e: e.tensor_copy(out=s_dbgsc[:, 1:2], in_=s_sx[:, 0:1]), inc=None)
            p.op("vector", lambda e: e.tensor_copy(out=s_dbgsc[:, 2:3], in_=s_invx[:, 0:1]), inc=None)
            p.op("vector", lambda e: e.tensor_copy(out=s_dbgsc[:, 3:5], in_=C.s_clipx), inc="vv")
            dsc = p.c("vv")
            p.wait("sync", "vv", dsc)
            p.dma("sync", T["dbg_sc"][:, 0:8], s_dbgsc[:, 0:8], sem="dqo")

        s_y1g = af.alloc(4096, "y1g")
        s_u = af.alloc(2048, "uq")
        s_cu = af.alloc(2048, "cuq")
        s_xq = ah.alloc(16384, "xq")
        xqp = s_xq.ap[0][0]
        y1q_ld = 0
        for tg in range(NG):
            p.wait("sync", "vv", p.c("vv"))
            y1q_ld = p.dma("sync", s_y1g,
                           dap(SP["y1_sp"], tg * 512, [[8 * TCORE, 128], [TCORE, 8], [1, 512]]),
                           sem="dyq")
            p.wait("vector", "dyq", y1q_ld)
            p.wait("vector", "vv", sx_ok)
            for hb in range(2):
                sl = slice(hb * 2048, (hb + 1) * 2048)
                p.op("vector", lambda e, sl=sl: e.tensor_scalar(
                    out=s_u[:, 0:2048], in0=s_y1g[:, sl], scalar1=s_invx[:, 0:1],
                    scalar2=256.5, op0=ALU.mult, op1=ALU.add), inc="vv")
                out16 = bass.AP(s_xq.tensor, s_xq.offset + (hb * 4) * 2048 + tg * 512,
                                [[xqp, 128], [2048, 4], [1, 512]])
                quant_to_int16(s_u[:, 0:2048].rearrange("p (c t) -> p c t", c=4),
                               s_cu[:, 0:2048].rearrange("p (c t) -> p c t", c=4),
                               out16, 2048, clip=True,
                               ci=ci2[:, 0:2048].rearrange("p (c t) -> p c t", c=4))
        xq_ok = p.c("vv")
        if C.debug:
            p.op("vector", lambda e: e.tensor_copy(out=s_g64[:, 8:9], in_=s_sx[:, 0:1]), inc=None)
            p.op("vector", lambda e: e.tensor_copy(out=s_g64[:, 9:10], in_=s_invx[:, 0:1]), inc=None)
            p.op("vector", lambda e: e.tensor_copy(out=s_g64[:, 10:11], in_=s_tmp1[:, 0:1]), inc="vv")
            dbgsc = p.c("vv")
            p.wait("sync", "vv", max(xq_ok, dbgsc))
            dd = p.dma("sync", T["dbg_xq"], s_xq[:, 0:16384], sem="dqo")
            dd = p.dma("sync", T["dbg_sc"][:, 0:8], s_g64[:, 8:16], sem="dqo")
            p.wait("vector", "dqo", dd)
            p.wait("scalar", "dqo", dd)

        s_wg = [af.alloc(1024, "wg0"), af.alloc(1024, "wg1")]
        s_wu = [af.alloc(1024, "wu0"), af.alloc(1024, "wu1")]
        s_wqu2 = [af.alloc(1024, "wqua"), af.alloc(1024, "wqub")]
        s_wqd2 = [af.alloc(1024, "wqda"), af.alloc(1024, "wqdb")]
        s_stash = af.alloc(2048, "sgstash")
        s_up2 = [af.alloc(512, "upbuf0"), af.alloc(512, "upbuf1")]
        s_t2 = [af.alloc(512, "tbuf0"), af.alloc(512, "tbuf1")]
        s_tmax = af.alloc(2, "tmax")
        s_sgu = af.alloc(16, "sgu")
        s_tsc = af.alloc(16, "tsc")
        w162 = [ah.alloc(1024, "w16a"), ah.alloc(1024, "w16b")]
        wgT2 = [ah.alloc(1024, "wgT0"), ah.alloc(1024, "wgT1")]
        wuT2 = [ah.alloc(1024, "wuT0"), ah.alloc(1024, "wuT1")]

        p.memset("vector", s_tmax, 0.0)
        p.wait("vector", "ms")

        wg_ld = [0, 0]
        wg_users = [0, 0]
        wg_users_ss = [0, 0]
        bank_ss = [0, 0, 0, 0]
        t_sp_done = [0, 0]
        w16_users = [0, 0]
        wT_users = [0, 0]
        wT_ok = [0, 0]
        wb45_evac = 0
        i1_done = [0, 0, 0, 0]
        sgu_cons = [0, 0, 0, 0]
        stash_users = 0
        up_cons = [0, 0]

        for step in range(NKI + 1):
            if step < NKI:
                ki = step
                wb = ki % 2
                p.wait("sync", "vv", wg_users[wb])
                p.wait("sync", "ss", wg_users_ss[wb])
                p.dma("sync", s_wg[wb], T["w_gate"][ki * 128:(ki + 1) * 128, :], sem=f"dwg{wb}")
                p.dma("sync", s_wu[wb], T["w_up"][ki * 128:(ki + 1) * 128, :], sem=f"dwg{wb}")
                wg_ld[wb] = p.c(f"dwg{wb}")
                p.wait("vector", f"dwg{wb}", wg_ld[wb])
                for wi, wsrc, wTdst in ((0, s_wg[wb], wgT2[wb]), (1, s_wu[wb], wuT2[wb])):
                    q4 = wb * 2 + wi
                    tsc = s_tsc[:, q4 * 4:q4 * 4 + 4]
                    sgs = (ki % 4) * 4 + 2 * wi
                    p.op("vector", lambda e, wsrc=wsrc, tsc=tsc: e.tensor_reduce(
                        out=tsc[:, 0:1], in_=wsrc, axis=mybir.AxisListType.X, op=ALU.max,
                        apply_absolute_value=True), inc=None)
                    p.op("vector", lambda e, wsrc=wsrc, tsc=tsc: e.tensor_reduce(
                        out=tsc[:, 1:2], in_=wsrc, axis=mybir.AxisListType.X, op=ALU.max,
                        apply_absolute_value=True), inc=None)
                    p.wait("vector", "ss", sgu_cons[ki % 4])
                    p.op("vector", lambda e, tsc=tsc, sgs=sgs: e.tensor_scalar(
                        out=s_sgu[:, sgs:sgs + 2], in0=tsc[:, 0:2], scalar1=1.0 / QM,
                        scalar2=None, op0=ALU.mult), inc=None)
                    p.op("vector", lambda e, sgs=sgs: e.tensor_tensor(
                        out=s_sgu[:, sgs:sgs + 2], in0=s_sgu[:, sgs:sgs + 2],
                        in1=s_sx[:, 0:2], op=ALU.mult), inc=None)
                    p.op("vector", lambda e, tsc=tsc: e.reciprocal(tsc[:, 0:2], tsc[:, 0:2]),
                         inc=None)
                    p.op("vector", lambda e, tsc=tsc: e.tensor_scalar(
                        out=tsc[:, 0:2], in0=tsc[:, 0:2], scalar1=QM, scalar2=None,
                        op0=ALU.mult), inc="vv")
                    p.op("vector", lambda e, wsrc=wsrc, wb=wb, tsc=tsc: e.tensor_scalar(
                        out=s_wqu2[wb][:, 0:1024], in0=wsrc, scalar1=tsc[:, 0:1],
                        scalar2=256.5, op0=ALU.mult, op1=ALU.add), inc="vv")
                    p.wait("vector", "tt", w16_users[wb])
                    w_ok = quant_to_int16(s_wqu2[wb][:, 0:1024], s_wqd2[wb][:, 0:1024],
                                          w162[wb][:, 0:1024], 1024, clip=False)
                    p.wait("tensor", "vv", max(w_ok, wb45_evac))
                    for i in range(8):
                        kb = 4 + i // 4
                        p.op("tensor", lambda e, i=i, kb=kb, wb=wb: e.matmul(
                            pb[kb][:, (i % 4) * 128:(i % 4) * 128 + 128],
                            w162[wb][:, i * 128:(i + 1) * 128], C.s_id16,
                            start=True, stop=True), inc="tt" if i == 7 else None)
                    w16_users[wb] = p.c("tt")
                    p.wait("vector", "tt", w16_users[wb])
                    p.wait("vector", "tt", wT_users[wb])
                    for kb in (4, 5):
                        p.op("vector", lambda e, kb=kb, wTdst=wTdst: e.tensor_copy(
                            out=wTdst[:, (kb - 4) * 512:(kb - 3) * 512], in_=pb[kb][:, 0:512]),
                            inc="vv" if kb == 5 else None)
                    wT_ok[wb] = p.c("vv")
                    wb45_evac = wT_ok[wb]
                wg_users[wb] = p.c("vv")
                wg_users_ss[wb] = p.c("ss")

            if step >= 1:
                ki = step - 1
                wb = ki % 2
                p.wait("tensor", "vv", max(wT_ok[wb], xq_ok))
                for wi in (0, 1):
                    wTdst = wgT2[wb] if wi == 0 else wuT2[wb]
                    sgs = (ki % 4) * 4 + 2 * wi
                    for tg in range(NG):
                        bank = (wi * 2) + (tg % 2)
                        p.wait("tensor", "ss", bank_ss[bank])
                        for hc in range(8):
                            p.op("tensor", lambda e, hc=hc, tg=tg, bank=bank, wTdst=wTdst: e.matmul(
                                pb[bank][:, 0:512], wTdst[:, hc * 128:(hc + 1) * 128],
                                s_xq[:, hc * 2048 + tg * 512: hc * 2048 + tg * 512 + 512],
                                start=(hc == 0), stop=(hc == 7)), inc="tt" if hc == 7 else None)
                        gmm = p.c("tt")
                        p.wait("scalar", "tt", gmm)
                        if wi == 0:
                            p.wait("scalar", "vv", stash_users)
                            p.op("scalar", lambda e, tg=tg, bank=bank, sgs=sgs: e.activation(
                                out=s_stash[:, tg * 512:(tg + 1) * 512], in_=pb[bank][:, 0:512],
                                func=AF.Silu, bias=C.constv(0.0), scale=s_sgu[:, sgs:sgs + 1]),
                                inc="ss")
                            bank_ss[bank] = p.c("ss")
                        else:
                            tb = (ki * NG + tg) % 2
                            p.wait("scalar", "vv", up_cons[tb])
                            p.op("scalar", lambda e, tg=tg, bank=bank, sgs=sgs, tb=tb: e.activation(
                                out=s_up2[tb][:, 0:512], in_=pb[bank][:, 0:512],
                                func=AF.Identity, bias=C.constv(0.0),
                                scale=s_sgu[:, sgs:sgs + 1]), inc="ss")
                            up_ok = p.c("ss")
                            bank_ss[bank] = up_ok
                            p.wait("vector", "ss", up_ok)
                            p.wait("vector", f"dtw{tb}", t_sp_done[tb])
                            p.op("vector", lambda e, tg=tg, tb=tb: e.tensor_tensor(
                                out=s_t2[tb][:, 0:512], in0=s_stash[:, tg * 512:(tg + 1) * 512],
                                in1=s_up2[tb][:, 0:512], op=ALU.mult), inc=None)
                            p.op("vector", lambda e, tb=tb: e.tensor_reduce(
                                out=s_tmp1[:, 3:4], in_=s_t2[tb][:, 0:512],
                                axis=mybir.AxisListType.X,
                                op=ALU.max, apply_absolute_value=True), inc=None)
                            p.op("vector", lambda e: e.tensor_tensor(
                                out=s_tmax[:, 0:1], in0=s_tmax[:, 0:1], in1=s_tmp1[:, 3:4],
                                op=ALU.max), inc="vv")
                            t_ok = p.c("vv")
                            up_cons[tb] = t_ok
                            if tg == NG - 1:
                                stash_users = t_ok
                            p.wait("sync", "vv", t_ok)
                            t_sp_done[tb] = p.dma("sync",
                                SP["t_sp"][:, tg * NKI * 512 + ki * 512:
                                           tg * NKI * 512 + ki * 512 + 512],
                                s_t2[tb][:, 0:512], sem=f"dtw{tb}")
                wT_users[wb] = p.c("tt")
                sgu_cons[ki % 4] = p.c("ss")

        p.wait("sync", "vv", p.c("vv"))
        mxs = p.dma("sync", dap(SP["mx_sp"], 0, [[1, 128], [1, 1]]), s_tmax[:, 0:1], sem="dqo")
        p.wait("sync", "dqo", mxs)
        mxl = p.dma("sync", s_mpad[0:1, 0:128], SP["mx_sp"][0:1, 0:128], sem="dqo")
        p.wait("vector", "dqo", mxl)
        p.op("vector", lambda e: e.tensor_reduce(
            out=s_tmp1[0:1, 0:1], in_=s_mpad[0:1, 0:128], axis=mybir.AxisListType.X,
            op=ALU.max), inc=None)
        pst = s_tmp1.ap[0][0]
        p.op("vector", lambda e: e.tensor_copy(
            out=s_g64[0:1, 0:8],
            in_=bass.AP(s_tmp1.tensor, s_tmp1.offset, [[pst, 1], [0, 8]])), inc="vv")
        m2rep = p.c("vv")
        p.wait("sync", "vv", m2rep)
        agm2_in = p.dma("sync", SP["ag_m2_in"].ap(), s_g64[0:1, 0:8], sem="dqo")
        p.wait("sync", "dqo", agm2_in)
        p.wait("gpsimd", "dqo", agm2_in)
        p.op("gpsimd", lambda e: e.collective_compute(
            "AllGather", ALU.bypass, replica_groups=[list(range(8))],
            ins=[SP["ag_m2_in"].ap()], outs=[SP["ag_m2_out"].ap()]), inc="cc")
        agm2 = p.c("cc")
        p.wait("sync", "cc", agm2)
        p.wait("sync", "dtw0", p.c("dtw0"))
        p.wait("sync", "dtw1", p.c("dtw1"))
        if C.debug:
            p.dma("sync", T["dbg_t"], SP["t_sp"][:, 0:2048], sem="dqo")
        g2_ok = repmax_scalar(SP["ag_m2_out"], None, s_tmp1[:, 0:2])
        s_s2 = af.alloc(4, "s2")
        s_inv2 = af.alloc(4, "inv2")
        p.wait("scalar", "vv", g2_ok)
        s2_ok = sig_scale(C.s_clipdn, s_tmp1[:, 0:2], s_s2[:, 0:2], s_inv2[:, 0:2])
        p.op("vector", lambda e: e.tensor_copy(out=C.s_scal[:, 0:2], in_=s_s2[:, 0:2]), inc=None)
        p.op("vector", lambda e: e.tensor_copy(out=C.s_scal[:, 2:4], in_=s_inv2[:, 0:2]), inc="vv")
        if C.debug:
            scok = p.c("vv")
            p.wait("sync", "vv", scok)
            dd = p.dma("sync", T["dbg_sc"][:, 8:10], C.s_scal[:, 0:2], sem="dqo")
            p.wait("vector", "dqo", dd)
        p.barrier()


def emit_phase_f2(C):
    p, pb, T, SP, nc = C.p, C.pb, C.T, C.SP, C.nc
    QM = 127.0
    CLIPLO = 128.0
    CLIPHI = 383.99996948242188

    with ExitStack() as es:
        af = Arena(es.enter_context(nc.sbuf_tensor("af2", [128, 23040], F32)), 23040)
        ah = Arena(es.enter_context(nc.sbuf_tensor("af2h", [128, 36608], BF16)), 36608)
        ai = Arena(es.enter_context(nc.sbuf_tensor("ai2", [128, 2944], I32)), 2944)
        ci2 = ai.alloc(2944, "ci32b")
        s_s2 = C.s_scal[:, 0:1]
        s_inv2 = C.s_scal[:, 2:3]

        if C.debug:
            dd = p.dma("sync", T["dbg_sc"][:, 10:14], C.s_scal[:, 0:4], sem="dqo")
        h2 = ah.alloc(11264, "h2")
        wdT = ah.alloc(22528, "wdT")
        w16d = ah.alloc(2816, "w16d")
        s_wd = af.alloc(2816, "wd")
        s_u = af.alloc(2816, "ud")
        s_cu = af.alloc(2816, "cud")
        s_sdo = af.alloc(32, "sdo")
        s_tslab = af.alloc(2816, "tslab")
        s_y1tg = af.alloc(4096, "y1tg")
        s_y2 = af.alloc(4096, "y2")
        s_mb = af.alloc(512, "mb2")
        s_t2s = af.alloc(512, "t22")
        s_uu = af.alloc(512, "uu2")
        s_var = af.alloc(512, "var2")
        s_r = af.alloc(512, "r2")
        s_ysq = af.alloc(512, "ysq2")
        s_tmp1 = af.alloc(4, "tmp1b")

        def quant_to_int16b(u_ap, cu_ap, out16_ap, n, clip):
            if clip:
                p.op("vector", lambda e: e.tensor_scalar(
                    out=cu_ap, in0=u_ap, scalar1=CLIPLO, scalar2=CLIPHI,
                    op0=ALU.max, op1=ALU.min), inc=None)
                cua = cu_ap
            else:
                cua = u_ap
            p.op("vector", lambda e, cua=cua: e.tensor_copy(out=ci2[:, 0:n], in_=cua), inc=None)
            dd = cu_ap if not clip else u_ap
            p.op("vector", lambda e, cua=cua, dd=dd: e.tensor_tensor(
                out=dd, in0=ci2[:, 0:n], in1=cua, op=ALU.is_gt), inc=None)
            p.op("vector", lambda e, dd=dd: e.scalar_tensor_tensor(
                out=out16_ap, in0=ci2[:, 0:n], scalar=-256.0, in1=dd,
                op0=ALU.add, op1=ALU.subtract), inc="vv")
            return p.c("vv")

        wd_ld = 0
        w16d_users = 0
        for hc in range(8):
            p.wait("sync", "vv", p.c("vv"))
            wd_ld = p.dma("sync", s_wd, T["w_down"][hc * 128:(hc + 1) * 128, :], sem="dwd")
            p.wait("vector", "dwd", wd_ld)
            p.op("vector", lambda e: e.tensor_reduce(
                out=s_tmp1[:, 2:3], in_=s_wd, axis=mybir.AxisListType.X, op=ALU.max,
                apply_absolute_value=True), inc=None)
            p.op("vector", lambda e: e.tensor_reduce(
                out=s_tmp1[:, 3:4], in_=s_wd, axis=mybir.AxisListType.X, op=ALU.max,
                apply_absolute_value=True), inc=None)
            p.op("vector", lambda e, hc=hc: e.tensor_scalar(
                out=s_tmp1[:, 0:2], in0=s_tmp1[:, 2:4], scalar1=1.0 / QM,
                scalar2=None, op0=ALU.mult), inc=None)
            p.op("vector", lambda e, hc=hc: e.scalar_tensor_tensor(
                out=s_sdo[:, 2 * hc:2 * hc + 2],
                in0=s_tmp1[:, 0:2], scalar=1.0, in1=C.s_scal[:, 0:2],
                op0=ALU.mult, op1=ALU.mult), inc=None)
            p.op("vector", lambda e: e.reciprocal(s_tmp1[:, 2:4], s_tmp1[:, 2:4]), inc=None)
            p.op("vector", lambda e: e.tensor_scalar(
                out=s_tmp1[:, 2:4], in0=s_tmp1[:, 2:4], scalar1=QM, scalar2=None,
                op0=ALU.mult), inc="vv")
            p.op("vector", lambda e: e.tensor_scalar(
                out=s_u[:, 0:2816], in0=s_wd, scalar1=s_tmp1[:, 2:3],
                scalar2=256.5, op0=ALU.mult, op1=ALU.add), inc="vv")
            p.wait("vector", "tt", w16d_users)
            w_ok = quant_to_int16b(s_u[:, 0:2816], s_cu[:, 0:2816], w16d[:, 0:2816],
                                   2816, clip=False)
            p.wait("tensor", "vv", w_ok)
            done = 0
            for wv in range(3):
                ntile = 8 if wv < 2 else 6
                for i in range(ntile):
                    ti = wv * 8 + i
                    kb = 4 + i // 4
                    p.op("tensor", lambda e, ti=ti, i=i, kb=kb: e.matmul(
                        pb[kb][:, (i % 4) * 128:(i % 4) * 128 + 128],
                        w16d[:, ti * 128:(ti + 1) * 128], C.s_id16,
                        start=True, stop=True), inc="tt" if i == ntile - 1 else None)
                trw = p.c("tt")
                p.wait("vector", "tt", trw)
                for kb in (4, 5):
                    nn = min(512, (ntile - (kb - 4) * 4) * 128)
                    if nn <= 0:
                        continue
                    p.op("vector", lambda e, kb=kb, nn=nn, wv=wv, hc=hc: e.tensor_copy(
                        out=wdT[:, hc * 2816 + wv * 1024 + (kb - 4) * 512:
                                hc * 2816 + wv * 1024 + (kb - 4) * 512 + nn],
                        in_=pb[kb][:, 0:nn]), inc="vv")
                evw = p.c("vv")
                p.wait("tensor", "vv", evw)
            w16d_users = p.c("tt")
        wdT_ok = p.c("vv")

        s_tslab2 = [s_tslab, s_wd]
        i1_cnt = [p.c("vv"), p.c("vv")]
        t_ld2 = [0, 0]
        h2_cons = 0
        y2_out_done = 0
        qk = 0
        for tg in range(NG):
            p.wait("sync", "vv", p.c("vv"))
            y1tg_ld = p.dma("sync", s_y1tg,
                            dap(SP["y1_sp"], tg * 512, [[8 * TCORE, 128], [TCORE, 8], [1, 512]]),
                            sem="dyr")
            for sb in range(4):
                gp = (tg * 4 + sb) % 2
                p.wait("sync", "vv", i1_cnt[gp])
                t_ld2[gp] = p.dma("sync", s_tslab2[gp],
                             SP["t_sp"][:, tg * NKI * 512 + sb * 2816:
                                        tg * NKI * 512 + (sb + 1) * 2816], sem=f"dtr{gp}")
                p.wait("vector", f"dtr{gp}", t_ld2[gp])
                p.op("vector", lambda e, gp=gp: e.tensor_scalar(
                    out=s_u[:, 0:2816], in0=s_tslab2[gp], scalar1=s_inv2,
                    scalar2=256.5, op0=ALU.mult, op1=ALU.add), inc="vv")
                i1_cnt[gp] = p.c("vv")
                p.wait("vector", "tt", h2_cons)
                qk = quant_to_int16b(s_u[:, 0:2816], s_cu[:, 0:2816],
                                h2[:, sb * 2816:(sb + 1) * 2816], 2816, clip=True)
                if C.debug and tg == 0 and sb == 0:
                    p.wait("sync", "vv", qk)
                    dd = p.dma("sync", T["dbg_q"][:, 0:2816], s_u[:, 0:2816], sem="dqo")
                    dd = p.dma("sync", T["dbg_q"][:, 2816:5632], s_cu[:, 0:2816], sem="dqo")
                    p.wait("scalar", "dqo", dd)
                    p.wait("vector", "dqo", dd)
            h2_ok = p.c("vv")
            if C.debug and tg == 0:
                p.wait("sync", "vv", h2_ok)
                dd = p.dma("sync", T["dbg_h2"], h2[:, 0:11264], sem="dqo")
                p.wait("vector", "dqo", dd)
            p.wait("tensor", "vv", max(h2_ok, wdT_ok))
            bank_vv = [0, 0]
            for hc in range(8):
                bank = hc % 2
                p.wait("tensor", "vv", bank_vv[bank])
                for ki in range(NKI):
                    p.op("tensor", lambda e, hc=hc, ki=ki, bank=bank: e.matmul(
                        pb[bank][:, 0:512], wdT[:, (hc * 22 + ki) * 128:(hc * 22 + ki + 1) * 128],
                        h2[:, ki * 512:(ki + 1) * 512],
                        start=(ki == 0), stop=(ki == NKI - 1)), inc="tt" if ki == NKI - 1 else None)
                dmm = p.c("tt")
                p.wait("vector", "tt", dmm)
                p.wait("vector", "dyr", y1tg_ld)
                p.op("vector", lambda e, hc=hc, bank=bank: e.scalar_tensor_tensor(
                    out=s_y2[:, hc * 512:(hc + 1) * 512], in0=pb[bank][:, 0:512],
                    scalar=s_sdo[:, 2 * hc:2 * hc + 1], in1=s_y1tg[:, hc * 512:(hc + 1) * 512],
                    op0=ALU.mult, op1=ALU.add), inc="vv")
                bank_vv[bank] = p.c("vv")
            y2_ok = p.c("vv")
            if C.debug and tg == 0:
                p.wait("sync", "vv", y2_ok)
                dd = p.dma("sync", T["dbg_y2"], s_y2[:, 0:4096], sem="dqo")
                p.wait("scalar", "dqo", dd)
                p.wait("vector", "dqo", dd)
            h2_cons = p.c("tt")
            s2mm = 0
            p.wait("scalar", "vv", y2_ok)
            for mg in range(8):
                p.wait("scalar", "tt", s2mm)
                p.op("scalar", lambda e, mg=mg: e.activation(
                    out=s_ysq[:, 0:512], in_=s_y2[:, mg * 512:(mg + 1) * 512],
                    func=AF.Square, bias=C.constv(0.0), scale=1.0), inc="ss")
                sqd = p.c("ss")
                p.wait("tensor", "vv", y2_ok)
                p.op("tensor", lambda e, mg=mg: e.matmul(
                    pb[4][:, 0:512], C.s_ones, s_y2[:, mg * 512:(mg + 1) * 512],
                    start=(mg == 0), stop=(mg == 7)), inc=None)
                p.wait("tensor", "ss", sqd)
                p.op("tensor", lambda e, mg=mg: e.matmul(
                    pb[5][:, 0:512], C.s_ones, s_ysq[:, 0:512],
                    start=(mg == 0), stop=(mg == 7)), inc="tt")
                s2mm = p.c("tt")
            statmm = p.c("tt")
            p.wait("vector", "tt", statmm)
            p.op("vector", lambda e: e.tensor_scalar(
                out=s_mb[:, 0:512], in0=pb[4][:, 0:512], scalar1=1.0 / D, scalar2=None,
                op0=ALU.mult), inc=None)
            p.op("vector", lambda e: e.tensor_scalar(
                out=s_t2s[:, 0:512], in0=pb[5][:, 0:512], scalar1=1.0 / D, scalar2=None,
                op0=ALU.mult), inc=None)
            p.op("vector", lambda e: e.tensor_tensor(
                out=s_uu[:, 0:512], in0=s_mb[:, 0:512], in1=s_mb[:, 0:512], op=ALU.mult),
                inc=None)
            p.op("vector", lambda e: e.scalar_tensor_tensor(
                out=s_var[:, 0:512], in0=s_t2s[:, 0:512], scalar=EPS_LN, in1=s_uu[:, 0:512],
                op0=ALU.add, op1=ALU.subtract), inc="vv")
            var_ok = p.c("vv")
            p.wait("scalar", "vv", var_ok)
            p.op("scalar", lambda e: e.activation(
                out=s_r[:, 0:512], in_=s_var[:, 0:512], func=AF.Ln,
                bias=C.constv(0.0), scale=1.0), inc=None)
            p.op("scalar", lambda e: e.activation(
                out=s_r[:, 0:512], in_=s_r[:, 0:512], func=AF.Exp,
                bias=C.constv(0.0), scale=-0.5), inc="ss")
            r_ok = p.c("ss")
            p.wait("vector", "ss", r_ok)
            p.wait("vector", "dout", y2_out_done)
            for mg in range(8):
                p.op("vector", lambda e, mg=mg: e.tensor_tensor(
                    out=s_ysq[:, 0:512], in0=s_y2[:, mg * 512:(mg + 1) * 512],
                    in1=s_mb[:, 0:512], op=ALU.subtract), inc=None)
                p.op("vector", lambda e, mg=mg: e.tensor_tensor(
                    out=s_y2[:, mg * 512:(mg + 1) * 512], in0=s_ysq[:, 0:512],
                    in1=s_r[:, 0:512], op=ALU.mult), inc="vv" if mg == 7 else None)
            y2n_ok = p.c("vv")
            p.wait("sync", "vv", y2n_ok)
            y2_out_done = p.dma("sync",
                dap(T["out_t"], tg * 512, [[TCORE, 128], [128 * TCORE, 8], [1, 512]]),
                s_y2[:, 0:4096].rearrange("p (c t) -> p c t", c=8), sem="dout")
    C.out_done = p.c("dout")


def emit_finish(C):
    p, T = C.p, C.T
    p.barrier()
    if C.stage == "F":
        p.wait("sync", "dout", C.out_done)
    else:
        fin = p.dma("sync", T["out_t"][0:128, 0:128], C.s_state[:, 0:128])
        p.wait("sync", "dq", fin)


def _stage_inputs(L0_attn_x, L0_attn_norm_w, L0_attn_Wq, L0_attn_Wk, L0_attn_Wv,
                  L0_attn_Wqf, L0_attn_Wkf, L0_attn_Wout, L0_ffn_clip_x,
                  L0_ffn_clip_down, L0_ffn_w_gate, L0_ffn_w_up, L0_ffn_w_down):
    f32 = np.float32
    x = np.asarray(L0_attn_x, f32)

    def fpad(w, hi):
        wp = np.zeros((128, 128), f32)
        if hi:
            wp[64:128, 0:16] = np.asarray(w, f32)
        else:
            wp[0:64, 0:16] = np.asarray(w, f32)
        return wp

    r = np.arange(128)
    m_diag = ((r[:, None] <= r[None, :]) & (r[:, None] >= r[None, :] - 64)).astype(f32)
    m_prev = (r[:, None] >= r[None, :] + 64).astype(f32)
    m_tril = (r[:, None] <= r[None, :]).astype(f32)
    ident = np.eye(128, dtype=f32)

    import ml_dtypes
    ident16 = np.eye(128, dtype=ml_dtypes.bfloat16)

    common = dict(
        wq=np.ascontiguousarray(np.asarray(L0_attn_Wq, f32)),
        wk=np.ascontiguousarray(np.asarray(L0_attn_Wk, f32)),
        wv=np.ascontiguousarray(np.asarray(L0_attn_Wv, f32)),
        wqf_lo=fpad(L0_attn_Wqf, 0), wqf_hi=fpad(L0_attn_Wqf, 1),
        wkf_lo=fpad(L0_attn_Wkf, 0), wkf_hi=fpad(L0_attn_Wkf, 1),
        wout=np.ascontiguousarray(np.asarray(L0_attn_Wout, f32)),
        normw=np.ascontiguousarray(np.asarray(L0_attn_norm_w, f32).reshape(8, 128).T),
        m_diag=m_diag, m_prev=m_prev, m_tril=m_tril, ident=ident, ident16=ident16,
        clip_x=np.full((128, 2), np.asarray(L0_ffn_clip_x, f32)[0], f32),
        clip_dn=np.full((128, 2), np.asarray(L0_ffn_clip_down, f32)[0], f32),
        w_gate=np.ascontiguousarray(np.asarray(L0_ffn_w_gate, f32)),
        w_up=np.ascontiguousarray(np.asarray(L0_ffn_w_up, f32)),
        w_down=np.ascontiguousarray(np.asarray(L0_ffn_w_down, f32)),
    )

    in_maps = []
    for core in range(8):
        b, half = core // 2, core % 2
        t0 = half * TCORE
        xs = np.zeros((NTT * 128, D), f32)
        if half == 1:
            xs[0:128] = x[b, t0 - 128:t0]
        xs[128:] = x[b, t0:t0 + TCORE]
        m = dict(common)
        m["x_t"] = np.ascontiguousarray(xs.T)
        m["m_first"] = m_prev if half == 1 else np.zeros_like(m_prev)
        m["lin_flag"] = np.full((128, 1), f32(half))
        in_maps.append(m)
    return in_maps


def kernel(**inputs):
    in_maps = _stage_inputs(**inputs)
    nc = build_program(stage="F", debug=False)
    res = run_bass_kernel_spmd(nc, in_maps, core_ids=list(range(8)))
    out = np.empty((4, 4096, D), np.float32)
    for core in range(8):
        b, half = core // 2, core % 2
        out[b, half * TCORE:(half + 1) * TCORE] = np.asarray(res.results[core]["out_t"]).T
    return out


# version 17
# speedup vs baseline: 1.5862x; 1.0035x over previous
import numpy as np
from contextlib import ExitStack

import concourse.bass as bass
import concourse.mybir as mybir
from concourse.bass_utils import run_bass_kernel_spmd

F32 = mybir.dt.float32
BF16 = mybir.dt.bfloat16
I32 = mybir.dt.int32
AF = mybir.ActivationFunctionType
ALU = mybir.AluOpType

D = 1024
H = 16
HD = 64
FT = 16
TCORE = 2048
NTT = 17
NG = 4
NCH = 8
DI = 2816
NKI = 22
EPS_NORM = 1e-6
EPS_LIN = 1e-6
EPS_LN = 1e-5

ENGS = ("sync", "vector", "scalar", "tensor", "gpsimd")
PARANOID = False
EMIT_WIN = True
EMIT_LIN = True
LIN_INTRA = True
LIN_INTER = True
INTER_KK = (0, 1, 2)
INTER_SU_ONES = False


class Prog:
    def __init__(self):
        self.ops = []
        self.cnt = {}

    def c(self, sem):
        return self.cnt.get(sem, 0)

    def bump(self, sem, n):
        self.cnt[sem] = self.c(sem) + n
        return self.cnt[sem]

    def wait(self, eng, sem, val=None):
        if val is None:
            val = self.c(sem)
        if val > 0:
            self.ops.append((eng, "wait", (sem, val)))

    def op(self, eng, fn, inc=None, inc_n=1):
        if eng == "vector":
            assert inc in (None, "vv")
            self.wait(eng, "vv")
            inc, inc_n = "vv", 1
        elif eng == "scalar":
            assert inc in (None, "ss")
            self.wait(eng, "ss")
            inc, inc_n = "ss", 1
        tgt = self.bump(inc, inc_n) if inc is not None else None
        self.ops.append((eng, "op", (fn, inc, inc_n)))
        return tgt

    def dma(self, eng, out, in_, sem="dq"):
        tgt = self.bump(sem, 16)
        self.ops.append((eng, "dma", (out, in_, sem)))
        if sem == "dqo":
            self.ops.append((eng, "wait", (sem, tgt)))
        return tgt

    def memset(self, eng, ap, const, sem="ms"):
        tgt = self.bump(sem, 16)
        self.ops.append((eng, "memset", (ap, const, sem)))
        return tgt

    def sync3(self, *engs):
        sems = [s for s in self.cnt if s not in ("bar", "cc")]
        for e in engs:
            for s in sems:
                self.wait(e, s)
        for s in ("tt", "vv", "ss"):
            self.wait("sync", s)

    def barrier(self):
        tgt = self.bump("bar", len(ENGS))
        for e in ENGS:
            self.ops.append((e, "barinc", None))
        for e in ENGS:
            self.ops.append((e, "wait", ("bar", tgt)))

    def run(self, engname, eng, sems):
        lastw = {}
        for e, kind, data in self.ops:
            if e != engname:
                continue
            if kind == "wait":
                sem, val = data
                if lastw.get(sem, -1) >= val:
                    continue
                eng.wait_ge(sems[sem], val)
                lastw[sem] = val
            elif kind == "op":
                fn, inc, inc_n = data
                inst = fn(eng)
                if inc is not None:
                    inst.then_inc(sems[inc], inc_n)
            elif kind == "dma":
                out, in_, sem = data
                eng.dma_start(out=out, in_=in_).then_inc(sems[sem], 16)
            elif kind == "memset":
                ap, const, sem = data
                eng.memset(ap, const).then_inc(sems[sem], 16)
            elif kind == "barinc":
                eng.sem_inc(sems["bar"], 1)


class Arena:
    def __init__(self, handle, words):
        self.h = handle
        self.words = words
        self.off = 0
        self.peak = 0

    def alloc(self, n, name=""):
        n = (n + 1) & ~1
        assert self.off + n <= self.words, (
            f"SBUF arena overflow at {name}: {self.off}+{n}>{self.words}")
        ap = self.h[:, self.off:self.off + n]
        self.off += n
        self.peak = max(self.peak, self.off)
        return ap


def dap(t, offset, dims):
    return bass.AP(t.tensor, offset, dims)


class Ctx:
    pass


def build_program(stage="F", debug=False):
    nc = bass.Bass("TRN2", target_bir_lowering=False, debug=False)
    p = Prog()
    T = {}

    def din(name, shape, dt=F32):
        T[name] = nc.dram_tensor(name, shape, dt, kind="ExternalInput").ap()

    din("x_t", [D, NTT * 128])
    din("wq", [D, D])
    din("wk", [D, D])
    din("wv", [D, D])
    din("wqf_lo", [128, 128])
    din("wqf_hi", [128, 128])
    din("wkf_lo", [128, 128])
    din("wkf_hi", [128, 128])
    din("wout", [2048, D])
    din("normw", [128, 8])
    din("m_diag", [128, 128])
    din("m_prev", [128, 128])
    din("m_first", [128, 128])
    din("m_tril", [128, 128])
    din("ident", [128, 128])
    din("ident16", [128, 128], BF16)
    din("lin_flag", [128, 1])
    din("clip_x", [128, 2])
    din("clip_dn", [128, 2])
    din("w_gate", [DI, D])
    din("w_up", [DI, D])
    din("w_down", [D, DI])

    T["out_t"] = nc.dram_tensor("out_t", [D, TCORE], F32, kind="ExternalOutput").ap()

    SP = {}
    for name, shape in (("h_sp", [128, 4 * 4096]), ("k_sp", [128, 8 * NTT * 128]),
                        ("v_sp", [128, NTT * 1024]), ("y1_sp", [128, 8 * TCORE]),
                        ("pref_sp", [128, NCH * 6144]), ("t_sp", [128, NKI * TCORE])):
        SP[name] = nc.dram_tensor(name, shape, F32).ap()
    SP["mx_sp"] = nc.dram_tensor("mx_sp", [1, 128], F32).ap()
    for name, shape in (("ag_st_in", [128, 6144]), ("ag_st_out", [256, 6144]),
                        ("ag_m1_in", [1, 8]), ("ag_m1_out", [8, 8]),
                        ("ag_m2_in", [1, 8]), ("ag_m2_out", [8, 8])):
        SP[name] = nc.dram_tensor(name, shape, F32)

    if debug and stage == "F":
        T["dbg_q"] = nc.dram_tensor("dbg_q", [128, 5632], F32, kind="ExternalOutput").ap()
        T["dbg_sc"] = nc.dram_tensor("dbg_sc", [128, 16], F32, kind="ExternalOutput").ap()
        T["dbg_chain"] = nc.dram_tensor("dbg_chain", [128, 32], F32, kind="ExternalOutput").ap()
        T["dbg_xq"] = nc.dram_tensor("dbg_xq", [128, 16384], BF16, kind="ExternalOutput").ap()
        T["dbg_t"] = nc.dram_tensor("dbg_t", [128, 2048], F32, kind="ExternalOutput").ap()
        T["dbg_h2"] = nc.dram_tensor("dbg_h2", [128, 11264], BF16, kind="ExternalOutput").ap()
        T["dbg_y2"] = nc.dram_tensor("dbg_y2", [128, 4096], F32, kind="ExternalOutput").ap()
    if debug:
        T["dbg_state"] = nc.dram_tensor("dbg_state", [128, 6144], F32, kind="ExternalOutput").ap()
        T["dbg_k"] = nc.dram_tensor("dbg_k", [128, 2176], F32, kind="ExternalOutput").ap()
        T["dbg_h"] = nc.dram_tensor("dbg_h", [128, 4096], F32, kind="ExternalOutput").ap()
        if stage in ("Q", "F"):
            T["dbg_y1"] = nc.dram_tensor("dbg_y1", [128, 8 * TCORE], F32, kind="ExternalOutput").ap()
            T["dbg_cat"] = nc.dram_tensor("dbg_cat", [128, 4096], F32, kind="ExternalOutput").ap()
            T["dbg_yh"] = nc.dram_tensor("dbg_yh", [128, 3072], F32, kind="ExternalOutput").ap()
            T["dbg_ff"] = nc.dram_tensor("dbg_ff", [128, 1280], F32, kind="ExternalOutput").ap()
            T["dbg_su"] = nc.dram_tensor("dbg_su", [128, 12288], F32, kind="ExternalOutput").ap()

    with ExitStack() as es:
        C = Ctx()
        C.p, C.nc, C.T, C.SP, C.stage, C.debug = p, nc, T, SP, stage, debug
        pers = es.enter_context(nc.sbuf_tensor("pers", [128, 1426], F32))
        pers16 = es.enter_context(nc.sbuf_tensor("pers16", [128, 128], BF16))
        st_t = es.enter_context(nc.sbuf_tensor("st_t", [128, 6144], F32))
        C.pb = [es.enter_context(nc.psum_tensor(f"pb{i}", [128, 512], F32)) for i in range(7)]
        C.pbh = es.enter_context(nc.psum_tensor("pbh", [128, 1024], BF16))
        sems = {}

        PA = Arena(pers, 1426)
        C.MD = PA.alloc(128, "m_diag")
        C.MP = PA.alloc(128, "m_prev")
        C.MF = PA.alloc(128, "m_first")
        C.MT = PA.alloc(128, "m_tril")
        C.s_id = PA.alloc(128, "ident")
        C.s_wqf = [PA.alloc(128, "wqf_lo"), PA.alloc(128, "wqf_hi")]
        C.s_wkf = [PA.alloc(128, "wkf_lo"), PA.alloc(128, "wkf_hi")]
        C.s_normw = PA.alloc(8, "normw")
        C.s_flag = PA.alloc(2, "flag")
        C.s_ones = PA.alloc(128, "allones")
        C.s_scal = PA.alloc(16, "scalars")
        C.s_clipx = PA.alloc(2, "clipx")
        C.s_clipdn = PA.alloc(2, "clipdn")
        C.s_id16 = pers16[:]
        C.s_state = st_t[:]
        C.s_const = PA.alloc(16, "consts")
        C.consts = {}

        def constv(val):
            val = float(val)
            if val not in C.consts:
                i = len(C.consts)
                assert i < 16
                C.consts[val] = C.s_const[:, i:i + 1]
            return C.consts[val]
        C.constv = constv

        emit_init(C)
        with nc.sbuf_tensor("akq", [128, 40000], F32) as akq:
            C.A = Arena(akq, 40000)
            emit_phase_k(C)
            if stage != "K":
                C.A = Arena(akq, 40000)
                C.A.off = 0
                emit_phase_q(C)
        if stage == "F":
            emit_phase_f(C)
        emit_finish(C)

        for s in set(list(p.cnt.keys()) + ["bar"]):
            sems[s] = es.enter_context(nc.semaphore(s))
        block = es.enter_context(nc.Block())
        for engname in ENGS:
            getattr(block, engname)(lambda eng, _n=engname: p.run(_n, eng, sems))

    return nc


def emit_init(C):
    p, T = C.p, C.T
    for dst, key in ((C.MD, "m_diag"), (C.MP, "m_prev"), (C.MF, "m_first"),
                     (C.MT, "m_tril"), (C.s_id, "ident"), (C.s_id16, "ident16"),
                     (C.s_wqf[0], "wqf_lo"), (C.s_wqf[1], "wqf_hi"),
                     (C.s_wkf[0], "wkf_lo"), (C.s_wkf[1], "wkf_hi"),
                     (C.s_normw, "normw"), (C.s_flag[:, 0:1], "lin_flag"),
                     (C.s_clipx[:, 0:2], "clip_x"), (C.s_clipdn[:, 0:2], "clip_dn")):
        p.dma("sync", dst, T[key])
    p.memset("vector", C.s_ones, 1.0)
    p.memset("vector", C.s_state, 0.0)
    for v in (0.0, EPS_NORM, EPS_LN, -1.0, 256.5, 1.0):
        p.memset("vector", C.constv(v), v)
    for e in ("vector", "tensor", "scalar"):
        p.wait(e, "dq")
        p.wait(e, "ms")


def sblk(C, h, kk):
    return C.s_state[:, (3 * h + kk) * 128:(3 * h + kk) * 128 + 128]


def x_ap(C, tfirst, ntok):
    return dap(C.T["x_t"], tfirst * 128,
               [[NTT * 128, 128], [128 * NTT * 128, 8], [1, ntok]])


def emit_phase_k(C):
    p, A, pb, T, SP = C.p, C.A, C.pb, C.T, C.SP

    s_wk = A.alloc(8 * 1024, "wk")
    s_wv = A.alloc(8 * 1024, "wv")
    for c in range(8):
        p.dma("sync", s_wk[:, c * 1024:(c + 1) * 1024], T["wk"][c * 128:(c + 1) * 128, :], sem="dwk")
        p.dma("sync", s_wv[:, c * 1024:(c + 1) * 1024], T["wv"][c * 128:(c + 1) * 128, :], sem="dwk")
    wk_ld = p.c("dwk")

    s_x = [A.alloc(8 * 512, "xg0"), A.alloc(8 * 512, "xg1")]
    s_h = A.alloc(8 * 512, "hg")
    s_hsq = A.alloc(8 * 512, "hsq")
    s_rms = A.alloc(512, "rms")
    s_ktmp = [A.alloc(512, "ktmp0"), A.alloc(512, "ktmp1")]
    s_vtile = [A.alloc(1024, "vt0"), A.alloc(1024, "vt1")]
    s_kftm = A.alloc(2 * 256, "kftm")
    s_phik2 = [[A.alloc(384, "phik00"), A.alloc(384, "phik01")],
               [A.alloc(384, "phik10"), A.alloc(384, "phik11")]]
    s_vb2 = [[A.alloc(128, "vb00"), A.alloc(128, "vb01")],
             [A.alloc(128, "vb10"), A.alloc(128, "vb11")]]

    for hp in (0, 1):
        for b in (0, 1):
            p.memset("vector", s_phik2[hp][b], 0.0)
            p.memset("vector", s_phik2[hp][b][:, 32:33], 1.0)
            p.memset("vector", s_vb2[hp][b][:, 64:128], 1.0)
    p.wait("vector", "ms")

    kgroups = [(0, 1)] + [(1 + 4 * g, 4) for g in range(4)]

    def xg_sb(buf, ntok):
        return s_x[buf][:, 0:8 * ntok].rearrange("p (c t) -> p c t", c=8)

    x_done = [0, 0]
    p.dma("sync", xg_sb(0, 128), x_ap(C, 0, 128), sem="dx0")
    x_done[0] = p.c("dx0")

    h_spill_done = 0
    prev_kv_done = 0
    h_done_prev = 0
    state_add_done = 0
    ktmp_free = [0, 0]
    ktmp_spill = [0, 0]
    vtile_spill = [0, 0]
    b23_free = 0
    kf_evac_all = 0

    for gi, (tfirst, ntile) in enumerate(kgroups):
        nt = ntile * 128
        buf = gi % 2
        if gi + 1 < len(kgroups):
            nf, nn = kgroups[gi + 1]
            p.wait("sync", "vv", h_done_prev)
            p.dma("sync", xg_sb(1 - buf, nn * 128), x_ap(C, nf, nn * 128), sem=f"dx{1 - buf}")
            x_done[1 - buf] = p.c(f"dx{1 - buf}")
        xa = s_x[buf]
        p.wait("scalar", f"dx{buf}", x_done[buf])
        p.wait("scalar", "tt", prev_kv_done)
        p.op("scalar", lambda e, xa=xa, nt=nt: e.activation(
            out=s_hsq[:, 0:8 * nt], in_=xa[:, 0:8 * nt], func=AF.Square,
            bias=C.constv(0.0), scale=1.0), inc="ss")
        sq_done = p.c("ss")
        p.wait("tensor", "ss", sq_done)
        for c in range(8):
            p.op("tensor", lambda e, c=c, nt=nt: e.matmul(
                pb[6][:, 0:nt], C.s_ones, s_hsq[:, c * nt:c * nt + nt],
                start=(c == 0), stop=(c == 7)), inc="tt" if c == 7 else None)
        mm_rms = p.c("tt")
        p.wait("scalar", "tt", mm_rms)
        p.op("scalar", lambda e, nt=nt: e.activation(
            out=s_rms[:, 0:nt], in_=pb[6][:, 0:nt], func=AF.Ln,
            bias=C.constv(EPS_NORM), scale=1.0 / D), inc="ss")
        p.op("scalar", lambda e, nt=nt: e.activation(
            out=s_rms[:, 0:nt], in_=s_rms[:, 0:nt], func=AF.Exp,
            bias=C.constv(0.0), scale=-0.5), inc="ss")
        rms_done = p.c("ss")
        p.wait("vector", "ss", rms_done)
        p.wait("vector", "dhs", h_spill_done)
        p.wait("vector", "tt", prev_kv_done)
        for c in range(8):
            p.op("vector", lambda e, c=c, xa=xa, nt=nt: e.scalar_tensor_tensor(
                out=s_h[:, c * nt:c * nt + nt], in0=xa[:, c * nt:c * nt + nt],
                scalar=C.s_normw[:, c:c + 1], in1=s_rms[:, 0:nt],
                op0=ALU.mult, op1=ALU.mult), inc="vv" if c == 7 else None)
        h_done = p.c("vv")
        h_done_prev = h_done
        if tfirst > 0:
            p.wait("sync", "vv", h_done)
            h_spill_done = p.dma("sync",
                SP["h_sp"][:, (gi - 1) * 4096:(gi - 1) * 4096 + 8 * nt],
                s_h[:, 0:8 * nt], sem="dhs")

        if PARANOID:
            p.sync3("tensor", "vector", "scalar")
        p.wait("tensor", "dwk", wk_ld)
        p.wait("tensor", "vv", h_done)
        kf_mm_last = 0
        for m in range(8):
            if PARANOID:
                p.sync3("tensor", "vector")
            bank = m % 2
            p.wait("tensor", "vv", ktmp_free[bank])
            for c in range(8):
                p.op("tensor", lambda e, m=m, c=c, bank=bank, nt=nt: e.matmul(
                    pb[bank][:, 0:nt],
                    s_wk[:, c * 1024 + m * 128: c * 1024 + m * 128 + 128],
                    s_h[:, c * nt:c * nt + nt], start=(c == 0), stop=(c == 7)),
                    inc="tt" if c == 7 else None)
            kmm = p.c("tt")
            p.wait("vector", "tt", kmm)
            p.wait("vector", f"dks{bank}", ktmp_spill[bank])
            p.op("vector", lambda e, bank=bank, nt=nt: e.tensor_copy(
                out=s_ktmp[bank][:, 0:nt], in_=pb[bank][:, 0:nt]), inc="vv")
            ktmp_free[bank] = p.c("vv")
            p.wait("sync", "vv", ktmp_free[bank])
            ktmp_spill[bank] = p.dma("sync",
                SP["k_sp"][:, m * NTT * 128 + tfirst * 128:
                           m * NTT * 128 + (tfirst + ntile) * 128],
                s_ktmp[bank][:, 0:nt], sem=f"dks{bank}")
            if tfirst > 0:
                p.wait("tensor", "vv", max(ktmp_free[bank], kf_evac_all))
                for tl in range(ntile):
                    for par in (0, 1):
                        h = 2 * m + par
                        kb = 4 + tl % 2
                        p.op("tensor", lambda e, tl=tl, par=par, h=h, kb=kb, bank=bank: e.matmul(
                            pb[kb][:, h * 16 + (tl // 2) * 256: h * 16 + (tl // 2) * 256 + 16],
                            s_ktmp[bank][:, tl * 128:(tl + 1) * 128],
                            C.s_wkf[par][:, 0:16], start=True, stop=True),
                            inc="tt" if (tl == ntile - 1 and par == 1) else None)
                kf_mm_last = p.c("tt")

        for tl in range(ntile):
            if PARANOID:
                p.sync3("tensor", "vector", "scalar")
            vbuf = tl % 2
            p.wait("tensor", "vv", b23_free)
            for hf in range(2):
                bank = 2 + hf
                for c in range(8):
                    p.op("tensor", lambda e, tl=tl, hf=hf, c=c, bank=bank, nt=nt: e.matmul(
                        pb[bank][:, 0:512],
                        s_h[:, c * nt + tl * 128: c * nt + tl * 128 + 128],
                        s_wv[:, c * 1024 + hf * 512: c * 1024 + (hf + 1) * 512],
                        start=(c == 0), stop=(c == 7)), inc="tt" if c == 7 else None)
            vmm = p.c("tt")
            p.wait("vector", "tt", vmm)
            p.wait("vector", f"dvs{vbuf}", vtile_spill[vbuf])
            for hf in range(2):
                p.op("vector", lambda e, hf=hf, vbuf=vbuf: e.tensor_copy(
                    out=s_vtile[vbuf][:, hf * 512:(hf + 1) * 512],
                    in_=pb[2 + hf][:, 0:512]), inc="vv" if hf == 1 else None)
            v_evac = p.c("vv")
            b23_free = v_evac
            p.wait("sync", "vv", v_evac)
            vtile_spill[vbuf] = p.dma("sync",
                SP["v_sp"][:, (tfirst + tl) * 1024:(tfirst + tl + 1) * 1024],
                s_vtile[vbuf], sem=f"dvs{vbuf}")

            if tfirst > 0 and tl % 2 == 1:
                cglob = (tfirst - 1) // 2 + tl // 2
                p.wait("vector", "tt", kf_mm_last)
                for b in (0, 1):
                    kb = 4 + (tl - 1 + b) % 2
                    colb = ((tl - 1 + b) // 2) * 256
                    p.op("vector", lambda e, b=b, kb=kb, colb=colb: e.tensor_copy(
                        out=s_kftm[:, b * 256:(b + 1) * 256],
                        in_=pb[kb][:, colb:colb + 256]), inc="vv" if b == 1 else None)
                kf_evac_all = p.c("vv")
                p.wait("sync", "vv", state_add_done if cglob > 0 else 0)
                pref_done = p.dma("sync",
                    SP["pref_sp"][:, cglob * 6144:(cglob + 1) * 6144], C.s_state, sem="dps")
                upd_aps = [
                    [pb[0][:, 0:128], pb[1][:, 0:128], pb[6][:, 0:128]],
                    [pb[4][:, 0:128], pb[5][:, 0:128], pb[6][:, 128:256]],
                ]
                phik_done = [0, 0]
                upd_mm2 = [0, 0]
                add_done2 = [0, 0]
                for hstep in range(H + 1):
                    if hstep < H:
                        h = hstep
                        hp = h % 2
                        p.wait("vector", "tt", upd_mm2[hp])
                        for b in (0, 1):
                            vtl = (tl - 1 + b) % 2
                            p.op("vector", lambda e, b=b, h=h, vtl=vtl, hp=hp: e.tensor_copy(
                                out=s_vb2[hp][b][:, 0:64],
                                in_=s_vtile[vtl][:, h * 64:(h + 1) * 64]), inc=None)
                            p.op("vector", lambda e, b=b, h=h, hp=hp: e.tensor_copy(
                                out=s_phik2[hp][b][:, 0:16],
                                in_=s_kftm[:, b * 256 + h * 16: b * 256 + h * 16 + 16]),
                                inc=None)
                            tns = s_kftm
                            pstride = tns.ap[0][0]
                            in0 = bass.AP(tns.tensor, tns.offset + b * 256 + h * 16,
                                          [[pstride, 128], [0, 16], [1, 16]])
                            in1 = bass.AP(tns.tensor, tns.offset + b * 256 + h * 16,
                                          [[pstride, 128], [1, 16], [0, 16]])
                            outo = s_phik2[hp][b][:, 128:384].rearrange("p (j i) -> p j i", j=16)
                            p.op("vector", lambda e, in0=in0, in1=in1, outo=outo: e.tensor_tensor(
                                out=outo, in0=in0, in1=in1, op=ALU.mult), inc="vv")
                        phik_done[hp] = p.c("vv")
                    if hstep >= 1:
                        h = hstep - 1
                        hp = h % 2
                        p.wait("tensor", "vv", phik_done[hp])
                        p.wait("tensor", "vv", add_done2[hp])
                        for kk in range(3):
                            for b in (0, 1):
                                p.op("tensor", lambda e, kk=kk, b=b, hp=hp: e.matmul(
                                    upd_aps[hp][kk],
                                    s_phik2[hp][b][:, kk * 128:(kk + 1) * 128],
                                    s_vb2[hp][b], start=(b == 0), stop=(b == 1)),
                                    inc="tt" if b == 1 else None)
                        upd_mm2[hp] = p.c("tt")
                        upd_mm_prev = upd_mm2[hp]
                        p.wait("vector", "tt", upd_mm2[hp])
                        p.wait("vector", "dps", pref_done)
                        for kk in range(3):
                            p.op("vector", lambda e, h=h, kk=kk, hp=hp: e.tensor_tensor(
                                out=sblk(C, h, kk), in0=sblk(C, h, kk),
                                in1=upd_aps[hp][kk], op=ALU.add),
                                inc="vv" if kk == 2 else None)
                        add_done2[hp] = p.c("vv")
                state_add_done = p.c("vv")
                ktmp_free[0] = max(ktmp_free[0], state_add_done)
                ktmp_free[1] = max(ktmp_free[1], state_add_done)
        prev_kv_done = p.c("tt")

    C.state_add_done = state_add_done
    C.k_tt_done = p.c("tt")

    p.wait("sync", "vv", state_add_done)
    ag_in_done = p.dma("sync", SP["ag_st_in"].ap(), C.s_state, sem="dag")
    p.wait("gpsimd", "dag", ag_in_done)
    p.op("gpsimd", lambda e: e.collective_compute(
        "AllGather", ALU.bypass, replica_groups=[[0, 1], [2, 3], [4, 5], [6, 7]],
        ins=[SP["ag_st_in"].ap()], outs=[SP["ag_st_out"].ap()]), inc="cc")
    C.ag_state_done = p.c("cc")

    if C.debug:
        p.wait("sync", "vv", state_add_done)
        p.dma("sync", T["dbg_state"], C.s_state, sem="dqo")
        p.dma("sync", T["dbg_k"], SP["k_sp"][:, 0:2176], sem="dqo")
        p.dma("sync", T["dbg_h"], SP["h_sp"][:, 0:4096], sem="dqo")


def emit_phase_q(C):
    p, A, pb, T, SP = C.p, C.A, C.pb, C.T, C.SP

    def fence_mm(bank, inc="tt"):
        p.op("tensor", lambda e, bank=bank: e.matmul(
            pb[bank][:, 504:506], C.s_ones, C.s_ones[:, 0:2],
            start=True, stop=True), inc=inc)

    A2 = Arena(A.h, A.words)
    s_hg = A2.alloc(4096, "hg")
    s_qg = A2.alloc(4096, "qg")
    s_vg = A2.alloc(5120, "vg")
    s_su = A2.alloc(6144, "su")
    s_cat = A2.alloc(4096, "cat")
    s_yh = A2.alloc(2048, "yh")
    s_xc = A2.alloc(2048, "xc")
    s_wqc = [A2.alloc(1024, "wqc0"), A2.alloc(1024, "wqc1")]
    s_woc = [A2.alloc(1024, "woc0"), A2.alloc(1024, "woc1")]
    s_ksl = [A2.alloc(640, "ksl0"), A2.alloc(640, "ksl1")]
    s_kslf = A2.alloc(640, "kslf")
    s_kff = A2.alloc(256, "kff")
    s_qff = A2.alloc(256, "qff")
    s_qftm = A2.alloc(32, "qftm")
    s_opqtm = A2.alloc(512, "opqtm")
    s_opqf = A2.alloc(512, "opqf")
    s_vbw = A2.alloc(3 * 128, "vbw")
    s_E = A2.alloc(512, "E")
    s_sq = A2.alloc(384, "sq")
    s_A = A2.alloc(384, "A")
    s_zr = A2.alloc(256, "zr")
    s_zt = A2.alloc(256, "zt")
    s_mpd = A2.alloc(256, "mpd")
    s_mfd = A2.alloc(256, "mfd")
    s_ysq = A2.alloc(256, "ysq")
    s_ysq2 = [s_ysq, A2.alloc(256, "ysqb")]
    s_mb = A2.alloc(256, "mb")
    s_t2 = A2.alloc(256, "t2")
    s_u = A2.alloc(256, "u")
    s_var = A2.alloc(256, "var")
    s_r = A2.alloc(256, "r")
    s_m1 = A2.alloc(2, "m1")
    s_red = A2.alloc(2, "red")

    p.memset("vector", s_kff[:, 0:256], 0.0)
    p.memset("vector", s_qff[:, 0:256], 0.0)
    p.wait("vector", "ms")
    onesrow = bass.AP(C.s_ones.tensor, C.s_ones.offset + 32 * C.s_ones.ap[0][0],
                      [[C.s_ones.ap[0][0], 1], [0, 256]])
    p.op("vector", lambda e: e.tensor_copy(out=s_kff[32:33, 0:256], in_=onesrow), inc=None)
    p.op("vector", lambda e: e.tensor_copy(out=s_qff[32:33, 0:256], in_=onesrow), inc="vv")
    for i in range(3):
        p.memset("vector", s_vbw[:, i * 128 + 64:i * 128 + 128], 1.0)
    p.memset("vector", s_m1, 0.0)
    p.wait("vector", "ms")
    p.op("vector", lambda e: e.tensor_copy(out=s_mpd[:, 0:128], in_=C.MP), inc=None)
    p.op("vector", lambda e: e.tensor_copy(out=s_mpd[:, 128:256], in_=C.MD), inc=None)
    p.op("vector", lambda e: e.tensor_copy(out=s_mfd[:, 0:128], in_=C.MF), inc=None)
    p.op("vector", lambda e: e.tensor_copy(out=s_mfd[:, 128:256], in_=C.MD), inc="vv")

    ksl_ld = [0, 0]
    qg_free = 0
    hg_users = 0
    vg_users = 0
    cat_users = 0
    kffqff_users = 0
    vbw_users = 0
    opq_users = 0
    su_users = 0
    opq_evac = 0
    e_consumed = 0
    sq_consumed = 0
    tr_done_prev = 0
    qftm_cons = 0
    b6_read = 0
    e_read = 0
    sq_read = 0
    yh_spill = 0
    wout_done = 0
    yh_built_all = 0

    hg_users_prev = 0
    vg_users_prev = 0
    su_ready = 0
    for g in range(NG):
        if g == 0:
            p.wait("sync", "tt", C.k_tt_done)
            p.wait("sync", "vv", C.state_add_done)
            for s in ("dhs", "dvs0", "dvs1", "dks0", "dks1"):
                p.wait("sync", s, p.c(s))
        p.wait("sync", "tt", hg_users_prev)
        hg_ld = p.dma("sync", s_hg, SP["h_sp"][:, g * 4096:(g + 1) * 4096], sem="dhg")
        p.wait("sync", "vv", vg_users_prev)
        vg_ld = p.dma("sync", s_vg, SP["v_sp"][:, 4 * g * 1024:(4 * g + 5) * 1024], sem="dvg")

        p.wait("tensor", "dhg", hg_ld)
        p.wait("tensor", "tt", qg_free)
        if g == 0:
            p.wait("tensor", "vv", C.state_add_done)
        wq_ld = [0, 0]
        wqc_users = [0, 0]
        for wave in range(2):
            for c in range(8):
                wb = c % 2
                p.wait("sync", "tt", wqc_users[wb])
                wq_ld[wb] = p.dma("sync", s_wqc[wb][:, 0:512],
                                  T["wq"][c * 128:(c + 1) * 128, wave * 512:(wave + 1) * 512],
                                  sem=f"dwq{wb}")
                p.wait("tensor", f"dwq{wb}", wq_ld[wb])
                for mo in range(4):
                    p.op("tensor", lambda e, c=c, mo=mo, wb=wb: e.matmul(
                        pb[mo][:, 0:512], s_wqc[wb][:, mo * 128:(mo + 1) * 128],
                        s_hg[:, c * 512:(c + 1) * 512], start=(c == 0), stop=(c == 7)),
                        inc="tt" if mo == 3 else None)
                wqc_users[wb] = p.c("tt")
            qmm = p.c("tt")
            p.wait("scalar", "tt", qmm)
            for mo in range(4):
                m = wave * 4 + mo
                p.op("scalar", lambda e, mo=mo, m=m: e.activation(
                    out=s_qg[:, m * 512:(m + 1) * 512], in_=pb[mo][:, 0:512],
                    func=AF.Identity, bias=C.constv(0.0), scale=1.0),
                    inc="ss" if mo == 3 else None)
            qevac = p.c("ss")
            p.wait("tensor", "ss", qevac)
        q_ready = p.c("ss")

        if g == 0:
            s_soff = A.h[:, 19456:25600]
            s_pref = [C.s_state[:, 0:6144], A.h[:, 13312:19456]]
            p.wait("sync", "cc", C.ag_state_done)
            p.wait("sync", "dps", p.c("dps"))
            soff_ld = p.dma("sync", s_soff,
                            dap(SP["ag_st_out"].ap(), 0, [[6144, 128], [1, 6144]]), sem="dso")
            p.wait("vector", "dso", soff_ld)
            p.op("vector", lambda e, s_soff=s_soff: e.tensor_scalar(
                out=s_soff, in0=s_soff, scalar1=C.s_flag[:, 0:1], scalar2=None,
                op0=ALU.mult), inc="vv")
            soff_ok = p.c("vv")
            pref_ld = [0, 0]
            pref_add = [0, 0]
            pref_st = [0, 0]
            for c2 in range(NCH):
                b = c2 % 2
                p.wait("sync", "vv", pref_add[b])
                p.wait("sync", "dps", pref_st[b])
                pref_ld[b] = p.dma("sync", s_pref[b],
                                   SP["pref_sp"][:, c2 * 6144:(c2 + 1) * 6144], sem=f"dpl{b}")
                p.wait("vector", f"dpl{b}", pref_ld[b])
                p.wait("vector", "vv", soff_ok)
                p.op("vector", lambda e, b=b, s_soff=s_soff, s_pref=s_pref: e.tensor_tensor(
                    out=s_pref[b], in0=s_pref[b], in1=s_soff, op=ALU.add), inc="vv")
                pref_add[b] = p.c("vv")
                p.wait("sync", "vv", pref_add[b])
                p.dma("sync", SP["pref_sp"][:, c2 * 6144:(c2 + 1) * 6144], s_pref[b], sem="dps")
                pref_st[b] = p.c("dps")
            su_ready = p.c("dps")

        for cc in range(2):
            c = 2 * g + cc
            p.wait("sync", "dps", su_ready)
            p.wait("sync", "tt", su_users)
            su_ld = p.dma("sync", s_su, SP["pref_sp"][:, c * 6144:(c + 1) * 6144], sem="dsu")

            ksl_ok = 0
            for h in range(H):
                if PARANOID:
                    p.sync3("tensor", "vector", "scalar")
                par = h % 2
                m = h // 2
                if par == 0:
                    p.wait("sync", "vv", ksl_ok)
                    kslf_ld = p.dma("sync", s_kslf[:, 0:640],
                                    SP["k_sp"][:, m * (NTT * 128) + 4 * g * 128:
                                               m * (NTT * 128) + 4 * g * 128 + 640], sem="dkf")
                    if g == 0 and cc == 0 and h == 0:
                        p.memset("vector", s_ksl[0][:, 0:640], 0.0)
                        p.memset("vector", s_ksl[1][:, 0:640], 0.0)
                        p.wait("vector", "ms")
                        p.wait("tensor", "ms")
                    p.wait("vector", "dkf", kslf_ld)
                    p.wait("vector", "tt", kffqff_users)
                    p.op("vector", lambda e: e.tensor_copy(
                        out=s_ksl[0][0:64, 0:640], in_=s_kslf[0:64, 0:640]), inc=None)
                    p.op("vector", lambda e: e.tensor_copy(
                        out=s_ksl[1][64:128, 0:640], in_=s_kslf[64:128, 0:640]), inc="vv")
                    ksl_ok = p.c("vv")
                p.wait("vector", "tt", vbw_users)
                p.wait("vector", "dvg", vg_ld)
                for i in range(3):
                    tloc = 2 * cc + i
                    p.op("vector", lambda e, i=i, tloc=tloc, h=h: e.tensor_copy(
                        out=s_vbw[:, i * 128:i * 128 + 64],
                        in_=s_vg[:, tloc * 1024 + h * 64:tloc * 1024 + h * 64 + 64]),
                        inc="vv" if i == 2 else None)
                vbw_ok = p.c("vv")
                p.wait("tensor", "vv", ksl_ok)
                p.wait("tensor", "ss", b6_read)
                p.wait("tensor", "ss", q_ready)
                p.op("tensor", lambda e, par=par, cc=cc: e.matmul(
                    pb[6][:, 0:256], C.s_wkf[par],
                    s_ksl[par][:, (2 * cc + 1) * 128:(2 * cc + 3) * 128],
                    start=True, stop=True), inc=None)
                p.op("tensor", lambda e, par=par, m=m, cc=cc: e.matmul(
                    pb[6][:, 256:512], C.s_wqf[par],
                    s_qg[:, m * 512 + cc * 256:m * 512 + cc * 256 + 256],
                    start=True, stop=True), inc="tt")
                fqmm = p.c("tt")
                p.wait("scalar", "tt", fqmm)
                p.wait("scalar", "tt", kffqff_users)
                p.op("scalar", lambda e: e.activation(
                    out=s_kff[0:16, 0:256], in_=pb[6][0:16, 0:256],
                    func=AF.Identity, bias=C.constv(0.0)[0:16], scale=1.0), inc=None)
                p.op("scalar", lambda e: e.activation(
                    out=s_qff[0:16, 0:256], in_=pb[6][0:16, 256:512],
                    func=AF.Identity, bias=C.constv(0.0)[0:16], scale=1.0), inc="ss")
                b6_read = p.c("ss")
                ffq_ok = b6_read
                for t2_ in range(2):
                    p.op("tensor", lambda e, t2_=t2_, m=m, cc=cc, par=par: e.matmul(
                        pb[3][:, t2_ * 16:t2_ * 16 + 16],
                        s_qg[:, m * 512 + (2 * cc + t2_) * 128:m * 512 + (2 * cc + t2_) * 128 + 128],
                        C.s_wqf[par][:, 0:16], start=True, stop=True),
                        inc="tt" if t2_ == 1 else None)
                qftm_mm = p.c("tt")
                p.wait("scalar", "tt", qftm_mm)
                p.wait("scalar", "vv", qftm_cons)
                p.op("scalar", lambda e: e.activation(
                    out=s_qftm[:, 0:32], in_=pb[3][:, 0:32],
                    func=AF.Identity, bias=C.constv(0.0), scale=1.0), inc="ss")
                qftm_ok = p.c("ss")
                tq0, tq1 = 2 * cc, 2 * cc + 1
                if EMIT_WIN:
                    p.wait("tensor", "ss", e_read)
                    p.op("tensor", lambda e, tq0=tq0, par=par, m=m: e.matmul(
                        pb[0][:, 0:128],
                        s_ksl[par][:, tq0 * 128:(tq0 + 1) * 128],
                        s_qg[:, m * 512 + tq0 * 128:m * 512 + tq0 * 128 + 128],
                        start=True, stop=True), inc=None)
                    p.op("tensor", lambda e, tq0=tq0, tq1=tq1, par=par, m=m: e.matmul(
                        pb[0][:, 128:384],
                        s_ksl[par][:, tq1 * 128:(tq1 + 1) * 128],
                        s_qg[:, m * 512 + tq0 * 128:m * 512 + tq0 * 128 + 256],
                        start=True, stop=True), inc=None)
                    p.op("tensor", lambda e, tq1=tq1, par=par, m=m: e.matmul(
                        pb[0][:, 384:512],
                        s_ksl[par][:, (tq1 + 1) * 128:(tq1 + 2) * 128],
                        s_qg[:, m * 512 + tq1 * 128:m * 512 + tq1 * 128 + 128],
                        start=True, stop=True), inc="tt")
                    smm = p.c("tt")
                    p.wait("scalar", "tt", smm)
                    p.wait("scalar", "vv", e_consumed)
                    p.op("scalar", lambda e: e.activation(
                        out=s_E[:, 0:512], in_=pb[0][:, 0:512],
                        func=AF.Exp, bias=C.constv(0.0), scale=0.125), inc="ss")
                    e_read = p.c("ss")
                p.wait("vector", "ss", qftm_ok)
                p.wait("vector", "tt", tr_done_prev)
                pstride = s_qftm.ap[0][0]
                for t2_ in range(2):
                    in0 = bass.AP(s_qftm.tensor, s_qftm.offset + t2_ * 16,
                                  [[pstride, 128], [0, 16], [1, 16]])
                    in1 = bass.AP(s_qftm.tensor, s_qftm.offset + t2_ * 16,
                                  [[pstride, 128], [1, 16], [0, 16]])
                    outo = s_opqtm[:, t2_ * 256:(t2_ + 1) * 256].rearrange(
                        "p (j i) -> p j i", j=16)
                    p.op("vector", lambda e, in0=in0, in1=in1, outo=outo: e.scalar_tensor_tensor(
                        out=outo, in0=in0, scalar=0.5, in1=in1,
                        op0=ALU.mult, op1=ALU.mult), inc="vv" if t2_ == 1 else None)
                opqtm_ok = p.c("vv")
                qftm_cons = opqtm_ok
                if EMIT_WIN:
                    p.wait("vector", "ss", e_read)
                    mh = s_mfd if (g == 0 and cc == 0) else s_mpd
                    p.op("vector", lambda e, mh=mh: e.tensor_tensor(
                        out=s_E[:, 0:256], in0=s_E[:, 0:256], in1=mh, op=ALU.mult), inc=None)
                    p.op("vector", lambda e: e.tensor_tensor(
                        out=s_E[:, 256:512], in0=s_E[:, 256:512], in1=s_mpd, op=ALU.mult),
                        inc="vv")
                    e_ok = p.c("vv")
                    e_consumed = e_ok
                p.wait("tensor", "vv", opqtm_ok)
                p.wait("tensor", "ss", opq_evac)
                for i in range(4):
                    p.op("tensor", lambda e, i=i: e.matmul(
                        pb[4][:, i * 128:(i + 1) * 128], s_opqtm[:, i * 128:(i + 1) * 128],
                        C.s_id, start=True, stop=True), inc="tt" if i == 3 else None)
                tr_done = p.c("tt")
                tr_done_prev = tr_done
                p.wait("scalar", "tt", tr_done)
                p.wait("scalar", "tt", opq_users)
                p.op("scalar", lambda e: e.activation(
                    out=s_opqf[:, 0:512], in_=pb[4][:, 0:512],
                    func=AF.Identity, bias=C.constv(0.0), scale=1.0), inc="ss")
                opq_evac = p.c("ss")
                if C.debug and g == 0 and cc == 0 and h == 0:
                    p.wait("sync", "vv", opq_evac)
                    ffd = p.dma("sync", T["dbg_ff"][:, 0:256], s_qff[:, 0:256], sem="dqo")
                    ffd = p.dma("sync", T["dbg_ff"][:, 256:512], s_kff[:, 0:256], sem="dqo")
                    ffd = p.dma("sync", T["dbg_ff"][:, 512:1024], s_opqf[:, 0:512], sem="dqo")
                    ffd = p.dma("sync", T["dbg_ff"][:, 1024:1056], s_qftm[:, 0:32], sem="dqo")
                    ffd = p.dma("sync", T["dbg_ff"][:, 1056:1184], s_opqtm[:, 0:128], sem="dqo")
                    p.wait("vector", "dqo", ffd)
                    p.wait("tensor", "dqo", ffd)
                    p.wait("scalar", "dqo", ffd)

                if EMIT_LIN:
                    p.wait("tensor", "ss", sq_read)
                    p.wait("tensor", "ss", ffq_ok)
                    p.op("tensor", lambda e: e.matmul(
                        pb[1][:, 0:256], s_kff[:, 0:128], s_qff[:, 0:256],
                        start=True, stop=True), inc=None)
                    p.op("tensor", lambda e: e.matmul(
                        pb[1][:, 256:384], s_kff[:, 128:256], s_qff[:, 128:256],
                        start=True, stop=True), inc="tt")
                    stmm = p.c("tt")
                    p.wait("scalar", "tt", stmm)
                    p.wait("scalar", "vv", sq_consumed)
                    p.op("scalar", lambda e: e.activation(
                        out=s_sq[:, 0:384], in_=pb[1][:, 0:384],
                        func=AF.Square, bias=C.constv(-1.0), scale=1.0), inc="ss")
                    sq_read = p.c("ss")
                    p.wait("vector", "ss", sq_read)
                    p.op("vector", lambda e: e.scalar_tensor_tensor(
                        out=s_A[:, 0:384], in0=s_sq[:, 0:384],
                        scalar=0.5, in1=pb[1][:, 0:384], op0=ALU.mult, op1=ALU.add),
                        inc=None)
                    pa = s_A.ap[0][0]
                    a02 = bass.AP(s_A.tensor, s_A.offset, [[pa, 128], [256, 2], [1, 128]])
                    mt2 = bass.AP(C.MT.tensor, C.MT.offset, [[C.MT.ap[0][0], 128], [0, 2], [1, 128]])
                    p.op("vector", lambda e, a02=a02, mt2=mt2: e.tensor_tensor(
                        out=a02, in0=a02, in1=mt2, op=ALU.mult), inc="vv")
                    a_ok = p.c("vv")
                    sq_consumed = a_ok
                if EMIT_WIN:
                    p.wait("tensor", "vv", max(e_ok, vbw_ok))
                    for j2 in range(2):
                        p.op("tensor", lambda e, j2=j2: e.matmul(
                            pb[2][:, 0:128], s_vbw[:, j2 * 128:(j2 + 1) * 128],
                            s_E[:, j2 * 128:(j2 + 1) * 128],
                            start=(j2 == 0), stop=(j2 == 1)), inc=None)
                    for j2 in range(2):
                        p.op("tensor", lambda e, j2=j2: e.matmul(
                            pb[2][:, 128:256], s_vbw[:, (1 + j2) * 128:(2 + j2) * 128],
                            s_E[:, (2 + j2) * 128:(3 + j2) * 128],
                            start=(j2 == 0), stop=(j2 == 1)), inc="tt" if j2 == 1 else None)
                    avmm = p.c("tt")
                    p.wait("vector", "tt", avmm)
                    p.op("vector", lambda e: e.reciprocal(
                        s_zr[0:64, 0:256], pb[2][64:128, 0:256]), inc=None)
                    catslot = (8 + m) * 256
                    p.op("vector", lambda e, catslot=catslot, par=par: e.tensor_tensor(
                        out=s_cat[64 * par:64 * par + 64, catslot:catslot + 256],
                        in0=pb[2][0:64, 0:256], in1=s_zr[0:64, 0:256], op=ALU.mult),
                        inc="vv")
                    win_ok = p.c("vv")

                if EMIT_LIN:
                    p.wait("tensor", "vv", a_ok)
                    p.wait("tensor", "ss", opq_evac)
                    p.wait("tensor", "dsu", su_ld)
                    ymms0 = [
                        (s_vbw[:, 128:256],                          s_A[:, 0:128]),
                        (s_su[:, (3 * h) * 128:(3 * h + 1) * 128],   s_qff[:, 0:128]),
                        (s_su[:, (3 * h + 1) * 128:(3 * h + 2) * 128], s_opqf[:, 0:128]),
                        (s_su[:, (3 * h + 2) * 128:(3 * h + 3) * 128], s_opqf[:, 128:256]),
                    ]
                    for i, (stat, mov) in enumerate(ymms0):
                        p.op("tensor", lambda e, stat=stat, mov=mov, st=(i == 0),
                             sp=(i == len(ymms0) - 1): e.matmul(
                            pb[5][:, 0:128], stat, mov, start=st, stop=sp), inc=None)
                    ymms1 = [
                        (s_vbw[:, 128:256],                          s_A[:, 128:256]),
                        (s_vbw[:, 256:384],                          s_A[:, 256:384]),
                        (s_su[:, (3 * h) * 128:(3 * h + 1) * 128],   s_qff[:, 128:256]),
                        (s_su[:, (3 * h + 1) * 128:(3 * h + 2) * 128], s_opqf[:, 256:384]),
                        (s_su[:, (3 * h + 2) * 128:(3 * h + 3) * 128], s_opqf[:, 384:512]),
                    ]
                    for i, (stat, mov) in enumerate(ymms1):
                        p.op("tensor", lambda e, stat=stat, mov=mov, st=(i == 0),
                             sp=(i == len(ymms1) - 1): e.matmul(
                            pb[5][:, 128:256], stat, mov, start=st, stop=sp),
                            inc="tt" if i == len(ymms1) - 1 else None)
                    ymm = p.c("tt")
                    p.wait("vector", "tt", ymm)
                    p.op("vector", lambda e: e.tensor_scalar(
                        out=s_zt[0:64, 0:256], in0=pb[5][64:128, 0:256],
                        scalar1=EPS_LIN, scalar2=None, op0=ALU.add), inc=None)
                    p.op("vector", lambda e: e.reciprocal(
                        s_zr[0:64, 0:256], s_zt[0:64, 0:256]), inc=None)
                    catslot2 = m * 256
                    p.op("vector", lambda e, catslot2=catslot2, par=par: e.tensor_tensor(
                        out=s_cat[64 * par:64 * par + 64, catslot2:catslot2 + 256],
                        in0=pb[5][0:64, 0:256], in1=s_zr[0:64, 0:256], op=ALU.mult),
                        inc="vv")
                    lin_ok = p.c("vv")
                vbw_users = p.c("tt")
                kffqff_users = p.c("tt")
                su_users = p.c("tt")
                opq_users = p.c("tt")
            cat_ok = p.c("vv")
            if C.debug and g == 0 and cc == 0:
                p.wait("sync", "vv", cat_ok)
                catdump = p.dma("sync", T["dbg_cat"], s_cat, sem="dqo")
                p.wait("vector", "dqo", catdump)

            if PARANOID:
                p.sync3("tensor", "vector", "scalar")
            p.wait("sync", "vv", yh_built_all)
            xc_ld = p.dma("sync", s_xc[:, 0:2048].rearrange("p (c t) -> p c t", c=8),
                          dap(T["x_t"], 128 + g * 512 + cc * 256,
                              [[NTT * 128, 128], [128 * NTT * 128, 8], [1, 256]]), sem="dxc")
            p.wait("tensor", "vv", cat_ok)
            p.wait("tensor", "ss", max(sq_read, e_read))
            wo_ld = [0, 0]
            woc_users = [0, 0]
            for wave in range(2):
                for kc in range(16):
                    wb = kc % 2
                    if PARANOID:
                        p.sync3("tensor", "vector")
                    p.wait("sync", "tt", woc_users[wb])
                    wo_ld[wb] = p.dma("sync", s_woc[wb][:, 0:512],
                                      T["wout"][kc * 128:(kc + 1) * 128, wave * 512:(wave + 1) * 512],
                                      sem=f"dwo{wb}")
                    p.wait("tensor", f"dwo{wb}", wo_ld[wb])
                    for mo in range(4):
                        p.op("tensor", lambda e, kc=kc, mo=mo, wb=wb: e.matmul(
                            pb[mo][:, 0:256], s_woc[wb][:, mo * 128:(mo + 1) * 128],
                            s_cat[:, kc * 256:(kc + 1) * 256],
                            start=(kc == 0), stop=(kc == 15)),
                            inc="tt" if mo == 3 else None)
                    woc_users[wb] = p.c("tt")
                womm = p.c("tt")
                p.wait("vector", "tt", womm)
                p.wait("vector", "dxc", xc_ld)
                p.wait("vector", "dy1", yh_spill)
                for mo in range(4):
                    mg = wave * 4 + mo
                    p.op("vector", lambda e, mo=mo, mg=mg: e.scalar_tensor_tensor(
                        out=s_yh[:, mg * 256:(mg + 1) * 256],
                        in0=s_xc[:, mg * 256:(mg + 1) * 256], scalar=2.0,
                        in1=pb[mo][:, 0:256], op0=ALU.mult, op1=ALU.add),
                        inc="vv" if mo == 3 else None)
                yh_built = p.c("vv")
                p.wait("tensor", "vv", yh_built)
            wout_done = p.c("tt")
            cat_users = wout_done
            yh_built_all = yh_built

            s2mm2 = [0, 0]
            p.wait("scalar", "vv", yh_built)
            for mg in range(8):
                mp2 = mg % 2
                p.wait("scalar", "tt", s2mm2[mp2])
                p.op("scalar", lambda e, mg=mg, mp2=mp2: e.activation(
                    out=s_ysq2[mp2][:, 0:256], in_=s_yh[:, mg * 256:(mg + 1) * 256],
                    func=AF.Square, bias=C.constv(0.0), scale=1.0), inc="ss")
                sqd = p.c("ss")
                p.op("tensor", lambda e, mg=mg: e.matmul(
                    pb[4][:, 0:256], C.s_ones, s_yh[:, mg * 256:(mg + 1) * 256],
                    start=(mg == 0), stop=(mg == 7)), inc=None)
                p.wait("tensor", "ss", sqd)
                p.op("tensor", lambda e, mg=mg, mp2=mp2: e.matmul(
                    pb[5][:, 0:256], C.s_ones, s_ysq2[mp2][:, 0:256],
                    start=(mg == 0), stop=(mg == 7)), inc="tt")
                s2mm2[mp2] = p.c("tt")
            statmm = p.c("tt")
            if C.debug and g == 0 and cc == 0:
                p.wait("sync", "vv", p.c("vv"))
                p.wait("sync", "tt", statmm)
                dbgd = p.dma("sync", T["dbg_yh"][:, 0:2048], s_yh[:, 0:2048], sem="dqo")
                p.wait("vector", "dqo", dbgd)
            p.wait("vector", "tt", statmm)
            p.op("vector", lambda e: e.tensor_scalar(
                out=s_mb[:, 0:256], in0=pb[4][:, 0:256], scalar1=1.0 / D, scalar2=None,
                op0=ALU.mult), inc=None)
            p.op("vector", lambda e: e.tensor_scalar(
                out=s_t2[:, 0:256], in0=pb[5][:, 0:256], scalar1=1.0 / D, scalar2=None,
                op0=ALU.mult), inc=None)
            p.op("vector", lambda e: e.tensor_tensor(
                out=s_u[:, 0:256], in0=s_mb[:, 0:256], in1=s_mb[:, 0:256], op=ALU.mult),
                inc=None)
            p.op("vector", lambda e: e.scalar_tensor_tensor(
                out=s_var[:, 0:256], in0=s_t2[:, 0:256], scalar=EPS_LN, in1=s_u[:, 0:256],
                op0=ALU.add, op1=ALU.subtract), inc="vv")
            var_ok = p.c("vv")
            p.wait("scalar", "vv", var_ok)
            p.op("scalar", lambda e: e.activation(
                out=s_r[:, 0:256], in_=s_var[:, 0:256], func=AF.Ln,
                bias=C.constv(0.0), scale=1.0), inc=None)
            p.op("scalar", lambda e: e.activation(
                out=s_r[:, 0:256], in_=s_r[:, 0:256], func=AF.Exp,
                bias=C.constv(0.0), scale=-0.5), inc="ss")
            r_ok = p.c("ss")
            if C.debug and g == 0 and cc == 0:
                p.wait("sync", "ss", r_ok)
                p.dma("sync", T["dbg_yh"][:, 2048:2304], s_mb[:, 0:256])
                p.dma("sync", T["dbg_yh"][:, 2304:2560], s_r[:, 0:256])
                p.dma("sync", T["dbg_yh"][:, 2560:2816], s_var[:, 0:256])
            p.wait("vector", "ss", r_ok)
            p.wait("vector", "dy1", yh_spill)
            for mg in range(8):
                p.op("vector", lambda e, mg=mg: e.tensor_tensor(
                    out=s_ysq[:, 0:256], in0=s_yh[:, mg * 256:(mg + 1) * 256],
                    in1=s_mb[:, 0:256], op=ALU.subtract), inc=None)
                p.op("vector", lambda e, mg=mg: e.tensor_tensor(
                    out=s_yh[:, mg * 256:(mg + 1) * 256], in0=s_ysq[:, 0:256],
                    in1=s_r[:, 0:256], op=ALU.mult), inc=None)
                p.op("vector", lambda e, mg=mg: e.tensor_reduce(
                    out=s_red[:, 0:1], in_=s_yh[:, mg * 256:(mg + 1) * 256],
                    axis=mybir.AxisListType.X, op=ALU.max, apply_absolute_value=True),
                    inc=None)
                p.op("vector", lambda e: e.tensor_tensor(
                    out=s_m1[:, 0:1], in0=s_m1[:, 0:1], in1=s_red[:, 0:1], op=ALU.max),
                    inc="vv" if mg == 7 else None)
            y1_ok = p.c("vv")
            p.wait("sync", "vv", y1_ok)
            yh_spill = p.dma("sync",
                dap(SP["y1_sp"], g * 512 + cc * 256, [[8 * TCORE, 128], [TCORE, 8], [1, 256]]),
                s_yh[:, 0:2048].rearrange("p (c t) -> p c t", c=8), sem="dy1")
        qg_free = p.c("tt")
        hg_users_prev = p.c("tt")
        vg_users_prev = p.c("vv")

    p.wait("sync", "vv", p.c("vv"))
    mxs = p.dma("sync", dap(SP["mx_sp"], 0, [[1, 128], [1, 1]]), s_m1[:, 0:1], sem="dqo")
    p.wait("sync", "dqo", mxs)
    mxl = p.dma("sync", s_mb[0:1, 0:128], SP["mx_sp"][0:1, 0:128], sem="dqo")
    p.wait("vector", "dqo", mxl)
    p.op("vector", lambda e: e.tensor_reduce(
        out=s_red[0:1, 0:1], in_=s_mb[0:1, 0:128], axis=mybir.AxisListType.X,
        op=ALU.max), inc=None)
    if C.debug:
        p.op("vector", lambda e: e.tensor_copy(out=s_mb[0:1, 128:256], in_=s_mb[0:1, 0:128]), inc="vv")
        dchk = p.c("vv")
        p.wait("sync", "vv", dchk)
        p.dma("sync", T["dbg_t"][0:1, 0:128], s_mb[0:1, 128:256], sem="dqo")
    pst = s_red.ap[0][0]
    p.op("vector", lambda e: e.tensor_copy(
        out=s_mb[0:1, 0:8],
        in_=bass.AP(s_red.tensor, s_red.offset, [[pst, 1], [0, 8]])), inc="vv")
    m1rep = p.c("vv")
    p.wait("sync", "vv", m1rep)
    agm_in = p.dma("sync", SP["ag_m1_in"].ap(), s_mb[0:1, 0:8], sem="dqo")
    p.wait("sync", "dqo", agm_in)
    p.wait("gpsimd", "dqo", agm_in)
    p.wait("gpsimd", "cc", C.ag_state_done)
    p.op("gpsimd", lambda e: e.collective_compute(
        "AllGather", ALU.bypass, replica_groups=[list(range(8))],
        ins=[SP["ag_m1_in"].ap()], outs=[SP["ag_m1_out"].ap()]), inc="cc")
    agm1 = p.c("cc")
    C.agm1 = agm1
    C.gmax1_ok = p.c("vv")
    if C.debug:
        p.op("vector", lambda e: e.tensor_copy(out=s_mb[:, 16:17], in_=s_m1[:, 0:1]), inc="vv")
        chain_ok = p.c("vv")
        p.wait("sync", "vv", chain_ok)
        p.wait("sync", "cc", agm1)
        dd = p.dma("sync", T["dbg_chain"][:, 0:8], s_mb[:, 16:24], sem="dqo")
        dd = p.dma("sync", T["dbg_chain"][:, 8:16], s_mb[:, 0:8], sem="dqo")
        dd = p.dma("sync", dap(T["dbg_chain"], 16, [[32, 1], [1, 8]]),
                   dap(SP["ag_m1_in"].ap(), 0, [[8, 1], [1, 8]]), sem="dqo")
        dd = p.dma("sync", dap(T["dbg_chain"], 24, [[32, 8], [1, 8]]),
                   dap(SP["ag_m1_out"].ap(), 0, [[8, 8], [1, 8]]), sem="dqo")
        p.wait("vector", "dqo", dd)

    C.yh_spill_last = yh_spill
    if C.debug:
        p.wait("sync", "vv", C.gmax1_ok)
        p.wait("sync", "dy1", yh_spill)
        p.dma("sync", T["dbg_su"][:, 0:6144], SP["pref_sp"][:, 4 * 6144:5 * 6144], sem="dqo")
        p.dma("sync", T["dbg_su"][:, 6144:12288],
              dap(SP["ag_st_out"].ap(), 0, [[6144, 128], [1, 6144]]), sem="dqo")
        p.dma("sync", T["dbg_y1"], SP["y1_sp"], sem="dqo")


def emit_phase_f(C):
    C.s_clipx = C.s_clipx[:, 0:2]
    C.s_clipdn = C.s_clipdn[:, 0:2]
    C.p.barrier()
    emit_phase_f1(C)
    emit_phase_f2(C)


def emit_phase_f1(C):
    p, pb, T, SP, nc = C.p, C.pb, C.T, C.SP, C.nc
    QM = 127.0
    CLIPLO = 128.0
    CLIPHI = 383.99996948242188

    with ExitStack() as es:
        af = Arena(es.enter_context(nc.sbuf_tensor("af1", [128, 23040], F32)), 23040)
        ah = Arena(es.enter_context(nc.sbuf_tensor("af1h", [128, 36608], BF16)), 36608)
        ai = Arena(es.enter_context(nc.sbuf_tensor("ai1", [128, 2944], I32)), 2944)

        s_g64 = af.alloc(64, "g64")
        s_sx = af.alloc(2, "sx")
        s_invx = af.alloc(2, "invx")
        s_tmp1 = af.alloc(4, "tmp1")
        s_mpad = af.alloc(128, "mpad")
        s_z128 = af.alloc(2, "z128")
        p.memset("vector", s_z128, 0.0)
        ci2 = ai.alloc(2048, "ci32")

        def repmax_scalar(ag_out, sem_done, out_col):
            ld = p.dma("sync", s_g64[0:1, 0:64], dap(ag_out.ap(), 0, [[64, 1], [1, 64]]),
                       sem="dqo")
            p.wait("vector", "dqo", ld)
            p.wait("vector", "ms", p.c("ms"))
            p.op("vector", lambda e: e.tensor_reduce(
                out=s_z128[0:1, 0:1], in_=s_g64[0:1, 0:64], axis=mybir.AxisListType.X,
                op=ALU.max), inc=None)
            p.op("vector", lambda e: e.tensor_reduce(
                out=s_z128[0:1, 1:2], in_=s_g64[0:1, 0:64], axis=mybir.AxisListType.X,
                op=ALU.max), inc="vv")
            rok = p.c("vv")
            p.wait("tensor", "vv", rok)
            p.op("tensor", lambda e: e.matmul(pb[6][:, 0:2], C.s_ones, s_z128[:, 0:2],
                                              start=True, stop=True), inc="tt")
            rok = p.c("tt")
            p.wait("vector", "tt", rok)
            p.op("vector", lambda e, out_col=out_col: e.tensor_copy(
                out=out_col, in_=pb[6][:, 0:2]), inc="vv")
            return p.c("vv")

        def sig_scale(clip_ap, gmax_col, s_col, inv_col):
            p.op("scalar", lambda e: e.activation(out=s_col, in_=clip_ap, func=AF.Exp,
                                                  bias=C.constv(0.0), scale=-1.0), inc="ss")
            p.wait("vector", "ss")
            p.op("vector", lambda e: e.tensor_scalar(out=s_col, in0=s_col,
                                                     scalar1=1.0, scalar2=None, op0=ALU.add), inc=None)
            p.op("vector", lambda e: e.reciprocal(s_col, s_col), inc=None)
            p.op("vector", lambda e: e.tensor_tensor(out=s_col, in0=s_col,
                                                     in1=gmax_col, op=ALU.mult), inc=None)
            p.op("vector", lambda e: e.tensor_scalar(out=s_col, in0=s_col,
                                                     scalar1=1.0 / QM, scalar2=None, op0=ALU.mult), inc=None)
            p.op("vector", lambda e: e.reciprocal(inv_col, s_col), inc="vv")
            return p.c("vv")

        def quant_to_int16(u_ap, cu_ap, out16_ap, n, clip, ci=None):
            civ = ci if ci is not None else ci2[:, 0:n]
            if clip:
                p.op("vector", lambda e: e.tensor_scalar(
                    out=cu_ap, in0=u_ap, scalar1=CLIPLO, scalar2=CLIPHI,
                    op0=ALU.max, op1=ALU.min), inc=None)
                cua = cu_ap
            else:
                cua = u_ap
            p.op("vector", lambda e, cua=cua, civ=civ: e.tensor_copy(out=civ, in_=cua), inc=None)
            dd = cu_ap if not clip else u_ap
            p.op("vector", lambda e, cua=cua, dd=dd, civ=civ: e.tensor_tensor(
                out=dd, in0=civ, in1=cua, op=ALU.is_gt), inc=None)
            p.op("vector", lambda e, dd=dd, civ=civ: e.scalar_tensor_tensor(
                out=out16_ap, in0=civ, scalar=-256.0, in1=dd,
                op0=ALU.add, op1=ALU.subtract), inc="vv")
            return p.c("vv")

        p.wait("sync", "cc", C.agm1)
        g1_ok = repmax_scalar(SP["ag_m1_out"], None, s_tmp1[:, 0:2])
        p.wait("scalar", "vv", g1_ok)
        sx_ok = sig_scale(C.s_clipx, s_tmp1[:, 0:2], s_sx[:, 0:2], s_invx[:, 0:2])
        if C.debug:
            s_dbgsc = af.alloc(16, "dbgsc")
            p.op("vector", lambda e: e.tensor_copy(out=s_dbgsc[:, 0:1], in_=s_tmp1[:, 0:1]), inc=None)
            p.op("vector", lambda e: e.tensor_copy(out=s_dbgsc[:, 1:2], in_=s_sx[:, 0:1]), inc=None)
            p.op("vector", lambda e: e.tensor_copy(out=s_dbgsc[:, 2:3], in_=s_invx[:, 0:1]), inc=None)
            p.op("vector", lambda e: e.tensor_copy(out=s_dbgsc[:, 3:5], in_=C.s_clipx), inc="vv")
            dsc = p.c("vv")
            p.wait("sync", "vv", dsc)
            p.dma("sync", T["dbg_sc"][:, 0:8], s_dbgsc[:, 0:8], sem="dqo")

        s_y1g = af.alloc(4096, "y1g")
        s_u = af.alloc(2048, "uq")
        s_cu = af.alloc(2048, "cuq")
        s_xq = ah.alloc(16384, "xq")
        xqp = s_xq.ap[0][0]
        y1q_ld = 0
        for tg in range(NG):
            p.wait("sync", "vv", p.c("vv"))
            y1q_ld = p.dma("sync", s_y1g,
                           dap(SP["y1_sp"], tg * 512, [[8 * TCORE, 128], [TCORE, 8], [1, 512]]),
                           sem="dyq")
            p.wait("vector", "dyq", y1q_ld)
            p.wait("vector", "vv", sx_ok)
            for hb in range(2):
                sl = slice(hb * 2048, (hb + 1) * 2048)
                p.op("vector", lambda e, sl=sl: e.tensor_scalar(
                    out=s_u[:, 0:2048], in0=s_y1g[:, sl], scalar1=s_invx[:, 0:1],
                    scalar2=256.5, op0=ALU.mult, op1=ALU.add), inc="vv")
                out16 = bass.AP(s_xq.tensor, s_xq.offset + (hb * 4) * 2048 + tg * 512,
                                [[xqp, 128], [2048, 4], [1, 512]])
                quant_to_int16(s_u[:, 0:2048].rearrange("p (c t) -> p c t", c=4),
                               s_cu[:, 0:2048].rearrange("p (c t) -> p c t", c=4),
                               out16, 2048, clip=True,
                               ci=ci2[:, 0:2048].rearrange("p (c t) -> p c t", c=4))
        xq_ok = p.c("vv")
        if C.debug:
            p.op("vector", lambda e: e.tensor_copy(out=s_g64[:, 8:9], in_=s_sx[:, 0:1]), inc=None)
            p.op("vector", lambda e: e.tensor_copy(out=s_g64[:, 9:10], in_=s_invx[:, 0:1]), inc=None)
            p.op("vector", lambda e: e.tensor_copy(out=s_g64[:, 10:11], in_=s_tmp1[:, 0:1]), inc="vv")
            dbgsc = p.c("vv")
            p.wait("sync", "vv", max(xq_ok, dbgsc))
            dd = p.dma("sync", T["dbg_xq"], s_xq[:, 0:16384], sem="dqo")
            dd = p.dma("sync", T["dbg_sc"][:, 0:8], s_g64[:, 8:16], sem="dqo")
            p.wait("vector", "dqo", dd)
            p.wait("scalar", "dqo", dd)

        s_wg = [af.alloc(1024, "wg0"), af.alloc(1024, "wg1")]
        s_wu = [af.alloc(1024, "wu0"), af.alloc(1024, "wu1")]
        s_wqu2 = [af.alloc(1024, "wqua"), af.alloc(1024, "wqub")]
        s_wqd2 = [af.alloc(1024, "wqda"), af.alloc(1024, "wqdb")]
        s_stash = af.alloc(2048, "sgstash")
        s_up2 = [af.alloc(512, "upbuf0"), af.alloc(512, "upbuf1")]
        s_t2 = [af.alloc(512, "tbuf0"), af.alloc(512, "tbuf1")]
        s_tmax = af.alloc(2, "tmax")
        s_sgu = af.alloc(16, "sgu")
        s_tsc = af.alloc(16, "tsc")
        w162 = [ah.alloc(1024, "w16a"), ah.alloc(1024, "w16b")]
        wgT2 = [ah.alloc(1024, "wgT0"), ah.alloc(1024, "wgT1")]
        wuT2 = [ah.alloc(1024, "wuT0"), ah.alloc(1024, "wuT1")]

        p.memset("vector", s_tmax, 0.0)
        p.wait("vector", "ms")

        wg_ld = [0, 0]
        wg_users = [0, 0]
        wg_users_ss = [0, 0]
        bank_ss = [0, 0, 0, 0]
        t_sp_done = [0, 0]
        w16_users = [0, 0]
        wT_users = [0, 0]
        wT_ok = [0, 0]
        wb45_evac = 0
        i1_done = [0, 0, 0, 0]
        sgu_cons = [0, 0, 0, 0]
        stash_users = 0
        up_cons = [0, 0]

        for step in range(NKI + 1):
            if step < NKI:
                ki = step
                wb = ki % 2
                p.wait("sync", "vv", wg_users[wb])
                p.wait("sync", "ss", wg_users_ss[wb])
                p.dma("sync", s_wg[wb], T["w_gate"][ki * 128:(ki + 1) * 128, :], sem=f"dwg{wb}")
                p.dma("sync", s_wu[wb], T["w_up"][ki * 128:(ki + 1) * 128, :], sem=f"dwg{wb}")
                wg_ld[wb] = p.c(f"dwg{wb}")
                p.wait("vector", f"dwg{wb}", wg_ld[wb])
                for wi, wsrc, wTdst in ((0, s_wg[wb], wgT2[wb]), (1, s_wu[wb], wuT2[wb])):
                    q4 = wb * 2 + wi
                    tsc = s_tsc[:, q4 * 4:q4 * 4 + 4]
                    sgs = (ki % 4) * 4 + 2 * wi
                    p.op("vector", lambda e, wsrc=wsrc, tsc=tsc: e.tensor_reduce(
                        out=tsc[:, 0:1], in_=wsrc, axis=mybir.AxisListType.X, op=ALU.max,
                        apply_absolute_value=True), inc=None)
                    p.op("vector", lambda e, wsrc=wsrc, tsc=tsc: e.tensor_reduce(
                        out=tsc[:, 1:2], in_=wsrc, axis=mybir.AxisListType.X, op=ALU.max,
                        apply_absolute_value=True), inc=None)
                    p.wait("vector", "ss", sgu_cons[ki % 4])
                    p.op("vector", lambda e, tsc=tsc, sgs=sgs: e.tensor_scalar(
                        out=s_sgu[:, sgs:sgs + 2], in0=tsc[:, 0:2], scalar1=1.0 / QM,
                        scalar2=None, op0=ALU.mult), inc=None)
                    p.op("vector", lambda e, sgs=sgs: e.tensor_tensor(
                        out=s_sgu[:, sgs:sgs + 2], in0=s_sgu[:, sgs:sgs + 2],
                        in1=s_sx[:, 0:2], op=ALU.mult), inc=None)
                    p.op("vector", lambda e, tsc=tsc: e.reciprocal(tsc[:, 0:2], tsc[:, 0:2]),
                         inc=None)
                    p.op("vector", lambda e, tsc=tsc: e.tensor_scalar(
                        out=tsc[:, 0:2], in0=tsc[:, 0:2], scalar1=QM, scalar2=None,
                        op0=ALU.mult), inc="vv")
                    p.op("vector", lambda e, wsrc=wsrc, wb=wb, tsc=tsc: e.tensor_scalar(
                        out=s_wqu2[wb][:, 0:1024], in0=wsrc, scalar1=tsc[:, 0:1],
                        scalar2=256.5, op0=ALU.mult, op1=ALU.add), inc="vv")
                    p.wait("vector", "tt", w16_users[wb])
                    w_ok = quant_to_int16(s_wqu2[wb][:, 0:1024], s_wqd2[wb][:, 0:1024],
                                          w162[wb][:, 0:1024], 1024, clip=False)
                    p.wait("tensor", "vv", max(w_ok, wb45_evac))
                    for i in range(8):
                        kb = 4 + i // 4
                        p.op("tensor", lambda e, i=i, kb=kb, wb=wb: e.matmul(
                            pb[kb][:, (i % 4) * 128:(i % 4) * 128 + 128],
                            w162[wb][:, i * 128:(i + 1) * 128], C.s_id16,
                            start=True, stop=True), inc="tt" if i == 7 else None)
                    w16_users[wb] = p.c("tt")
                    p.wait("vector", "tt", w16_users[wb])
                    p.wait("vector", "tt", wT_users[wb])
                    for kb in (4, 5):
                        p.op("vector", lambda e, kb=kb, wTdst=wTdst: e.tensor_copy(
                            out=wTdst[:, (kb - 4) * 512:(kb - 3) * 512], in_=pb[kb][:, 0:512]),
                            inc="vv" if kb == 5 else None)
                    wT_ok[wb] = p.c("vv")
                    wb45_evac = wT_ok[wb]
                wg_users[wb] = p.c("vv")
                wg_users_ss[wb] = p.c("ss")

            if step >= 1:
                ki = step - 1
                wb = ki % 2
                p.wait("tensor", "vv", max(wT_ok[wb], xq_ok))
                for wi in (0, 1):
                    wTdst = wgT2[wb] if wi == 0 else wuT2[wb]
                    sgs = (ki % 4) * 4 + 2 * wi
                    for tg in range(NG):
                        bank = (wi * 2) + (tg % 2)
                        p.wait("tensor", "ss", bank_ss[bank])
                        for hc in range(8):
                            p.op("tensor", lambda e, hc=hc, tg=tg, bank=bank, wTdst=wTdst: e.matmul(
                                pb[bank][:, 0:512], wTdst[:, hc * 128:(hc + 1) * 128],
                                s_xq[:, hc * 2048 + tg * 512: hc * 2048 + tg * 512 + 512],
                                start=(hc == 0), stop=(hc == 7)), inc="tt" if hc == 7 else None)
                        gmm = p.c("tt")
                        p.wait("scalar", "tt", gmm)
                        if wi == 0:
                            p.wait("scalar", "vv", stash_users)
                            p.op("scalar", lambda e, tg=tg, bank=bank, sgs=sgs: e.activation(
                                out=s_stash[:, tg * 512:(tg + 1) * 512], in_=pb[bank][:, 0:512],
                                func=AF.Silu, bias=C.constv(0.0), scale=s_sgu[:, sgs:sgs + 1]),
                                inc="ss")
                            bank_ss[bank] = p.c("ss")
                        else:
                            tb = (ki * NG + tg) % 2
                            p.wait("scalar", "vv", up_cons[tb])
                            p.op("scalar", lambda e, tg=tg, bank=bank, sgs=sgs, tb=tb: e.activation(
                                out=s_up2[tb][:, 0:512], in_=pb[bank][:, 0:512],
                                func=AF.Identity, bias=C.constv(0.0),
                                scale=s_sgu[:, sgs:sgs + 1]), inc="ss")
                            up_ok = p.c("ss")
                            bank_ss[bank] = up_ok
                            p.wait("vector", "ss", up_ok)
                            p.wait("vector", f"dtw{tb}", t_sp_done[tb])
                            p.op("vector", lambda e, tg=tg, tb=tb: e.tensor_tensor(
                                out=s_t2[tb][:, 0:512], in0=s_stash[:, tg * 512:(tg + 1) * 512],
                                in1=s_up2[tb][:, 0:512], op=ALU.mult), inc=None)
                            p.op("vector", lambda e, tb=tb: e.tensor_reduce(
                                out=s_tmp1[:, 3:4], in_=s_t2[tb][:, 0:512],
                                axis=mybir.AxisListType.X,
                                op=ALU.max, apply_absolute_value=True), inc=None)
                            p.op("vector", lambda e: e.tensor_tensor(
                                out=s_tmax[:, 0:1], in0=s_tmax[:, 0:1], in1=s_tmp1[:, 3:4],
                                op=ALU.max), inc="vv")
                            t_ok = p.c("vv")
                            up_cons[tb] = t_ok
                            if tg == NG - 1:
                                stash_users = t_ok
                            p.wait("sync", "vv", t_ok)
                            t_sp_done[tb] = p.dma("sync",
                                SP["t_sp"][:, tg * NKI * 512 + ki * 512:
                                           tg * NKI * 512 + ki * 512 + 512],
                                s_t2[tb][:, 0:512], sem=f"dtw{tb}")
                wT_users[wb] = p.c("tt")
                sgu_cons[ki % 4] = p.c("ss")

        p.wait("sync", "vv", p.c("vv"))
        mxs = p.dma("sync", dap(SP["mx_sp"], 0, [[1, 128], [1, 1]]), s_tmax[:, 0:1], sem="dqo")
        p.wait("sync", "dqo", mxs)
        mxl = p.dma("sync", s_mpad[0:1, 0:128], SP["mx_sp"][0:1, 0:128], sem="dqo")
        p.wait("vector", "dqo", mxl)
        p.op("vector", lambda e: e.tensor_reduce(
            out=s_tmp1[0:1, 0:1], in_=s_mpad[0:1, 0:128], axis=mybir.AxisListType.X,
            op=ALU.max), inc=None)
        pst = s_tmp1.ap[0][0]
        p.op("vector", lambda e: e.tensor_copy(
            out=s_g64[0:1, 0:8],
            in_=bass.AP(s_tmp1.tensor, s_tmp1.offset, [[pst, 1], [0, 8]])), inc="vv")
        m2rep = p.c("vv")
        p.wait("sync", "vv", m2rep)
        agm2_in = p.dma("sync", SP["ag_m2_in"].ap(), s_g64[0:1, 0:8], sem="dqo")
        p.wait("sync", "dqo", agm2_in)
        p.wait("gpsimd", "dqo", agm2_in)
        p.op("gpsimd", lambda e: e.collective_compute(
            "AllGather", ALU.bypass, replica_groups=[list(range(8))],
            ins=[SP["ag_m2_in"].ap()], outs=[SP["ag_m2_out"].ap()]), inc="cc")
        agm2 = p.c("cc")
        p.wait("sync", "cc", agm2)
        p.wait("sync", "dtw0", p.c("dtw0"))
        p.wait("sync", "dtw1", p.c("dtw1"))
        if C.debug:
            p.dma("sync", T["dbg_t"], SP["t_sp"][:, 0:2048], sem="dqo")
        g2_ok = repmax_scalar(SP["ag_m2_out"], None, s_tmp1[:, 0:2])
        s_s2 = af.alloc(4, "s2")
        s_inv2 = af.alloc(4, "inv2")
        p.wait("scalar", "vv", g2_ok)
        s2_ok = sig_scale(C.s_clipdn, s_tmp1[:, 0:2], s_s2[:, 0:2], s_inv2[:, 0:2])
        p.op("vector", lambda e: e.tensor_copy(out=C.s_scal[:, 0:2], in_=s_s2[:, 0:2]), inc=None)
        p.op("vector", lambda e: e.tensor_copy(out=C.s_scal[:, 2:4], in_=s_inv2[:, 0:2]), inc="vv")
        if C.debug:
            scok = p.c("vv")
            p.wait("sync", "vv", scok)
            dd = p.dma("sync", T["dbg_sc"][:, 8:10], C.s_scal[:, 0:2], sem="dqo")
            p.wait("vector", "dqo", dd)
        p.barrier()


def emit_phase_f2(C):
    p, pb, T, SP, nc = C.p, C.pb, C.T, C.SP, C.nc
    QM = 127.0
    CLIPLO = 128.0
    CLIPHI = 383.99996948242188

    with ExitStack() as es:
        af = Arena(es.enter_context(nc.sbuf_tensor("af2", [128, 23040], F32)), 23040)
        ah = Arena(es.enter_context(nc.sbuf_tensor("af2h", [128, 36608], BF16)), 36608)
        ai = Arena(es.enter_context(nc.sbuf_tensor("ai2", [128, 2944], I32)), 2944)
        ci2 = ai.alloc(2944, "ci32b")
        s_s2 = C.s_scal[:, 0:1]
        s_inv2 = C.s_scal[:, 2:3]

        if C.debug:
            dd = p.dma("sync", T["dbg_sc"][:, 10:14], C.s_scal[:, 0:4], sem="dqo")
        h2 = ah.alloc(11264, "h2")
        wdT = ah.alloc(22528, "wdT")
        w16d = ah.alloc(2816, "w16d")
        s_wd = af.alloc(2816, "wd")
        s_u = af.alloc(2816, "ud")
        s_cu = af.alloc(2816, "cud")
        s_sdo = af.alloc(32, "sdo")
        s_tslab = af.alloc(2816, "tslab")
        s_y1tg = af.alloc(4096, "y1tg")
        s_y2 = af.alloc(4096, "y2")
        s_mb = af.alloc(512, "mb2")
        s_t2s = af.alloc(512, "t22")
        s_uu = af.alloc(512, "uu2")
        s_var = af.alloc(512, "var2")
        s_r = af.alloc(512, "r2")
        s_ysq = af.alloc(512, "ysq2")
        s_tmp1 = af.alloc(4, "tmp1b")

        def quant_to_int16b(u_ap, cu_ap, out16_ap, n, clip):
            if clip:
                p.op("vector", lambda e: e.tensor_scalar(
                    out=cu_ap, in0=u_ap, scalar1=CLIPLO, scalar2=CLIPHI,
                    op0=ALU.max, op1=ALU.min), inc=None)
                cua = cu_ap
            else:
                cua = u_ap
            p.op("vector", lambda e, cua=cua: e.tensor_copy(out=ci2[:, 0:n], in_=cua), inc=None)
            dd = cu_ap if not clip else u_ap
            p.op("vector", lambda e, cua=cua, dd=dd: e.tensor_tensor(
                out=dd, in0=ci2[:, 0:n], in1=cua, op=ALU.is_gt), inc=None)
            p.op("vector", lambda e, dd=dd: e.scalar_tensor_tensor(
                out=out16_ap, in0=ci2[:, 0:n], scalar=-256.0, in1=dd,
                op0=ALU.add, op1=ALU.subtract), inc="vv")
            return p.c("vv")

        wd_ld = 0
        w16d_users = 0
        for hc in range(8):
            p.wait("sync", "vv", p.c("vv"))
            wd_ld = p.dma("sync", s_wd, T["w_down"][hc * 128:(hc + 1) * 128, :], sem="dwd")
            p.wait("vector", "dwd", wd_ld)
            p.op("vector", lambda e: e.tensor_reduce(
                out=s_tmp1[:, 2:3], in_=s_wd, axis=mybir.AxisListType.X, op=ALU.max,
                apply_absolute_value=True), inc=None)
            p.op("vector", lambda e: e.tensor_reduce(
                out=s_tmp1[:, 3:4], in_=s_wd, axis=mybir.AxisListType.X, op=ALU.max,
                apply_absolute_value=True), inc=None)
            p.op("vector", lambda e, hc=hc: e.tensor_scalar(
                out=s_tmp1[:, 0:2], in0=s_tmp1[:, 2:4], scalar1=1.0 / QM,
                scalar2=None, op0=ALU.mult), inc=None)
            p.op("vector", lambda e, hc=hc: e.scalar_tensor_tensor(
                out=s_sdo[:, 2 * hc:2 * hc + 2],
                in0=s_tmp1[:, 0:2], scalar=1.0, in1=C.s_scal[:, 0:2],
                op0=ALU.mult, op1=ALU.mult), inc=None)
            p.op("vector", lambda e: e.reciprocal(s_tmp1[:, 2:4], s_tmp1[:, 2:4]), inc=None)
            p.op("vector", lambda e: e.tensor_scalar(
                out=s_tmp1[:, 2:4], in0=s_tmp1[:, 2:4], scalar1=QM, scalar2=None,
                op0=ALU.mult), inc="vv")
            p.op("vector", lambda e: e.tensor_scalar(
                out=s_u[:, 0:2816], in0=s_wd, scalar1=s_tmp1[:, 2:3],
                scalar2=256.5, op0=ALU.mult, op1=ALU.add), inc="vv")
            p.wait("vector", "tt", w16d_users)
            w_ok = quant_to_int16b(s_u[:, 0:2816], s_cu[:, 0:2816], w16d[:, 0:2816],
                                   2816, clip=False)
            p.wait("tensor", "vv", w_ok)
            done = 0
            for wv in range(3):
                ntile = 8 if wv < 2 else 6
                for i in range(ntile):
                    ti = wv * 8 + i
                    kb = 4 + i // 4
                    p.op("tensor", lambda e, ti=ti, i=i, kb=kb: e.matmul(
                        pb[kb][:, (i % 4) * 128:(i % 4) * 128 + 128],
                        w16d[:, ti * 128:(ti + 1) * 128], C.s_id16,
                        start=True, stop=True), inc="tt" if i == ntile - 1 else None)
                trw = p.c("tt")
                p.wait("vector", "tt", trw)
                for kb in (4, 5):
                    nn = min(512, (ntile - (kb - 4) * 4) * 128)
                    if nn <= 0:
                        continue
                    p.op("vector", lambda e, kb=kb, nn=nn, wv=wv, hc=hc: e.tensor_copy(
                        out=wdT[:, hc * 2816 + wv * 1024 + (kb - 4) * 512:
                                hc * 2816 + wv * 1024 + (kb - 4) * 512 + nn],
                        in_=pb[kb][:, 0:nn]), inc="vv")
                evw = p.c("vv")
                p.wait("tensor", "vv", evw)
            w16d_users = p.c("tt")
        wdT_ok = p.c("vv")

        s_tslab2 = [s_tslab, s_wd]
        i1_cnt = [p.c("vv"), p.c("vv")]
        t_ld2 = [0, 0]
        h2_cons = 0
        y2_out_done = 0
        qk = 0
        for tg in range(NG):
            p.wait("sync", "vv", p.c("vv"))
            y1tg_ld = p.dma("sync", s_y1tg,
                            dap(SP["y1_sp"], tg * 512, [[8 * TCORE, 128], [TCORE, 8], [1, 512]]),
                            sem="dyr")
            for sb in range(4):
                gp = (tg * 4 + sb) % 2
                p.wait("sync", "vv", i1_cnt[gp])
                t_ld2[gp] = p.dma("sync", s_tslab2[gp],
                             SP["t_sp"][:, tg * NKI * 512 + sb * 2816:
                                        tg * NKI * 512 + (sb + 1) * 2816], sem=f"dtr{gp}")
                p.wait("vector", f"dtr{gp}", t_ld2[gp])
                p.op("vector", lambda e, gp=gp: e.tensor_scalar(
                    out=s_u[:, 0:2816], in0=s_tslab2[gp], scalar1=s_inv2,
                    scalar2=256.5, op0=ALU.mult, op1=ALU.add), inc="vv")
                i1_cnt[gp] = p.c("vv")
                p.wait("vector", "tt", h2_cons)
                qk = quant_to_int16b(s_u[:, 0:2816], s_cu[:, 0:2816],
                                h2[:, sb * 2816:(sb + 1) * 2816], 2816, clip=True)
                if C.debug and tg == 0 and sb == 0:
                    p.wait("sync", "vv", qk)
                    dd = p.dma("sync", T["dbg_q"][:, 0:2816], s_u[:, 0:2816], sem="dqo")
                    dd = p.dma("sync", T["dbg_q"][:, 2816:5632], s_cu[:, 0:2816], sem="dqo")
                    p.wait("scalar", "dqo", dd)
                    p.wait("vector", "dqo", dd)
            h2_ok = p.c("vv")
            if C.debug and tg == 0:
                p.wait("sync", "vv", h2_ok)
                dd = p.dma("sync", T["dbg_h2"], h2[:, 0:11264], sem="dqo")
                p.wait("vector", "dqo", dd)
            p.wait("tensor", "vv", max(h2_ok, wdT_ok))
            bank_vv = [0, 0]
            for hc in range(8):
                bank = hc % 2
                p.wait("tensor", "vv", bank_vv[bank])
                for ki in range(NKI):
                    p.op("tensor", lambda e, hc=hc, ki=ki, bank=bank: e.matmul(
                        pb[bank][:, 0:512], wdT[:, (hc * 22 + ki) * 128:(hc * 22 + ki + 1) * 128],
                        h2[:, ki * 512:(ki + 1) * 512],
                        start=(ki == 0), stop=(ki == NKI - 1)), inc="tt" if ki == NKI - 1 else None)
                dmm = p.c("tt")
                p.wait("vector", "tt", dmm)
                p.wait("vector", "dyr", y1tg_ld)
                p.op("vector", lambda e, hc=hc, bank=bank: e.scalar_tensor_tensor(
                    out=s_y2[:, hc * 512:(hc + 1) * 512], in0=pb[bank][:, 0:512],
                    scalar=s_sdo[:, 2 * hc:2 * hc + 1], in1=s_y1tg[:, hc * 512:(hc + 1) * 512],
                    op0=ALU.mult, op1=ALU.add), inc="vv")
                bank_vv[bank] = p.c("vv")
            y2_ok = p.c("vv")
            if C.debug and tg == 0:
                p.wait("sync", "vv", y2_ok)
                dd = p.dma("sync", T["dbg_y2"], s_y2[:, 0:4096], sem="dqo")
                p.wait("scalar", "dqo", dd)
                p.wait("vector", "dqo", dd)
            h2_cons = p.c("tt")
            s2mm = 0
            p.wait("scalar", "vv", y2_ok)
            for mg in range(8):
                p.wait("scalar", "tt", s2mm)
                p.op("scalar", lambda e, mg=mg: e.activation(
                    out=s_ysq[:, 0:512], in_=s_y2[:, mg * 512:(mg + 1) * 512],
                    func=AF.Square, bias=C.constv(0.0), scale=1.0), inc="ss")
                sqd = p.c("ss")
                p.wait("tensor", "vv", y2_ok)
                p.op("tensor", lambda e, mg=mg: e.matmul(
                    pb[4][:, 0:512], C.s_ones, s_y2[:, mg * 512:(mg + 1) * 512],
                    start=(mg == 0), stop=(mg == 7)), inc=None)
                p.wait("tensor", "ss", sqd)
                p.op("tensor", lambda e, mg=mg: e.matmul(
                    pb[5][:, 0:512], C.s_ones, s_ysq[:, 0:512],
                    start=(mg == 0), stop=(mg == 7)), inc="tt")
                s2mm = p.c("tt")
            statmm = p.c("tt")
            p.wait("vector", "tt", statmm)
            p.op("vector", lambda e: e.tensor_scalar(
                out=s_mb[:, 0:512], in0=pb[4][:, 0:512], scalar1=1.0 / D, scalar2=None,
                op0=ALU.mult), inc=None)
            p.op("vector", lambda e: e.tensor_scalar(
                out=s_t2s[:, 0:512], in0=pb[5][:, 0:512], scalar1=1.0 / D, scalar2=None,
                op0=ALU.mult), inc=None)
            p.op("vector", lambda e: e.tensor_tensor(
                out=s_uu[:, 0:512], in0=s_mb[:, 0:512], in1=s_mb[:, 0:512], op=ALU.mult),
                inc=None)
            p.op("vector", lambda e: e.scalar_tensor_tensor(
                out=s_var[:, 0:512], in0=s_t2s[:, 0:512], scalar=EPS_LN, in1=s_uu[:, 0:512],
                op0=ALU.add, op1=ALU.subtract), inc="vv")
            var_ok = p.c("vv")
            p.wait("scalar", "vv", var_ok)
            p.op("scalar", lambda e: e.activation(
                out=s_r[:, 0:512], in_=s_var[:, 0:512], func=AF.Ln,
                bias=C.constv(0.0), scale=1.0), inc=None)
            p.op("scalar", lambda e: e.activation(
                out=s_r[:, 0:512], in_=s_r[:, 0:512], func=AF.Exp,
                bias=C.constv(0.0), scale=-0.5), inc="ss")
            r_ok = p.c("ss")
            p.wait("vector", "ss", r_ok)
            p.wait("vector", "dout", y2_out_done)
            for mg in range(8):
                p.op("vector", lambda e, mg=mg: e.tensor_tensor(
                    out=s_ysq[:, 0:512], in0=s_y2[:, mg * 512:(mg + 1) * 512],
                    in1=s_mb[:, 0:512], op=ALU.subtract), inc=None)
                p.op("vector", lambda e, mg=mg: e.tensor_tensor(
                    out=s_y2[:, mg * 512:(mg + 1) * 512], in0=s_ysq[:, 0:512],
                    in1=s_r[:, 0:512], op=ALU.mult), inc="vv" if mg == 7 else None)
            y2n_ok = p.c("vv")
            p.wait("sync", "vv", y2n_ok)
            y2_out_done = p.dma("sync",
                dap(T["out_t"], tg * 512, [[TCORE, 128], [128 * TCORE, 8], [1, 512]]),
                s_y2[:, 0:4096].rearrange("p (c t) -> p c t", c=8), sem="dout")
    C.out_done = p.c("dout")


def emit_finish(C):
    p, T = C.p, C.T
    p.barrier()
    if C.stage == "F":
        p.wait("sync", "dout", C.out_done)
    else:
        fin = p.dma("sync", T["out_t"][0:128, 0:128], C.s_state[:, 0:128])
        p.wait("sync", "dq", fin)


def _stage_inputs(L0_attn_x, L0_attn_norm_w, L0_attn_Wq, L0_attn_Wk, L0_attn_Wv,
                  L0_attn_Wqf, L0_attn_Wkf, L0_attn_Wout, L0_ffn_clip_x,
                  L0_ffn_clip_down, L0_ffn_w_gate, L0_ffn_w_up, L0_ffn_w_down):
    f32 = np.float32
    x = np.asarray(L0_attn_x, f32)

    def fpad(w, hi):
        wp = np.zeros((128, 128), f32)
        if hi:
            wp[64:128, 0:16] = np.asarray(w, f32)
        else:
            wp[0:64, 0:16] = np.asarray(w, f32)
        return wp

    r = np.arange(128)
    m_diag = ((r[:, None] <= r[None, :]) & (r[:, None] >= r[None, :] - 64)).astype(f32)
    m_prev = (r[:, None] >= r[None, :] + 64).astype(f32)
    m_tril = (r[:, None] <= r[None, :]).astype(f32)
    ident = np.eye(128, dtype=f32)

    import ml_dtypes
    ident16 = np.eye(128, dtype=ml_dtypes.bfloat16)

    common = dict(
        wq=np.ascontiguousarray(np.asarray(L0_attn_Wq, f32)),
        wk=np.ascontiguousarray(np.asarray(L0_attn_Wk, f32)),
        wv=np.ascontiguousarray(np.asarray(L0_attn_Wv, f32)),
        wqf_lo=fpad(L0_attn_Wqf, 0), wqf_hi=fpad(L0_attn_Wqf, 1),
        wkf_lo=fpad(L0_attn_Wkf, 0), wkf_hi=fpad(L0_attn_Wkf, 1),
        wout=np.ascontiguousarray(np.asarray(L0_attn_Wout, f32)),
        normw=np.ascontiguousarray(np.asarray(L0_attn_norm_w, f32).reshape(8, 128).T),
        m_diag=m_diag, m_prev=m_prev, m_tril=m_tril, ident=ident, ident16=ident16,
        clip_x=np.full((128, 2), np.asarray(L0_ffn_clip_x, f32)[0], f32),
        clip_dn=np.full((128, 2), np.asarray(L0_ffn_clip_down, f32)[0], f32),
        w_gate=np.ascontiguousarray(np.asarray(L0_ffn_w_gate, f32)),
        w_up=np.ascontiguousarray(np.asarray(L0_ffn_w_up, f32)),
        w_down=np.ascontiguousarray(np.asarray(L0_ffn_w_down, f32)),
    )

    in_maps = []
    for core in range(8):
        b, half = core // 2, core % 2
        t0 = half * TCORE
        xs = np.zeros((NTT * 128, D), f32)
        if half == 1:
            xs[0:128] = x[b, t0 - 128:t0]
        xs[128:] = x[b, t0:t0 + TCORE]
        m = dict(common)
        m["x_t"] = np.ascontiguousarray(xs.T)
        m["m_first"] = m_prev if half == 1 else np.zeros_like(m_prev)
        m["lin_flag"] = np.full((128, 1), f32(half))
        in_maps.append(m)
    return in_maps


def kernel(**inputs):
    in_maps = _stage_inputs(**inputs)
    nc = build_program(stage="F", debug=False)
    res = run_bass_kernel_spmd(nc, in_maps, core_ids=list(range(8)))
    out = np.empty((4, 4096, D), np.float32)
    for core in range(8):
        b, half = core // 2, core % 2
        out[b, half * TCORE:(half + 1) * TCORE] = np.asarray(res.results[core]["out_t"]).T
    return out


# version 18
# speedup vs baseline: 1.5911x; 1.0031x over previous
import numpy as np
from contextlib import ExitStack

import concourse.bass as bass
import concourse.mybir as mybir
from concourse.bass_utils import run_bass_kernel_spmd

F32 = mybir.dt.float32
BF16 = mybir.dt.bfloat16
I32 = mybir.dt.int32
AF = mybir.ActivationFunctionType
ALU = mybir.AluOpType

D = 1024
H = 16
HD = 64
FT = 16
TCORE = 2048
NTT = 17
NG = 4
NCH = 8
DI = 2816
NKI = 22
EPS_NORM = 1e-6
EPS_LIN = 1e-6
EPS_LN = 1e-5

ENGS = ("sync", "vector", "scalar", "tensor", "gpsimd")
PARANOID = False
EMIT_WIN = True
EMIT_LIN = True
LIN_INTRA = True
LIN_INTER = True
INTER_KK = (0, 1, 2)
INTER_SU_ONES = False


class Prog:
    def __init__(self):
        self.ops = []
        self.cnt = {}

    def c(self, sem):
        return self.cnt.get(sem, 0)

    def bump(self, sem, n):
        self.cnt[sem] = self.c(sem) + n
        return self.cnt[sem]

    def wait(self, eng, sem, val=None):
        if val is None:
            val = self.c(sem)
        if val > 0:
            self.ops.append((eng, "wait", (sem, val)))

    def op(self, eng, fn, inc=None, inc_n=1):
        if eng == "vector":
            assert inc in (None, "vv")
            self.wait(eng, "vv")
            inc, inc_n = "vv", 1
        elif eng == "scalar":
            assert inc in (None, "ss")
            self.wait(eng, "ss")
            inc, inc_n = "ss", 1
        tgt = self.bump(inc, inc_n) if inc is not None else None
        self.ops.append((eng, "op", (fn, inc, inc_n)))
        return tgt

    def dma(self, eng, out, in_, sem="dq"):
        tgt = self.bump(sem, 16)
        self.ops.append((eng, "dma", (out, in_, sem)))
        if sem == "dqo":
            self.ops.append((eng, "wait", (sem, tgt)))
        return tgt

    def memset(self, eng, ap, const, sem="ms"):
        tgt = self.bump(sem, 16)
        self.ops.append((eng, "memset", (ap, const, sem)))
        return tgt

    def sync3(self, *engs):
        sems = [s for s in self.cnt if s not in ("bar", "cc")]
        for e in engs:
            for s in sems:
                self.wait(e, s)
        for s in ("tt", "vv", "ss"):
            self.wait("sync", s)

    def barrier(self):
        tgt = self.bump("bar", len(ENGS))
        for e in ENGS:
            self.ops.append((e, "barinc", None))
        for e in ENGS:
            self.ops.append((e, "wait", ("bar", tgt)))

    def run(self, engname, eng, sems):
        lastw = {}
        for e, kind, data in self.ops:
            if e != engname:
                continue
            if kind == "wait":
                sem, val = data
                if lastw.get(sem, -1) >= val:
                    continue
                eng.wait_ge(sems[sem], val)
                lastw[sem] = val
            elif kind == "op":
                fn, inc, inc_n = data
                inst = fn(eng)
                if inc is not None:
                    inst.then_inc(sems[inc], inc_n)
            elif kind == "dma":
                out, in_, sem = data
                eng.dma_start(out=out, in_=in_).then_inc(sems[sem], 16)
            elif kind == "memset":
                ap, const, sem = data
                eng.memset(ap, const).then_inc(sems[sem], 16)
            elif kind == "barinc":
                eng.sem_inc(sems["bar"], 1)


class Arena:
    def __init__(self, handle, words):
        self.h = handle
        self.words = words
        self.off = 0
        self.peak = 0

    def alloc(self, n, name=""):
        n = (n + 1) & ~1
        assert self.off + n <= self.words, (
            f"SBUF arena overflow at {name}: {self.off}+{n}>{self.words}")
        ap = self.h[:, self.off:self.off + n]
        self.off += n
        self.peak = max(self.peak, self.off)
        return ap


def dap(t, offset, dims):
    return bass.AP(t.tensor, offset, dims)


class Ctx:
    pass


def build_program(stage="F", debug=False):
    nc = bass.Bass("TRN2", target_bir_lowering=False, debug=False)
    p = Prog()
    T = {}

    def din(name, shape, dt=F32):
        T[name] = nc.dram_tensor(name, shape, dt, kind="ExternalInput").ap()

    din("x_t", [D, NTT * 128])
    din("wq", [D, D])
    din("wk", [D, D])
    din("wv", [D, D])
    din("wqf_lo", [128, 128])
    din("wqf_hi", [128, 128])
    din("wkf_lo", [128, 128])
    din("wkf_hi", [128, 128])
    din("wout", [2048, D])
    din("normw", [128, 8])
    din("m_diag", [128, 128])
    din("m_prev", [128, 128])
    din("m_first", [128, 128])
    din("m_tril", [128, 128])
    din("ident", [128, 128])
    din("ident16", [128, 128], BF16)
    din("lin_flag", [128, 1])
    din("clip_x", [128, 2])
    din("clip_dn", [128, 2])
    din("w_gate", [DI, D])
    din("w_up", [DI, D])
    din("w_down", [D, DI])

    T["out_t"] = nc.dram_tensor("out_t", [D, TCORE], F32, kind="ExternalOutput").ap()

    SP = {}
    for name, shape in (("h_sp", [128, 4 * 4096]), ("k_sp", [128, 8 * NTT * 128]),
                        ("v_sp", [128, NTT * 1024]), ("y1_sp", [128, 8 * TCORE]),
                        ("pref_sp", [128, NCH * 6144]), ("t_sp", [128, NKI * TCORE])):
        SP[name] = nc.dram_tensor(name, shape, F32).ap()
    SP["mx_sp"] = nc.dram_tensor("mx_sp", [1, 128], F32).ap()
    for name, shape in (("ag_st_in", [128, 6144]), ("ag_st_out", [256, 6144]),
                        ("ag_m1_in", [1, 8]), ("ag_m1_out", [8, 8]),
                        ("ag_m2_in", [1, 8]), ("ag_m2_out", [8, 8])):
        SP[name] = nc.dram_tensor(name, shape, F32)

    if debug and stage == "F":
        T["dbg_q"] = nc.dram_tensor("dbg_q", [128, 5632], F32, kind="ExternalOutput").ap()
        T["dbg_sc"] = nc.dram_tensor("dbg_sc", [128, 16], F32, kind="ExternalOutput").ap()
        T["dbg_chain"] = nc.dram_tensor("dbg_chain", [128, 32], F32, kind="ExternalOutput").ap()
        T["dbg_xq"] = nc.dram_tensor("dbg_xq", [128, 16384], BF16, kind="ExternalOutput").ap()
        T["dbg_t"] = nc.dram_tensor("dbg_t", [128, 2048], F32, kind="ExternalOutput").ap()
        T["dbg_h2"] = nc.dram_tensor("dbg_h2", [128, 11264], BF16, kind="ExternalOutput").ap()
        T["dbg_y2"] = nc.dram_tensor("dbg_y2", [128, 4096], F32, kind="ExternalOutput").ap()
    if debug:
        T["dbg_state"] = nc.dram_tensor("dbg_state", [128, 6144], F32, kind="ExternalOutput").ap()
        T["dbg_k"] = nc.dram_tensor("dbg_k", [128, 2176], F32, kind="ExternalOutput").ap()
        T["dbg_h"] = nc.dram_tensor("dbg_h", [128, 4096], F32, kind="ExternalOutput").ap()
        if stage in ("Q", "F"):
            T["dbg_y1"] = nc.dram_tensor("dbg_y1", [128, 8 * TCORE], F32, kind="ExternalOutput").ap()
            T["dbg_cat"] = nc.dram_tensor("dbg_cat", [128, 4096], F32, kind="ExternalOutput").ap()
            T["dbg_yh"] = nc.dram_tensor("dbg_yh", [128, 3072], F32, kind="ExternalOutput").ap()
            T["dbg_ff"] = nc.dram_tensor("dbg_ff", [128, 1280], F32, kind="ExternalOutput").ap()
            T["dbg_su"] = nc.dram_tensor("dbg_su", [128, 12288], F32, kind="ExternalOutput").ap()

    with ExitStack() as es:
        C = Ctx()
        C.p, C.nc, C.T, C.SP, C.stage, C.debug = p, nc, T, SP, stage, debug
        pers = es.enter_context(nc.sbuf_tensor("pers", [128, 1426], F32))
        pers16 = es.enter_context(nc.sbuf_tensor("pers16", [128, 128], BF16))
        st_t = es.enter_context(nc.sbuf_tensor("st_t", [128, 6144], F32))
        C.pb = [es.enter_context(nc.psum_tensor(f"pb{i}", [128, 512], F32)) for i in range(7)]
        C.pbh = es.enter_context(nc.psum_tensor("pbh", [128, 1024], BF16))
        sems = {}

        PA = Arena(pers, 1426)
        C.MD = PA.alloc(128, "m_diag")
        C.MP = PA.alloc(128, "m_prev")
        C.MF = PA.alloc(128, "m_first")
        C.MT = PA.alloc(128, "m_tril")
        C.s_id = PA.alloc(128, "ident")
        C.s_wqf = [PA.alloc(128, "wqf_lo"), PA.alloc(128, "wqf_hi")]
        C.s_wkf = [PA.alloc(128, "wkf_lo"), PA.alloc(128, "wkf_hi")]
        C.s_normw = PA.alloc(8, "normw")
        C.s_flag = PA.alloc(2, "flag")
        C.s_ones = PA.alloc(128, "allones")
        C.s_scal = PA.alloc(16, "scalars")
        C.s_clipx = PA.alloc(2, "clipx")
        C.s_clipdn = PA.alloc(2, "clipdn")
        C.s_id16 = pers16[:]
        C.s_state = st_t[:]
        C.s_const = PA.alloc(16, "consts")
        C.consts = {}

        def constv(val):
            val = float(val)
            if val not in C.consts:
                i = len(C.consts)
                assert i < 16
                C.consts[val] = C.s_const[:, i:i + 1]
            return C.consts[val]
        C.constv = constv

        emit_init(C)
        with nc.sbuf_tensor("akq", [128, 40000], F32) as akq:
            C.A = Arena(akq, 40000)
            emit_phase_k(C)
            if stage != "K":
                C.A = Arena(akq, 40000)
                C.A.off = 0
                emit_phase_q(C)
        if stage == "F":
            emit_phase_f(C)
        emit_finish(C)

        for s in set(list(p.cnt.keys()) + ["bar"]):
            sems[s] = es.enter_context(nc.semaphore(s))
        block = es.enter_context(nc.Block())
        for engname in ENGS:
            getattr(block, engname)(lambda eng, _n=engname: p.run(_n, eng, sems))

    return nc


def emit_init(C):
    p, T = C.p, C.T
    for dst, key in ((C.MD, "m_diag"), (C.MP, "m_prev"), (C.MF, "m_first"),
                     (C.MT, "m_tril"), (C.s_id, "ident"), (C.s_id16, "ident16"),
                     (C.s_wqf[0], "wqf_lo"), (C.s_wqf[1], "wqf_hi"),
                     (C.s_wkf[0], "wkf_lo"), (C.s_wkf[1], "wkf_hi"),
                     (C.s_normw, "normw"), (C.s_flag[:, 0:1], "lin_flag"),
                     (C.s_clipx[:, 0:2], "clip_x"), (C.s_clipdn[:, 0:2], "clip_dn")):
        p.dma("sync", dst, T[key])
    p.memset("vector", C.s_ones, 1.0)
    p.memset("vector", C.s_state, 0.0)
    for v in (0.0, EPS_NORM, EPS_LN, -1.0, 256.5, 1.0):
        p.memset("vector", C.constv(v), v)
    for e in ("vector", "tensor", "scalar"):
        p.wait(e, "dq")
        p.wait(e, "ms")


def sblk(C, h, kk):
    return C.s_state[:, (3 * h + kk) * 128:(3 * h + kk) * 128 + 128]


def x_ap(C, tfirst, ntok):
    return dap(C.T["x_t"], tfirst * 128,
               [[NTT * 128, 128], [128 * NTT * 128, 8], [1, ntok]])


def emit_phase_k(C):
    p, A, pb, T, SP = C.p, C.A, C.pb, C.T, C.SP

    s_wk = A.alloc(8 * 1024, "wk")
    s_wv = A.alloc(8 * 1024, "wv")
    for c in range(8):
        p.dma("sync", s_wk[:, c * 1024:(c + 1) * 1024], T["wk"][c * 128:(c + 1) * 128, :], sem="dwk")
        p.dma("sync", s_wv[:, c * 1024:(c + 1) * 1024], T["wv"][c * 128:(c + 1) * 128, :], sem="dwk")
    wk_ld = p.c("dwk")

    s_x = [A.alloc(8 * 512, "xg0"), A.alloc(8 * 512, "xg1")]
    s_h = A.alloc(8 * 512, "hg")
    s_hsq = A.alloc(8 * 512, "hsq")
    s_rms = A.alloc(512, "rms")
    s_ktmp = [A.alloc(512, "ktmp0"), A.alloc(512, "ktmp1")]
    s_vtile = [A.alloc(1024, "vt0"), A.alloc(1024, "vt1")]
    s_kftm = A.alloc(2 * 256, "kftm")
    s_phik2 = [[A.alloc(384, "phik00"), A.alloc(384, "phik01")],
               [A.alloc(384, "phik10"), A.alloc(384, "phik11")]]
    s_vb2 = [[A.alloc(128, "vb00"), A.alloc(128, "vb01")],
             [A.alloc(128, "vb10"), A.alloc(128, "vb11")]]

    for hp in (0, 1):
        for b in (0, 1):
            p.memset("vector", s_phik2[hp][b], 0.0)
            p.memset("vector", s_phik2[hp][b][:, 32:33], 1.0)
            p.memset("vector", s_vb2[hp][b][:, 64:128], 1.0)
    p.wait("vector", "ms")

    kgroups = [(0, 1)] + [(1 + 4 * g, 4) for g in range(4)]

    def xg_sb(buf, ntok):
        return s_x[buf][:, 0:8 * ntok].rearrange("p (c t) -> p c t", c=8)

    x_done = [0, 0]
    p.dma("sync", xg_sb(0, 128), x_ap(C, 0, 128), sem="dx0")
    x_done[0] = p.c("dx0")

    h_spill_done = 0
    prev_kv_done = 0
    h_done_prev = 0
    state_add_done = 0
    ktmp_free = [0, 0]
    ktmp_spill = [0, 0]
    vtile_spill = [0, 0]
    b23_free = 0
    kf_evac_all = 0

    for gi, (tfirst, ntile) in enumerate(kgroups):
        nt = ntile * 128
        buf = gi % 2
        if gi + 1 < len(kgroups):
            nf, nn = kgroups[gi + 1]
            p.wait("sync", "vv", h_done_prev)
            p.dma("sync", xg_sb(1 - buf, nn * 128), x_ap(C, nf, nn * 128), sem=f"dx{1 - buf}")
            x_done[1 - buf] = p.c(f"dx{1 - buf}")
        xa = s_x[buf]
        p.wait("scalar", f"dx{buf}", x_done[buf])
        p.wait("scalar", "tt", prev_kv_done)
        p.op("scalar", lambda e, xa=xa, nt=nt: e.activation(
            out=s_hsq[:, 0:8 * nt], in_=xa[:, 0:8 * nt], func=AF.Square,
            bias=C.constv(0.0), scale=1.0), inc="ss")
        sq_done = p.c("ss")
        p.wait("tensor", "ss", sq_done)
        for c in range(8):
            p.op("tensor", lambda e, c=c, nt=nt: e.matmul(
                pb[6][:, 0:nt], C.s_ones, s_hsq[:, c * nt:c * nt + nt],
                start=(c == 0), stop=(c == 7)), inc="tt" if c == 7 else None)
        mm_rms = p.c("tt")
        p.wait("scalar", "tt", mm_rms)
        p.op("scalar", lambda e, nt=nt: e.activation(
            out=s_rms[:, 0:nt], in_=pb[6][:, 0:nt], func=AF.Ln,
            bias=C.constv(EPS_NORM), scale=1.0 / D), inc="ss")
        p.op("scalar", lambda e, nt=nt: e.activation(
            out=s_rms[:, 0:nt], in_=s_rms[:, 0:nt], func=AF.Exp,
            bias=C.constv(0.0), scale=-0.5), inc="ss")
        rms_done = p.c("ss")
        p.wait("vector", "ss", rms_done)
        p.wait("vector", "dhs", h_spill_done)
        p.wait("vector", "tt", prev_kv_done)
        for c in range(8):
            p.op("vector", lambda e, c=c, xa=xa, nt=nt: e.scalar_tensor_tensor(
                out=s_h[:, c * nt:c * nt + nt], in0=xa[:, c * nt:c * nt + nt],
                scalar=C.s_normw[:, c:c + 1], in1=s_rms[:, 0:nt],
                op0=ALU.mult, op1=ALU.mult), inc="vv" if c == 7 else None)
        h_done = p.c("vv")
        h_done_prev = h_done
        if tfirst > 0:
            p.wait("sync", "vv", h_done)
            h_spill_done = p.dma("sync",
                SP["h_sp"][:, (gi - 1) * 4096:(gi - 1) * 4096 + 8 * nt],
                s_h[:, 0:8 * nt], sem="dhs")

        if PARANOID:
            p.sync3("tensor", "vector", "scalar")
        p.wait("tensor", "dwk", wk_ld)
        p.wait("tensor", "vv", h_done)
        kf_mm_last = 0
        for m in range(8):
            if PARANOID:
                p.sync3("tensor", "vector")
            bank = m % 2
            p.wait("tensor", "vv", ktmp_free[bank])
            for c in range(8):
                p.op("tensor", lambda e, m=m, c=c, bank=bank, nt=nt: e.matmul(
                    pb[bank][:, 0:nt],
                    s_wk[:, c * 1024 + m * 128: c * 1024 + m * 128 + 128],
                    s_h[:, c * nt:c * nt + nt], start=(c == 0), stop=(c == 7)),
                    inc="tt" if c == 7 else None)
            kmm = p.c("tt")
            p.wait("vector", "tt", kmm)
            p.wait("vector", f"dks{bank}", ktmp_spill[bank])
            p.op("vector", lambda e, bank=bank, nt=nt: e.tensor_copy(
                out=s_ktmp[bank][:, 0:nt], in_=pb[bank][:, 0:nt]), inc="vv")
            ktmp_free[bank] = p.c("vv")
            p.wait("sync", "vv", ktmp_free[bank])
            ktmp_spill[bank] = p.dma("sync",
                SP["k_sp"][:, m * NTT * 128 + tfirst * 128:
                           m * NTT * 128 + (tfirst + ntile) * 128],
                s_ktmp[bank][:, 0:nt], sem=f"dks{bank}")
            if tfirst > 0:
                p.wait("tensor", "vv", max(ktmp_free[bank], kf_evac_all))
                for tl in range(ntile):
                    for par in (0, 1):
                        h = 2 * m + par
                        kb = 4 + tl % 2
                        p.op("tensor", lambda e, tl=tl, par=par, h=h, kb=kb, bank=bank: e.matmul(
                            pb[kb][:, h * 16 + (tl // 2) * 256: h * 16 + (tl // 2) * 256 + 16],
                            s_ktmp[bank][:, tl * 128:(tl + 1) * 128],
                            C.s_wkf[par][:, 0:16], start=True, stop=True),
                            inc="tt" if (tl == ntile - 1 and par == 1) else None)
                kf_mm_last = p.c("tt")

        for tl in range(ntile):
            if PARANOID:
                p.sync3("tensor", "vector", "scalar")
            vbuf = tl % 2
            p.wait("tensor", "vv", b23_free)
            for hf in range(2):
                bank = 2 + hf
                for c in range(8):
                    p.op("tensor", lambda e, tl=tl, hf=hf, c=c, bank=bank, nt=nt: e.matmul(
                        pb[bank][:, 0:512],
                        s_h[:, c * nt + tl * 128: c * nt + tl * 128 + 128],
                        s_wv[:, c * 1024 + hf * 512: c * 1024 + (hf + 1) * 512],
                        start=(c == 0), stop=(c == 7)), inc="tt" if c == 7 else None)
            vmm = p.c("tt")
            p.wait("vector", "tt", vmm)
            p.wait("vector", f"dvs{vbuf}", vtile_spill[vbuf])
            for hf in range(2):
                p.op("vector", lambda e, hf=hf, vbuf=vbuf: e.tensor_copy(
                    out=s_vtile[vbuf][:, hf * 512:(hf + 1) * 512],
                    in_=pb[2 + hf][:, 0:512]), inc="vv" if hf == 1 else None)
            v_evac = p.c("vv")
            b23_free = v_evac
            p.wait("sync", "vv", v_evac)
            vtile_spill[vbuf] = p.dma("sync",
                SP["v_sp"][:, (tfirst + tl) * 1024:(tfirst + tl + 1) * 1024],
                s_vtile[vbuf], sem=f"dvs{vbuf}")

            if tfirst > 0 and tl % 2 == 1:
                cglob = (tfirst - 1) // 2 + tl // 2
                p.wait("vector", "tt", kf_mm_last)
                for b in (0, 1):
                    kb = 4 + (tl - 1 + b) % 2
                    colb = ((tl - 1 + b) // 2) * 256
                    p.op("vector", lambda e, b=b, kb=kb, colb=colb: e.tensor_copy(
                        out=s_kftm[:, b * 256:(b + 1) * 256],
                        in_=pb[kb][:, colb:colb + 256]), inc="vv" if b == 1 else None)
                kf_evac_all = p.c("vv")
                p.wait("sync", "vv", state_add_done if cglob > 0 else 0)
                pref_done = p.dma("sync",
                    SP["pref_sp"][:, cglob * 6144:(cglob + 1) * 6144], C.s_state, sem="dps")
                upd_aps = [
                    [pb[0][:, 0:128], pb[1][:, 0:128], pb[6][:, 0:128]],
                    [pb[4][:, 0:128], pb[5][:, 0:128], pb[6][:, 128:256]],
                ]
                phik_done = [0, 0]
                upd_mm2 = [0, 0]
                add_done2 = [0, 0]
                for hstep in range(H + 1):
                    if hstep < H:
                        h = hstep
                        hp = h % 2
                        p.wait("vector", "tt", upd_mm2[hp])
                        for b in (0, 1):
                            vtl = (tl - 1 + b) % 2
                            p.op("vector", lambda e, b=b, h=h, vtl=vtl, hp=hp: e.tensor_copy(
                                out=s_vb2[hp][b][:, 0:64],
                                in_=s_vtile[vtl][:, h * 64:(h + 1) * 64]), inc=None)
                            p.op("vector", lambda e, b=b, h=h, hp=hp: e.tensor_copy(
                                out=s_phik2[hp][b][:, 0:16],
                                in_=s_kftm[:, b * 256 + h * 16: b * 256 + h * 16 + 16]),
                                inc=None)
                            tns = s_kftm
                            pstride = tns.ap[0][0]
                            in0 = bass.AP(tns.tensor, tns.offset + b * 256 + h * 16,
                                          [[pstride, 128], [0, 16], [1, 16]])
                            in1 = bass.AP(tns.tensor, tns.offset + b * 256 + h * 16,
                                          [[pstride, 128], [1, 16], [0, 16]])
                            outo = s_phik2[hp][b][:, 128:384].rearrange("p (j i) -> p j i", j=16)
                            p.op("vector", lambda e, in0=in0, in1=in1, outo=outo: e.tensor_tensor(
                                out=outo, in0=in0, in1=in1, op=ALU.mult), inc="vv")
                        phik_done[hp] = p.c("vv")
                    if hstep >= 1:
                        h = hstep - 1
                        hp = h % 2
                        p.wait("tensor", "vv", phik_done[hp])
                        p.wait("tensor", "vv", add_done2[hp])
                        for kk in range(3):
                            for b in (0, 1):
                                p.op("tensor", lambda e, kk=kk, b=b, hp=hp: e.matmul(
                                    upd_aps[hp][kk],
                                    s_phik2[hp][b][:, kk * 128:(kk + 1) * 128],
                                    s_vb2[hp][b], start=(b == 0), stop=(b == 1)),
                                    inc="tt" if b == 1 else None)
                        upd_mm2[hp] = p.c("tt")
                        upd_mm_prev = upd_mm2[hp]
                        p.wait("vector", "tt", upd_mm2[hp])
                        p.wait("vector", "dps", pref_done)
                        for kk in range(3):
                            p.op("vector", lambda e, h=h, kk=kk, hp=hp: e.tensor_tensor(
                                out=sblk(C, h, kk), in0=sblk(C, h, kk),
                                in1=upd_aps[hp][kk], op=ALU.add),
                                inc="vv" if kk == 2 else None)
                        add_done2[hp] = p.c("vv")
                state_add_done = p.c("vv")
                ktmp_free[0] = max(ktmp_free[0], state_add_done)
                ktmp_free[1] = max(ktmp_free[1], state_add_done)
        prev_kv_done = p.c("tt")

    C.state_add_done = state_add_done
    C.k_tt_done = p.c("tt")

    p.wait("sync", "vv", state_add_done)
    ag_in_done = p.dma("sync", SP["ag_st_in"].ap(), C.s_state, sem="dag")
    p.wait("gpsimd", "dag", ag_in_done)
    p.op("gpsimd", lambda e: e.collective_compute(
        "AllGather", ALU.bypass, replica_groups=[[0, 1], [2, 3], [4, 5], [6, 7]],
        ins=[SP["ag_st_in"].ap()], outs=[SP["ag_st_out"].ap()]), inc="cc")
    C.ag_state_done = p.c("cc")

    if C.debug:
        p.wait("sync", "vv", state_add_done)
        p.dma("sync", T["dbg_state"], C.s_state, sem="dqo")
        p.dma("sync", T["dbg_k"], SP["k_sp"][:, 0:2176], sem="dqo")
        p.dma("sync", T["dbg_h"], SP["h_sp"][:, 0:4096], sem="dqo")


def emit_phase_q(C):
    p, A, pb, T, SP = C.p, C.A, C.pb, C.T, C.SP

    def fence_mm(bank, inc="tt"):
        p.op("tensor", lambda e, bank=bank: e.matmul(
            pb[bank][:, 504:506], C.s_ones, C.s_ones[:, 0:2],
            start=True, stop=True), inc=inc)

    A2 = Arena(A.h, A.words)
    s_hg = A2.alloc(4096, "hg")
    s_qg = A2.alloc(4096, "qg")
    s_vg = A2.alloc(5120, "vg")
    s_su = A2.alloc(6144, "su")
    s_cat = A2.alloc(4096, "cat")
    s_yh = A2.alloc(2048, "yh")
    s_xc = A2.alloc(2048, "xc")
    s_wqc = [A2.alloc(1024, "wqc0"), A2.alloc(1024, "wqc1")]
    s_woc = [A2.alloc(1024, "woc0"), A2.alloc(1024, "woc1")]
    s_ksl = [A2.alloc(640, "ksl0"), A2.alloc(640, "ksl1")]
    s_kslf = A2.alloc(640, "kslf")
    s_kff = A2.alloc(256, "kff")
    s_qff = A2.alloc(256, "qff")
    s_qftm = A2.alloc(32, "qftm")
    s_opqtm = A2.alloc(512, "opqtm")
    s_opqf = A2.alloc(512, "opqf")
    s_vbw = A2.alloc(3 * 128, "vbw")
    s_E = A2.alloc(512, "E")
    s_sq = A2.alloc(384, "sq")
    s_A = A2.alloc(384, "A")
    s_zr = A2.alloc(256, "zr")
    s_zt = A2.alloc(256, "zt")
    s_mpd = A2.alloc(256, "mpd")
    s_mfd = A2.alloc(256, "mfd")
    s_ysq = A2.alloc(256, "ysq")
    s_ysq2 = [s_ysq, A2.alloc(256, "ysqb")]
    s_mb = A2.alloc(256, "mb")
    s_t2 = A2.alloc(256, "t2")
    s_u = A2.alloc(256, "u")
    s_var = A2.alloc(256, "var")
    s_r = A2.alloc(256, "r")
    s_m1 = A2.alloc(2, "m1")
    s_red = A2.alloc(2, "red")

    p.memset("vector", s_kff[:, 0:256], 0.0)
    p.memset("vector", s_qff[:, 0:256], 0.0)
    p.wait("vector", "ms")
    onesrow = bass.AP(C.s_ones.tensor, C.s_ones.offset + 32 * C.s_ones.ap[0][0],
                      [[C.s_ones.ap[0][0], 1], [0, 256]])
    p.op("vector", lambda e: e.tensor_copy(out=s_kff[32:33, 0:256], in_=onesrow), inc=None)
    p.op("vector", lambda e: e.tensor_copy(out=s_qff[32:33, 0:256], in_=onesrow), inc="vv")
    for i in range(3):
        p.memset("vector", s_vbw[:, i * 128 + 64:i * 128 + 128], 1.0)
    p.memset("vector", s_m1, 0.0)
    p.wait("vector", "ms")
    p.op("vector", lambda e: e.tensor_copy(out=s_mpd[:, 0:128], in_=C.MP), inc=None)
    p.op("vector", lambda e: e.tensor_copy(out=s_mpd[:, 128:256], in_=C.MD), inc=None)
    p.op("vector", lambda e: e.tensor_copy(out=s_mfd[:, 0:128], in_=C.MF), inc=None)
    p.op("vector", lambda e: e.tensor_copy(out=s_mfd[:, 128:256], in_=C.MD), inc="vv")

    ksl_ld = [0, 0]
    qg_free = 0
    hg_users = 0
    vg_users = 0
    cat_users = 0
    kffqff_users = 0
    vbw_users = 0
    opq_users = 0
    su_users = 0
    opq_evac = 0
    e_consumed = 0
    sq_consumed = 0
    tr_done_prev = 0
    qftm_cons = 0
    b6_read = 0
    e_read = 0
    sq_read = 0
    yh_spill = 0
    wout_done = 0
    yh_built_all = 0

    hg_users_prev = 0
    vg_users_prev = 0
    su_ready = 0
    for g in range(NG):
        if g == 0:
            p.wait("sync", "tt", C.k_tt_done)
            p.wait("sync", "vv", C.state_add_done)
            for s in ("dhs", "dvs0", "dvs1", "dks0", "dks1"):
                p.wait("sync", s, p.c(s))
        p.wait("sync", "tt", hg_users_prev)
        hg_ld = p.dma("sync", s_hg, SP["h_sp"][:, g * 4096:(g + 1) * 4096], sem="dhg")
        p.wait("sync", "vv", vg_users_prev)
        vg_ld = p.dma("sync", s_vg, SP["v_sp"][:, 4 * g * 1024:(4 * g + 5) * 1024], sem="dvg")

        p.wait("tensor", "dhg", hg_ld)
        p.wait("tensor", "tt", qg_free)
        if g == 0:
            p.wait("tensor", "vv", C.state_add_done)
        wq_ld = [0, 0]
        wqc_users = [0, 0]
        for wave in range(2):
            for c in range(8):
                wb = c % 2
                p.wait("sync", "tt", wqc_users[wb])
                wq_ld[wb] = p.dma("sync", s_wqc[wb][:, 0:512],
                                  T["wq"][c * 128:(c + 1) * 128, wave * 512:(wave + 1) * 512],
                                  sem=f"dwq{wb}")
                p.wait("tensor", f"dwq{wb}", wq_ld[wb])
                for mo in range(4):
                    p.op("tensor", lambda e, c=c, mo=mo, wb=wb: e.matmul(
                        pb[mo][:, 0:512], s_wqc[wb][:, mo * 128:(mo + 1) * 128],
                        s_hg[:, c * 512:(c + 1) * 512], start=(c == 0), stop=(c == 7)),
                        inc="tt" if mo == 3 else None)
                wqc_users[wb] = p.c("tt")
            qmm = p.c("tt")
            p.wait("scalar", "tt", qmm)
            for mo in range(4):
                m = wave * 4 + mo
                p.op("scalar", lambda e, mo=mo, m=m: e.activation(
                    out=s_qg[:, m * 512:(m + 1) * 512], in_=pb[mo][:, 0:512],
                    func=AF.Identity, bias=C.constv(0.0), scale=1.0),
                    inc="ss" if mo == 3 else None)
            qevac = p.c("ss")
            p.wait("tensor", "ss", qevac)
        q_ready = p.c("ss")

        if g == 0:
            s_soff = A.h[:, 19456:25600]
            s_pref = [C.s_state[:, 0:6144], A.h[:, 13312:19456]]
            p.wait("sync", "cc", C.ag_state_done)
            p.wait("sync", "dps", p.c("dps"))
            soff_ld = p.dma("sync", s_soff,
                            dap(SP["ag_st_out"].ap(), 0, [[6144, 128], [1, 6144]]), sem="dso")
            p.wait("vector", "dso", soff_ld)
            p.op("vector", lambda e, s_soff=s_soff: e.tensor_scalar(
                out=s_soff, in0=s_soff, scalar1=C.s_flag[:, 0:1], scalar2=None,
                op0=ALU.mult), inc="vv")
            soff_ok = p.c("vv")
            pref_ld = [0, 0]
            pref_add = [0, 0]
            pref_st = [0, 0]
            for c2 in range(NCH):
                b = c2 % 2
                p.wait("sync", "vv", pref_add[b])
                p.wait("sync", "dps", pref_st[b])
                pref_ld[b] = p.dma("sync", s_pref[b],
                                   SP["pref_sp"][:, c2 * 6144:(c2 + 1) * 6144], sem=f"dpl{b}")
                p.wait("vector", f"dpl{b}", pref_ld[b])
                p.wait("vector", "vv", soff_ok)
                p.op("vector", lambda e, b=b, s_soff=s_soff, s_pref=s_pref: e.tensor_tensor(
                    out=s_pref[b], in0=s_pref[b], in1=s_soff, op=ALU.add), inc="vv")
                pref_add[b] = p.c("vv")
                p.wait("sync", "vv", pref_add[b])
                p.dma("sync", SP["pref_sp"][:, c2 * 6144:(c2 + 1) * 6144], s_pref[b], sem="dps")
                pref_st[b] = p.c("dps")
            su_ready = p.c("dps")

        for cc in range(2):
            c = 2 * g + cc
            p.wait("sync", "dps", su_ready)
            p.wait("sync", "tt", su_users)
            su_ld = p.dma("sync", s_su, SP["pref_sp"][:, c * 6144:(c + 1) * 6144], sem="dsu")

            ksl_ok = 0
            for h in range(H):
                if PARANOID:
                    p.sync3("tensor", "vector", "scalar")
                par = h % 2
                m = h // 2
                if par == 0:
                    p.wait("sync", "vv", ksl_ok)
                    kslf_ld = p.dma("sync", s_kslf[:, 0:640],
                                    SP["k_sp"][:, m * (NTT * 128) + 4 * g * 128:
                                               m * (NTT * 128) + 4 * g * 128 + 640], sem="dkf")
                    if g == 0 and cc == 0 and h == 0:
                        p.memset("vector", s_ksl[0][:, 0:640], 0.0)
                        p.memset("vector", s_ksl[1][:, 0:640], 0.0)
                        p.wait("vector", "ms")
                        p.wait("tensor", "ms")
                    p.wait("vector", "dkf", kslf_ld)
                    p.wait("vector", "tt", kffqff_users)
                    p.op("vector", lambda e: e.tensor_copy(
                        out=s_ksl[0][0:64, 0:640], in_=s_kslf[0:64, 0:640]), inc=None)
                    p.op("vector", lambda e: e.tensor_copy(
                        out=s_ksl[1][64:128, 0:640], in_=s_kslf[64:128, 0:640]), inc="vv")
                    ksl_ok = p.c("vv")
                p.wait("vector", "tt", vbw_users)
                p.wait("vector", "dvg", vg_ld)
                for i in range(3):
                    tloc = 2 * cc + i
                    p.op("vector", lambda e, i=i, tloc=tloc, h=h: e.tensor_copy(
                        out=s_vbw[:, i * 128:i * 128 + 64],
                        in_=s_vg[:, tloc * 1024 + h * 64:tloc * 1024 + h * 64 + 64]),
                        inc="vv" if i == 2 else None)
                vbw_ok = p.c("vv")
                p.wait("tensor", "vv", ksl_ok)
                p.wait("tensor", "ss", b6_read)
                p.wait("tensor", "ss", q_ready)
                p.op("tensor", lambda e, par=par, cc=cc: e.matmul(
                    pb[6][:, 0:256], C.s_wkf[par],
                    s_ksl[par][:, (2 * cc + 1) * 128:(2 * cc + 3) * 128],
                    start=True, stop=True), inc=None)
                p.op("tensor", lambda e, par=par, m=m, cc=cc: e.matmul(
                    pb[6][:, 256:512], C.s_wqf[par],
                    s_qg[:, m * 512 + cc * 256:m * 512 + cc * 256 + 256],
                    start=True, stop=True), inc="tt")
                fqmm = p.c("tt")
                p.wait("scalar", "tt", fqmm)
                p.wait("scalar", "tt", kffqff_users)
                p.op("scalar", lambda e: e.activation(
                    out=s_kff[0:16, 0:256], in_=pb[6][0:16, 0:256],
                    func=AF.Identity, bias=C.constv(0.0)[0:16], scale=1.0), inc=None)
                p.op("scalar", lambda e: e.activation(
                    out=s_qff[0:16, 0:256], in_=pb[6][0:16, 256:512],
                    func=AF.Identity, bias=C.constv(0.0)[0:16], scale=1.0), inc="ss")
                b6_read = p.c("ss")
                ffq_ok = b6_read
                for t2_ in range(2):
                    p.op("tensor", lambda e, t2_=t2_, m=m, cc=cc, par=par: e.matmul(
                        pb[3][:, t2_ * 16:t2_ * 16 + 16],
                        s_qg[:, m * 512 + (2 * cc + t2_) * 128:m * 512 + (2 * cc + t2_) * 128 + 128],
                        C.s_wqf[par][:, 0:16], start=True, stop=True),
                        inc="tt" if t2_ == 1 else None)
                qftm_mm = p.c("tt")
                p.wait("scalar", "tt", qftm_mm)
                p.wait("scalar", "vv", qftm_cons)
                p.op("scalar", lambda e: e.activation(
                    out=s_qftm[:, 0:32], in_=pb[3][:, 0:32],
                    func=AF.Identity, bias=C.constv(0.0), scale=1.0), inc="ss")
                qftm_ok = p.c("ss")
                tq0, tq1 = 2 * cc, 2 * cc + 1
                if EMIT_WIN:
                    p.wait("tensor", "ss", e_read)
                    p.op("tensor", lambda e, tq0=tq0, par=par, m=m: e.matmul(
                        pb[0][:, 0:128],
                        s_ksl[par][:, tq0 * 128:(tq0 + 1) * 128],
                        s_qg[:, m * 512 + tq0 * 128:m * 512 + tq0 * 128 + 128],
                        start=True, stop=True), inc=None)
                    p.op("tensor", lambda e, tq0=tq0, tq1=tq1, par=par, m=m: e.matmul(
                        pb[0][:, 128:384],
                        s_ksl[par][:, tq1 * 128:(tq1 + 1) * 128],
                        s_qg[:, m * 512 + tq0 * 128:m * 512 + tq0 * 128 + 256],
                        start=True, stop=True), inc=None)
                    p.op("tensor", lambda e, tq1=tq1, par=par, m=m: e.matmul(
                        pb[0][:, 384:512],
                        s_ksl[par][:, (tq1 + 1) * 128:(tq1 + 2) * 128],
                        s_qg[:, m * 512 + tq1 * 128:m * 512 + tq1 * 128 + 128],
                        start=True, stop=True), inc="tt")
                    smm = p.c("tt")
                    p.wait("scalar", "tt", smm)
                    p.wait("scalar", "vv", e_consumed)
                    p.op("scalar", lambda e: e.activation(
                        out=s_E[:, 0:512], in_=pb[0][:, 0:512],
                        func=AF.Exp, bias=C.constv(0.0), scale=0.125), inc="ss")
                    e_read = p.c("ss")
                p.wait("vector", "ss", qftm_ok)
                p.wait("vector", "tt", tr_done_prev)
                pstride = s_qftm.ap[0][0]
                for t2_ in range(2):
                    in0 = bass.AP(s_qftm.tensor, s_qftm.offset + t2_ * 16,
                                  [[pstride, 128], [0, 16], [1, 16]])
                    in1 = bass.AP(s_qftm.tensor, s_qftm.offset + t2_ * 16,
                                  [[pstride, 128], [1, 16], [0, 16]])
                    outo = s_opqtm[:, t2_ * 256:(t2_ + 1) * 256].rearrange(
                        "p (j i) -> p j i", j=16)
                    p.op("vector", lambda e, in0=in0, in1=in1, outo=outo: e.scalar_tensor_tensor(
                        out=outo, in0=in0, scalar=0.5, in1=in1,
                        op0=ALU.mult, op1=ALU.mult), inc="vv" if t2_ == 1 else None)
                opqtm_ok = p.c("vv")
                qftm_cons = opqtm_ok
                if EMIT_WIN:
                    p.wait("vector", "ss", e_read)
                    mh = s_mfd if (g == 0 and cc == 0) else s_mpd
                    p.op("vector", lambda e, mh=mh: e.tensor_tensor(
                        out=s_E[:, 0:256], in0=s_E[:, 0:256], in1=mh, op=ALU.mult), inc=None)
                    p.op("vector", lambda e: e.tensor_tensor(
                        out=s_E[:, 256:512], in0=s_E[:, 256:512], in1=s_mpd, op=ALU.mult),
                        inc="vv")
                    e_ok = p.c("vv")
                    e_consumed = e_ok
                p.wait("tensor", "vv", opqtm_ok)
                p.wait("tensor", "ss", opq_evac)
                for i in range(4):
                    p.op("tensor", lambda e, i=i: e.matmul(
                        pb[4][:, i * 128:(i + 1) * 128], s_opqtm[:, i * 128:(i + 1) * 128],
                        C.s_id, start=True, stop=True), inc="tt" if i == 3 else None)
                tr_done = p.c("tt")
                tr_done_prev = tr_done
                p.wait("scalar", "tt", tr_done)
                p.wait("scalar", "tt", opq_users)
                p.op("scalar", lambda e: e.activation(
                    out=s_opqf[:, 0:512], in_=pb[4][:, 0:512],
                    func=AF.Identity, bias=C.constv(0.0), scale=1.0), inc="ss")
                opq_evac = p.c("ss")
                if C.debug and g == 0 and cc == 0 and h == 0:
                    p.wait("sync", "vv", opq_evac)
                    ffd = p.dma("sync", T["dbg_ff"][:, 0:256], s_qff[:, 0:256], sem="dqo")
                    ffd = p.dma("sync", T["dbg_ff"][:, 256:512], s_kff[:, 0:256], sem="dqo")
                    ffd = p.dma("sync", T["dbg_ff"][:, 512:1024], s_opqf[:, 0:512], sem="dqo")
                    ffd = p.dma("sync", T["dbg_ff"][:, 1024:1056], s_qftm[:, 0:32], sem="dqo")
                    ffd = p.dma("sync", T["dbg_ff"][:, 1056:1184], s_opqtm[:, 0:128], sem="dqo")
                    p.wait("vector", "dqo", ffd)
                    p.wait("tensor", "dqo", ffd)
                    p.wait("scalar", "dqo", ffd)

                if EMIT_LIN:
                    p.wait("tensor", "ss", sq_read)
                    p.wait("tensor", "ss", ffq_ok)
                    p.op("tensor", lambda e: e.matmul(
                        pb[1][:, 0:256], s_kff[:, 0:128], s_qff[:, 0:256],
                        start=True, stop=True), inc=None)
                    p.op("tensor", lambda e: e.matmul(
                        pb[1][:, 256:384], s_kff[:, 128:256], s_qff[:, 128:256],
                        start=True, stop=True), inc="tt")
                    stmm = p.c("tt")
                    p.wait("scalar", "tt", stmm)
                    p.wait("scalar", "vv", sq_consumed)
                    p.op("scalar", lambda e: e.activation(
                        out=s_sq[:, 0:384], in_=pb[1][:, 0:384],
                        func=AF.Square, bias=C.constv(-1.0), scale=1.0), inc="ss")
                    sq_read = p.c("ss")
                    p.wait("vector", "ss", sq_read)
                    p.op("vector", lambda e: e.scalar_tensor_tensor(
                        out=s_A[:, 0:384], in0=s_sq[:, 0:384],
                        scalar=0.5, in1=pb[1][:, 0:384], op0=ALU.mult, op1=ALU.add),
                        inc=None)
                    pa = s_A.ap[0][0]
                    a02 = bass.AP(s_A.tensor, s_A.offset, [[pa, 128], [256, 2], [1, 128]])
                    mt2 = bass.AP(C.MT.tensor, C.MT.offset, [[C.MT.ap[0][0], 128], [0, 2], [1, 128]])
                    p.op("vector", lambda e, a02=a02, mt2=mt2: e.tensor_tensor(
                        out=a02, in0=a02, in1=mt2, op=ALU.mult), inc="vv")
                    a_ok = p.c("vv")
                    sq_consumed = a_ok
                if EMIT_WIN:
                    p.wait("tensor", "vv", max(e_ok, vbw_ok))
                    for j2 in range(2):
                        p.op("tensor", lambda e, j2=j2: e.matmul(
                            pb[2][:, 0:128], s_vbw[:, j2 * 128:(j2 + 1) * 128],
                            s_E[:, j2 * 128:(j2 + 1) * 128],
                            start=(j2 == 0), stop=(j2 == 1)), inc=None)
                    for j2 in range(2):
                        p.op("tensor", lambda e, j2=j2: e.matmul(
                            pb[2][:, 128:256], s_vbw[:, (1 + j2) * 128:(2 + j2) * 128],
                            s_E[:, (2 + j2) * 128:(3 + j2) * 128],
                            start=(j2 == 0), stop=(j2 == 1)), inc="tt" if j2 == 1 else None)
                    avmm = p.c("tt")
                    p.wait("vector", "tt", avmm)
                    p.op("vector", lambda e: e.reciprocal(
                        s_zr[0:64, 0:256], pb[2][64:128, 0:256]), inc=None)
                    catslot = (8 + m) * 256
                    p.op("vector", lambda e, catslot=catslot, par=par: e.tensor_tensor(
                        out=s_cat[64 * par:64 * par + 64, catslot:catslot + 256],
                        in0=pb[2][0:64, 0:256], in1=s_zr[0:64, 0:256], op=ALU.mult),
                        inc="vv")
                    win_ok = p.c("vv")

                if EMIT_LIN:
                    p.wait("tensor", "vv", a_ok)
                    p.wait("tensor", "ss", opq_evac)
                    p.wait("tensor", "dsu", su_ld)
                    ymms0 = [
                        (s_vbw[:, 128:256],                          s_A[:, 0:128]),
                        (s_su[:, (3 * h) * 128:(3 * h + 1) * 128],   s_qff[:, 0:128]),
                        (s_su[:, (3 * h + 1) * 128:(3 * h + 2) * 128], s_opqf[:, 0:128]),
                        (s_su[:, (3 * h + 2) * 128:(3 * h + 3) * 128], s_opqf[:, 128:256]),
                    ]
                    for i, (stat, mov) in enumerate(ymms0):
                        p.op("tensor", lambda e, stat=stat, mov=mov, st=(i == 0),
                             sp=(i == len(ymms0) - 1): e.matmul(
                            pb[5][:, 0:128], stat, mov, start=st, stop=sp), inc=None)
                    ymms1 = [
                        (s_vbw[:, 128:256],                          s_A[:, 128:256]),
                        (s_vbw[:, 256:384],                          s_A[:, 256:384]),
                        (s_su[:, (3 * h) * 128:(3 * h + 1) * 128],   s_qff[:, 128:256]),
                        (s_su[:, (3 * h + 1) * 128:(3 * h + 2) * 128], s_opqf[:, 256:384]),
                        (s_su[:, (3 * h + 2) * 128:(3 * h + 3) * 128], s_opqf[:, 384:512]),
                    ]
                    for i, (stat, mov) in enumerate(ymms1):
                        p.op("tensor", lambda e, stat=stat, mov=mov, st=(i == 0),
                             sp=(i == len(ymms1) - 1): e.matmul(
                            pb[5][:, 128:256], stat, mov, start=st, stop=sp),
                            inc="tt" if i == len(ymms1) - 1 else None)
                    ymm = p.c("tt")
                    p.wait("vector", "tt", ymm)
                    p.op("vector", lambda e: e.tensor_scalar(
                        out=s_zt[0:64, 0:256], in0=pb[5][64:128, 0:256],
                        scalar1=EPS_LIN, scalar2=None, op0=ALU.add), inc=None)
                    p.op("vector", lambda e: e.reciprocal(
                        s_zr[0:64, 0:256], s_zt[0:64, 0:256]), inc=None)
                    catslot2 = m * 256
                    p.op("vector", lambda e, catslot2=catslot2, par=par: e.tensor_tensor(
                        out=s_cat[64 * par:64 * par + 64, catslot2:catslot2 + 256],
                        in0=pb[5][0:64, 0:256], in1=s_zr[0:64, 0:256], op=ALU.mult),
                        inc="vv")
                    lin_ok = p.c("vv")
                vbw_users = p.c("tt")
                kffqff_users = p.c("tt")
                su_users = p.c("tt")
                opq_users = p.c("tt")
            cat_ok = p.c("vv")
            if C.debug and g == 0 and cc == 0:
                p.wait("sync", "vv", cat_ok)
                catdump = p.dma("sync", T["dbg_cat"], s_cat, sem="dqo")
                p.wait("vector", "dqo", catdump)

            if PARANOID:
                p.sync3("tensor", "vector", "scalar")
            p.wait("sync", "vv", yh_built_all)
            xc_ld = p.dma("sync", s_xc[:, 0:2048].rearrange("p (c t) -> p c t", c=8),
                          dap(T["x_t"], 128 + g * 512 + cc * 256,
                              [[NTT * 128, 128], [128 * NTT * 128, 8], [1, 256]]), sem="dxc")
            p.wait("tensor", "vv", cat_ok)
            p.wait("tensor", "ss", max(sq_read, e_read))
            wo_ld = [0, 0]
            woc_users = [0, 0]
            for wave in range(2):
                for kc in range(16):
                    wb = kc % 2
                    if PARANOID:
                        p.sync3("tensor", "vector")
                    p.wait("sync", "tt", woc_users[wb])
                    wo_ld[wb] = p.dma("sync", s_woc[wb][:, 0:512],
                                      T["wout"][kc * 128:(kc + 1) * 128, wave * 512:(wave + 1) * 512],
                                      sem=f"dwo{wb}")
                    p.wait("tensor", f"dwo{wb}", wo_ld[wb])
                    for mo in range(4):
                        p.op("tensor", lambda e, kc=kc, mo=mo, wb=wb: e.matmul(
                            pb[mo][:, 0:256], s_woc[wb][:, mo * 128:(mo + 1) * 128],
                            s_cat[:, kc * 256:(kc + 1) * 256],
                            start=(kc == 0), stop=(kc == 15)),
                            inc="tt" if mo == 3 else None)
                    woc_users[wb] = p.c("tt")
                womm = p.c("tt")
                p.wait("vector", "tt", womm)
                p.wait("vector", "dxc", xc_ld)
                p.wait("vector", "dy1", yh_spill)
                for mo in range(4):
                    mg = wave * 4 + mo
                    p.op("vector", lambda e, mo=mo, mg=mg: e.scalar_tensor_tensor(
                        out=s_yh[:, mg * 256:(mg + 1) * 256],
                        in0=s_xc[:, mg * 256:(mg + 1) * 256], scalar=2.0,
                        in1=pb[mo][:, 0:256], op0=ALU.mult, op1=ALU.add),
                        inc="vv" if mo == 3 else None)
                yh_built = p.c("vv")
                p.wait("tensor", "vv", yh_built)
            wout_done = p.c("tt")
            cat_users = wout_done
            yh_built_all = yh_built

            s2mm2 = [0, 0]
            p.wait("scalar", "vv", yh_built)
            for mg in range(8):
                mp2 = mg % 2
                p.wait("scalar", "tt", s2mm2[mp2])
                p.op("scalar", lambda e, mg=mg, mp2=mp2: e.activation(
                    out=s_ysq2[mp2][:, 0:256], in_=s_yh[:, mg * 256:(mg + 1) * 256],
                    func=AF.Square, bias=C.constv(0.0), scale=1.0), inc="ss")
                sqd = p.c("ss")
                p.op("tensor", lambda e, mg=mg: e.matmul(
                    pb[4][:, 0:256], C.s_ones, s_yh[:, mg * 256:(mg + 1) * 256],
                    start=(mg == 0), stop=(mg == 7)), inc=None)
                p.wait("tensor", "ss", sqd)
                p.op("tensor", lambda e, mg=mg, mp2=mp2: e.matmul(
                    pb[5][:, 0:256], C.s_ones, s_ysq2[mp2][:, 0:256],
                    start=(mg == 0), stop=(mg == 7)), inc="tt")
                s2mm2[mp2] = p.c("tt")
            statmm = p.c("tt")
            if C.debug and g == 0 and cc == 0:
                p.wait("sync", "vv", p.c("vv"))
                p.wait("sync", "tt", statmm)
                dbgd = p.dma("sync", T["dbg_yh"][:, 0:2048], s_yh[:, 0:2048], sem="dqo")
                p.wait("vector", "dqo", dbgd)
            p.wait("vector", "tt", statmm)
            p.op("vector", lambda e: e.tensor_scalar(
                out=s_mb[:, 0:256], in0=pb[4][:, 0:256], scalar1=1.0 / D, scalar2=None,
                op0=ALU.mult), inc=None)
            p.op("vector", lambda e: e.tensor_scalar(
                out=s_t2[:, 0:256], in0=pb[5][:, 0:256], scalar1=1.0 / D, scalar2=None,
                op0=ALU.mult), inc=None)
            p.op("vector", lambda e: e.tensor_tensor(
                out=s_u[:, 0:256], in0=s_mb[:, 0:256], in1=s_mb[:, 0:256], op=ALU.mult),
                inc=None)
            p.op("vector", lambda e: e.scalar_tensor_tensor(
                out=s_var[:, 0:256], in0=s_t2[:, 0:256], scalar=EPS_LN, in1=s_u[:, 0:256],
                op0=ALU.add, op1=ALU.subtract), inc="vv")
            var_ok = p.c("vv")
            p.wait("scalar", "vv", var_ok)
            p.op("scalar", lambda e: e.activation(
                out=s_r[:, 0:256], in_=s_var[:, 0:256], func=AF.Ln,
                bias=C.constv(0.0), scale=1.0), inc=None)
            p.op("scalar", lambda e: e.activation(
                out=s_r[:, 0:256], in_=s_r[:, 0:256], func=AF.Exp,
                bias=C.constv(0.0), scale=-0.5), inc="ss")
            r_ok = p.c("ss")
            if C.debug and g == 0 and cc == 0:
                p.wait("sync", "ss", r_ok)
                p.dma("sync", T["dbg_yh"][:, 2048:2304], s_mb[:, 0:256])
                p.dma("sync", T["dbg_yh"][:, 2304:2560], s_r[:, 0:256])
                p.dma("sync", T["dbg_yh"][:, 2560:2816], s_var[:, 0:256])
            p.wait("vector", "ss", r_ok)
            p.wait("vector", "dy1", yh_spill)
            for mg in range(8):
                p.op("vector", lambda e, mg=mg: e.tensor_tensor(
                    out=s_ysq[:, 0:256], in0=s_yh[:, mg * 256:(mg + 1) * 256],
                    in1=s_mb[:, 0:256], op=ALU.subtract), inc=None)
                p.op("vector", lambda e, mg=mg: e.tensor_tensor(
                    out=s_yh[:, mg * 256:(mg + 1) * 256], in0=s_ysq[:, 0:256],
                    in1=s_r[:, 0:256], op=ALU.mult), inc=None)
                p.op("vector", lambda e, mg=mg: e.tensor_reduce(
                    out=s_red[:, 0:1], in_=s_yh[:, mg * 256:(mg + 1) * 256],
                    axis=mybir.AxisListType.X, op=ALU.max, apply_absolute_value=True),
                    inc=None)
                p.op("vector", lambda e: e.tensor_tensor(
                    out=s_m1[:, 0:1], in0=s_m1[:, 0:1], in1=s_red[:, 0:1], op=ALU.max),
                    inc="vv" if mg == 7 else None)
            y1_ok = p.c("vv")
            p.wait("sync", "vv", y1_ok)
            yh_spill = p.dma("sync",
                dap(SP["y1_sp"], g * 512 + cc * 256, [[8 * TCORE, 128], [TCORE, 8], [1, 256]]),
                s_yh[:, 0:2048].rearrange("p (c t) -> p c t", c=8), sem="dy1")
        qg_free = p.c("tt")
        hg_users_prev = p.c("tt")
        vg_users_prev = p.c("vv")

    p.wait("sync", "vv", p.c("vv"))
    mxs = p.dma("sync", dap(SP["mx_sp"], 0, [[1, 128], [1, 1]]), s_m1[:, 0:1], sem="dqo")
    p.wait("sync", "dqo", mxs)
    mxl = p.dma("sync", s_mb[0:1, 0:128], SP["mx_sp"][0:1, 0:128], sem="dqo")
    p.wait("vector", "dqo", mxl)
    p.op("vector", lambda e: e.tensor_reduce(
        out=s_red[0:1, 0:1], in_=s_mb[0:1, 0:128], axis=mybir.AxisListType.X,
        op=ALU.max), inc=None)
    if C.debug:
        p.op("vector", lambda e: e.tensor_copy(out=s_mb[0:1, 128:256], in_=s_mb[0:1, 0:128]), inc="vv")
        dchk = p.c("vv")
        p.wait("sync", "vv", dchk)
        p.dma("sync", T["dbg_t"][0:1, 0:128], s_mb[0:1, 128:256], sem="dqo")
    pst = s_red.ap[0][0]
    p.op("vector", lambda e: e.tensor_copy(
        out=s_mb[0:1, 0:8],
        in_=bass.AP(s_red.tensor, s_red.offset, [[pst, 1], [0, 8]])), inc="vv")
    m1rep = p.c("vv")
    p.wait("sync", "vv", m1rep)
    agm_in = p.dma("sync", SP["ag_m1_in"].ap(), s_mb[0:1, 0:8], sem="dqo")
    p.wait("sync", "dqo", agm_in)
    p.wait("gpsimd", "dqo", agm_in)
    p.wait("gpsimd", "cc", C.ag_state_done)
    p.op("gpsimd", lambda e: e.collective_compute(
        "AllGather", ALU.bypass, replica_groups=[list(range(8))],
        ins=[SP["ag_m1_in"].ap()], outs=[SP["ag_m1_out"].ap()]), inc="cc")
    agm1 = p.c("cc")
    C.agm1 = agm1
    C.gmax1_ok = p.c("vv")
    if C.debug:
        p.op("vector", lambda e: e.tensor_copy(out=s_mb[:, 16:17], in_=s_m1[:, 0:1]), inc="vv")
        chain_ok = p.c("vv")
        p.wait("sync", "vv", chain_ok)
        p.wait("sync", "cc", agm1)
        dd = p.dma("sync", T["dbg_chain"][:, 0:8], s_mb[:, 16:24], sem="dqo")
        dd = p.dma("sync", T["dbg_chain"][:, 8:16], s_mb[:, 0:8], sem="dqo")
        dd = p.dma("sync", dap(T["dbg_chain"], 16, [[32, 1], [1, 8]]),
                   dap(SP["ag_m1_in"].ap(), 0, [[8, 1], [1, 8]]), sem="dqo")
        dd = p.dma("sync", dap(T["dbg_chain"], 24, [[32, 8], [1, 8]]),
                   dap(SP["ag_m1_out"].ap(), 0, [[8, 8], [1, 8]]), sem="dqo")
        p.wait("vector", "dqo", dd)

    C.yh_spill_last = yh_spill
    if C.debug:
        p.wait("sync", "vv", C.gmax1_ok)
        p.wait("sync", "dy1", yh_spill)
        p.dma("sync", T["dbg_su"][:, 0:6144], SP["pref_sp"][:, 4 * 6144:5 * 6144], sem="dqo")
        p.dma("sync", T["dbg_su"][:, 6144:12288],
              dap(SP["ag_st_out"].ap(), 0, [[6144, 128], [1, 6144]]), sem="dqo")
        p.dma("sync", T["dbg_y1"], SP["y1_sp"], sem="dqo")


def emit_phase_f(C):
    C.s_clipx = C.s_clipx[:, 0:2]
    C.s_clipdn = C.s_clipdn[:, 0:2]
    C.p.barrier()
    emit_phase_f1(C)
    emit_phase_f2(C)


def emit_phase_f1(C):
    p, pb, T, SP, nc = C.p, C.pb, C.T, C.SP, C.nc
    QM = 127.0
    CLIPLO = 128.0
    CLIPHI = 383.99996948242188

    with ExitStack() as es:
        af = Arena(es.enter_context(nc.sbuf_tensor("af1", [128, 23040], F32)), 23040)
        ah = Arena(es.enter_context(nc.sbuf_tensor("af1h", [128, 36608], BF16)), 36608)
        ai = Arena(es.enter_context(nc.sbuf_tensor("ai1", [128, 2944], I32)), 2944)

        s_g64 = af.alloc(64, "g64")
        s_sx = af.alloc(2, "sx")
        s_invx = af.alloc(2, "invx")
        s_tmp1 = af.alloc(4, "tmp1")
        s_mpad = af.alloc(128, "mpad")
        s_z128 = af.alloc(2, "z128")
        p.memset("vector", s_z128, 0.0)
        ci2 = ai.alloc(2048, "ci32")

        def repmax_scalar(ag_out, sem_done, out_col):
            ld = p.dma("sync", s_g64[0:1, 0:64], dap(ag_out.ap(), 0, [[64, 1], [1, 64]]),
                       sem="dqo")
            p.wait("vector", "dqo", ld)
            p.wait("vector", "ms", p.c("ms"))
            p.op("vector", lambda e: e.tensor_reduce(
                out=s_z128[0:1, 0:1], in_=s_g64[0:1, 0:64], axis=mybir.AxisListType.X,
                op=ALU.max), inc=None)
            p.op("vector", lambda e: e.tensor_reduce(
                out=s_z128[0:1, 1:2], in_=s_g64[0:1, 0:64], axis=mybir.AxisListType.X,
                op=ALU.max), inc="vv")
            rok = p.c("vv")
            p.wait("tensor", "vv", rok)
            p.op("tensor", lambda e: e.matmul(pb[6][:, 0:2], C.s_ones, s_z128[:, 0:2],
                                              start=True, stop=True), inc="tt")
            rok = p.c("tt")
            p.wait("vector", "tt", rok)
            p.op("vector", lambda e, out_col=out_col: e.tensor_copy(
                out=out_col, in_=pb[6][:, 0:2]), inc="vv")
            return p.c("vv")

        def sig_scale(clip_ap, gmax_col, s_col, inv_col):
            p.op("scalar", lambda e: e.activation(out=s_col, in_=clip_ap, func=AF.Exp,
                                                  bias=C.constv(0.0), scale=-1.0), inc="ss")
            p.wait("vector", "ss")
            p.op("vector", lambda e: e.tensor_scalar(out=s_col, in0=s_col,
                                                     scalar1=1.0, scalar2=None, op0=ALU.add), inc=None)
            p.op("vector", lambda e: e.reciprocal(s_col, s_col), inc=None)
            p.op("vector", lambda e: e.tensor_tensor(out=s_col, in0=s_col,
                                                     in1=gmax_col, op=ALU.mult), inc=None)
            p.op("vector", lambda e: e.tensor_scalar(out=s_col, in0=s_col,
                                                     scalar1=1.0 / QM, scalar2=None, op0=ALU.mult), inc=None)
            p.op("vector", lambda e: e.reciprocal(inv_col, s_col), inc="vv")
            return p.c("vv")

        def quant_to_int16(u_ap, cu_ap, out16_ap, n, clip, ci=None):
            civ = ci if ci is not None else ci2[:, 0:n]
            if clip:
                p.op("vector", lambda e: e.tensor_scalar(
                    out=cu_ap, in0=u_ap, scalar1=CLIPLO, scalar2=CLIPHI,
                    op0=ALU.max, op1=ALU.min), inc=None)
                cua = cu_ap
            else:
                cua = u_ap
            p.op("vector", lambda e, cua=cua, civ=civ: e.tensor_copy(out=civ, in_=cua), inc=None)
            dd = cu_ap if not clip else u_ap
            p.op("vector", lambda e, cua=cua, dd=dd, civ=civ: e.tensor_tensor(
                out=dd, in0=civ, in1=cua, op=ALU.is_gt), inc=None)
            p.op("vector", lambda e, dd=dd, civ=civ: e.scalar_tensor_tensor(
                out=out16_ap, in0=civ, scalar=-256.0, in1=dd,
                op0=ALU.add, op1=ALU.subtract), inc="vv")
            return p.c("vv")

        p.wait("sync", "cc", C.agm1)
        g1_ok = repmax_scalar(SP["ag_m1_out"], None, s_tmp1[:, 0:2])
        p.wait("scalar", "vv", g1_ok)
        sx_ok = sig_scale(C.s_clipx, s_tmp1[:, 0:2], s_sx[:, 0:2], s_invx[:, 0:2])
        if C.debug:
            s_dbgsc = af.alloc(16, "dbgsc")
            p.op("vector", lambda e: e.tensor_copy(out=s_dbgsc[:, 0:1], in_=s_tmp1[:, 0:1]), inc=None)
            p.op("vector", lambda e: e.tensor_copy(out=s_dbgsc[:, 1:2], in_=s_sx[:, 0:1]), inc=None)
            p.op("vector", lambda e: e.tensor_copy(out=s_dbgsc[:, 2:3], in_=s_invx[:, 0:1]), inc=None)
            p.op("vector", lambda e: e.tensor_copy(out=s_dbgsc[:, 3:5], in_=C.s_clipx), inc="vv")
            dsc = p.c("vv")
            p.wait("sync", "vv", dsc)
            p.dma("sync", T["dbg_sc"][:, 0:8], s_dbgsc[:, 0:8], sem="dqo")

        s_y1g = af.alloc(4096, "y1g")
        s_u = af.alloc(2048, "uq")
        s_cu = af.alloc(2048, "cuq")
        s_xq = ah.alloc(16384, "xq")
        xqp = s_xq.ap[0][0]
        y1q_ld = 0
        for tg in range(NG):
            p.wait("sync", "vv", p.c("vv"))
            y1q_ld = p.dma("sync", s_y1g,
                           dap(SP["y1_sp"], tg * 512, [[8 * TCORE, 128], [TCORE, 8], [1, 512]]),
                           sem="dyq")
            p.wait("vector", "dyq", y1q_ld)
            p.wait("vector", "vv", sx_ok)
            for hb in range(2):
                sl = slice(hb * 2048, (hb + 1) * 2048)
                p.op("vector", lambda e, sl=sl: e.tensor_scalar(
                    out=s_u[:, 0:2048], in0=s_y1g[:, sl], scalar1=s_invx[:, 0:1],
                    scalar2=256.5, op0=ALU.mult, op1=ALU.add), inc="vv")
                out16 = bass.AP(s_xq.tensor, s_xq.offset + (hb * 4) * 2048 + tg * 512,
                                [[xqp, 128], [2048, 4], [1, 512]])
                quant_to_int16(s_u[:, 0:2048].rearrange("p (c t) -> p c t", c=4),
                               s_cu[:, 0:2048].rearrange("p (c t) -> p c t", c=4),
                               out16, 2048, clip=True,
                               ci=ci2[:, 0:2048].rearrange("p (c t) -> p c t", c=4))
        xq_ok = p.c("vv")
        if C.debug:
            p.op("vector", lambda e: e.tensor_copy(out=s_g64[:, 8:9], in_=s_sx[:, 0:1]), inc=None)
            p.op("vector", lambda e: e.tensor_copy(out=s_g64[:, 9:10], in_=s_invx[:, 0:1]), inc=None)
            p.op("vector", lambda e: e.tensor_copy(out=s_g64[:, 10:11], in_=s_tmp1[:, 0:1]), inc="vv")
            dbgsc = p.c("vv")
            p.wait("sync", "vv", max(xq_ok, dbgsc))
            dd = p.dma("sync", T["dbg_xq"], s_xq[:, 0:16384], sem="dqo")
            dd = p.dma("sync", T["dbg_sc"][:, 0:8], s_g64[:, 8:16], sem="dqo")
            p.wait("vector", "dqo", dd)
            p.wait("scalar", "dqo", dd)

        s_wg = [af.alloc(1024, "wg0"), af.alloc(1024, "wg1")]
        s_wu = [af.alloc(1024, "wu0"), af.alloc(1024, "wu1")]
        s_wqu2 = [af.alloc(1024, "wqua"), af.alloc(1024, "wqub")]
        s_wqd2 = [af.alloc(1024, "wqda"), af.alloc(1024, "wqdb")]
        s_stash = af.alloc(2048, "sgstash")
        s_up2 = [af.alloc(512, "upbuf0"), af.alloc(512, "upbuf1")]
        s_t2 = [af.alloc(512, "tbuf0"), af.alloc(512, "tbuf1")]
        s_tmax = af.alloc(2, "tmax")
        s_sgu = af.alloc(16, "sgu")
        s_tsc = af.alloc(16, "tsc")
        w162 = [ah.alloc(1024, "w16a"), ah.alloc(1024, "w16b")]
        wgT2 = [ah.alloc(1024, "wgT0"), ah.alloc(1024, "wgT1")]
        wuT2 = [ah.alloc(1024, "wuT0"), ah.alloc(1024, "wuT1")]

        p.memset("vector", s_tmax, 0.0)
        p.wait("vector", "ms")

        wg_ld = [0, 0]
        wg_users = [0, 0]
        wg_users_ss = [0, 0]
        bank_ss = [0, 0, 0, 0]
        t_sp_done = [0, 0]
        w16_users = [0, 0]
        wT_users = [0, 0]
        wT_ok = [0, 0]
        wb45_evac = 0
        i1_done = [0, 0, 0, 0]
        sgu_cons = [0, 0, 0, 0]
        stash_users = 0
        up_cons = [0, 0]

        for step in range(NKI + 1):
            if step < NKI:
                ki = step
                wb = ki % 2
                p.wait("sync", "vv", wg_users[wb])
                p.wait("sync", "ss", wg_users_ss[wb])
                p.dma("sync", s_wg[wb], T["w_gate"][ki * 128:(ki + 1) * 128, :], sem=f"dwg{wb}")
                p.dma("sync", s_wu[wb], T["w_up"][ki * 128:(ki + 1) * 128, :], sem=f"dwg{wb}")
                wg_ld[wb] = p.c(f"dwg{wb}")
                p.wait("vector", f"dwg{wb}", wg_ld[wb])
                for wi, wsrc, wTdst in ((0, s_wg[wb], wgT2[wb]), (1, s_wu[wb], wuT2[wb])):
                    q4 = wb * 2 + wi
                    tsc = s_tsc[:, q4 * 4:q4 * 4 + 4]
                    sgs = (ki % 4) * 4 + 2 * wi
                    p.op("vector", lambda e, wsrc=wsrc, tsc=tsc: e.tensor_reduce(
                        out=tsc[:, 0:1], in_=wsrc, axis=mybir.AxisListType.X, op=ALU.max,
                        apply_absolute_value=True), inc=None)
                    p.op("vector", lambda e, wsrc=wsrc, tsc=tsc: e.tensor_reduce(
                        out=tsc[:, 1:2], in_=wsrc, axis=mybir.AxisListType.X, op=ALU.max,
                        apply_absolute_value=True), inc=None)
                    p.wait("vector", "ss", sgu_cons[ki % 4])
                    p.op("vector", lambda e, tsc=tsc, sgs=sgs: e.tensor_scalar(
                        out=s_sgu[:, sgs:sgs + 2], in0=tsc[:, 0:2], scalar1=1.0 / QM,
                        scalar2=None, op0=ALU.mult), inc=None)
                    p.op("vector", lambda e, sgs=sgs: e.tensor_tensor(
                        out=s_sgu[:, sgs:sgs + 2], in0=s_sgu[:, sgs:sgs + 2],
                        in1=s_sx[:, 0:2], op=ALU.mult), inc=None)
                    p.op("vector", lambda e, tsc=tsc: e.reciprocal(tsc[:, 0:2], tsc[:, 0:2]),
                         inc=None)
                    p.op("vector", lambda e, tsc=tsc: e.tensor_scalar(
                        out=tsc[:, 0:2], in0=tsc[:, 0:2], scalar1=QM, scalar2=None,
                        op0=ALU.mult), inc="vv")
                    p.op("vector", lambda e, wsrc=wsrc, wb=wb, tsc=tsc: e.tensor_scalar(
                        out=s_wqu2[wb][:, 0:1024], in0=wsrc, scalar1=tsc[:, 0:1],
                        scalar2=256.5, op0=ALU.mult, op1=ALU.add), inc="vv")
                    p.wait("vector", "tt", w16_users[wb])
                    w_ok = quant_to_int16(s_wqu2[wb][:, 0:1024], s_wqd2[wb][:, 0:1024],
                                          w162[wb][:, 0:1024], 1024, clip=False)
                    p.wait("tensor", "vv", max(w_ok, wb45_evac))
                    for i in range(8):
                        kb = 4 + i // 4
                        p.op("tensor", lambda e, i=i, kb=kb, wb=wb: e.matmul(
                            pb[kb][:, (i % 4) * 128:(i % 4) * 128 + 128],
                            w162[wb][:, i * 128:(i + 1) * 128], C.s_id16,
                            start=True, stop=True), inc="tt" if i == 7 else None)
                    w16_users[wb] = p.c("tt")
                    p.wait("vector", "tt", w16_users[wb])
                    p.wait("vector", "tt", wT_users[wb])
                    for kb in (4, 5):
                        p.op("vector", lambda e, kb=kb, wTdst=wTdst: e.tensor_copy(
                            out=wTdst[:, (kb - 4) * 512:(kb - 3) * 512], in_=pb[kb][:, 0:512]),
                            inc="vv" if kb == 5 else None)
                    wT_ok[wb] = p.c("vv")
                    wb45_evac = wT_ok[wb]
                wg_users[wb] = p.c("vv")
                wg_users_ss[wb] = p.c("ss")

            if step >= 1:
                ki = step - 1
                wb = ki % 2
                p.wait("tensor", "vv", max(wT_ok[wb], xq_ok))
                for wi in (0, 1):
                    wTdst = wgT2[wb] if wi == 0 else wuT2[wb]
                    sgs = (ki % 4) * 4 + 2 * wi
                    for tg in range(NG):
                        bank = (wi * 2) + (tg % 2)
                        p.wait("tensor", "ss", bank_ss[bank])
                        for hc in range(8):
                            p.op("tensor", lambda e, hc=hc, tg=tg, bank=bank, wTdst=wTdst: e.matmul(
                                pb[bank][:, 0:512], wTdst[:, hc * 128:(hc + 1) * 128],
                                s_xq[:, hc * 2048 + tg * 512: hc * 2048 + tg * 512 + 512],
                                start=(hc == 0), stop=(hc == 7)), inc="tt" if hc == 7 else None)
                        gmm = p.c("tt")
                        p.wait("scalar", "tt", gmm)
                        if wi == 0:
                            p.wait("scalar", "vv", stash_users)
                            p.op("scalar", lambda e, tg=tg, bank=bank, sgs=sgs: e.activation(
                                out=s_stash[:, tg * 512:(tg + 1) * 512], in_=pb[bank][:, 0:512],
                                func=AF.Silu, bias=C.constv(0.0), scale=s_sgu[:, sgs:sgs + 1]),
                                inc="ss")
                            bank_ss[bank] = p.c("ss")
                        else:
                            tb = (ki * NG + tg) % 2
                            p.wait("scalar", "vv", up_cons[tb])
                            p.op("scalar", lambda e, tg=tg, bank=bank, sgs=sgs, tb=tb: e.activation(
                                out=s_up2[tb][:, 0:512], in_=pb[bank][:, 0:512],
                                func=AF.Identity, bias=C.constv(0.0),
                                scale=s_sgu[:, sgs:sgs + 1]), inc="ss")
                            up_ok = p.c("ss")
                            bank_ss[bank] = up_ok
                            p.wait("vector", "ss", up_ok)
                            p.wait("vector", f"dtw{tb}", t_sp_done[tb])
                            p.op("vector", lambda e, tg=tg, tb=tb: e.tensor_tensor(
                                out=s_t2[tb][:, 0:512], in0=s_stash[:, tg * 512:(tg + 1) * 512],
                                in1=s_up2[tb][:, 0:512], op=ALU.mult), inc=None)
                            p.op("vector", lambda e, tb=tb: e.tensor_reduce(
                                out=s_tmp1[:, 3:4], in_=s_t2[tb][:, 0:512],
                                axis=mybir.AxisListType.X,
                                op=ALU.max, apply_absolute_value=True), inc=None)
                            p.op("vector", lambda e: e.tensor_tensor(
                                out=s_tmax[:, 0:1], in0=s_tmax[:, 0:1], in1=s_tmp1[:, 3:4],
                                op=ALU.max), inc="vv")
                            t_ok = p.c("vv")
                            up_cons[tb] = t_ok
                            if tg == NG - 1:
                                stash_users = t_ok
                            p.wait("sync", "vv", t_ok)
                            t_sp_done[tb] = p.dma("sync",
                                SP["t_sp"][:, tg * NKI * 512 + ki * 512:
                                           tg * NKI * 512 + ki * 512 + 512],
                                s_t2[tb][:, 0:512], sem=f"dtw{tb}")
                wT_users[wb] = p.c("tt")
                sgu_cons[ki % 4] = p.c("ss")

        p.wait("sync", "vv", p.c("vv"))
        mxs = p.dma("sync", dap(SP["mx_sp"], 0, [[1, 128], [1, 1]]), s_tmax[:, 0:1], sem="dqo")
        p.wait("sync", "dqo", mxs)
        mxl = p.dma("sync", s_mpad[0:1, 0:128], SP["mx_sp"][0:1, 0:128], sem="dqo")
        p.wait("vector", "dqo", mxl)
        p.op("vector", lambda e: e.tensor_reduce(
            out=s_tmp1[0:1, 0:1], in_=s_mpad[0:1, 0:128], axis=mybir.AxisListType.X,
            op=ALU.max), inc=None)
        pst = s_tmp1.ap[0][0]
        p.op("vector", lambda e: e.tensor_copy(
            out=s_g64[0:1, 0:8],
            in_=bass.AP(s_tmp1.tensor, s_tmp1.offset, [[pst, 1], [0, 8]])), inc="vv")
        m2rep = p.c("vv")
        p.wait("sync", "vv", m2rep)
        agm2_in = p.dma("sync", SP["ag_m2_in"].ap(), s_g64[0:1, 0:8], sem="dqo")
        p.wait("sync", "dqo", agm2_in)
        p.wait("gpsimd", "dqo", agm2_in)
        p.op("gpsimd", lambda e: e.collective_compute(
            "AllGather", ALU.bypass, replica_groups=[list(range(8))],
            ins=[SP["ag_m2_in"].ap()], outs=[SP["ag_m2_out"].ap()]), inc="cc")
        agm2 = p.c("cc")
        p.wait("sync", "cc", agm2)
        p.wait("sync", "dtw0", p.c("dtw0"))
        p.wait("sync", "dtw1", p.c("dtw1"))
        if C.debug:
            p.dma("sync", T["dbg_t"], SP["t_sp"][:, 0:2048], sem="dqo")
        g2_ok = repmax_scalar(SP["ag_m2_out"], None, s_tmp1[:, 0:2])
        s_s2 = af.alloc(4, "s2")
        s_inv2 = af.alloc(4, "inv2")
        p.wait("scalar", "vv", g2_ok)
        s2_ok = sig_scale(C.s_clipdn, s_tmp1[:, 0:2], s_s2[:, 0:2], s_inv2[:, 0:2])
        p.op("vector", lambda e: e.tensor_copy(out=C.s_scal[:, 0:2], in_=s_s2[:, 0:2]), inc=None)
        p.op("vector", lambda e: e.tensor_copy(out=C.s_scal[:, 2:4], in_=s_inv2[:, 0:2]), inc="vv")
        if C.debug:
            scok = p.c("vv")
            p.wait("sync", "vv", scok)
            dd = p.dma("sync", T["dbg_sc"][:, 8:10], C.s_scal[:, 0:2], sem="dqo")
            p.wait("vector", "dqo", dd)
        p.barrier()


def emit_phase_f2(C):
    p, pb, T, SP, nc = C.p, C.pb, C.T, C.SP, C.nc
    QM = 127.0
    CLIPLO = 128.0
    CLIPHI = 383.99996948242188

    with ExitStack() as es:
        af = Arena(es.enter_context(nc.sbuf_tensor("af2", [128, 23040], F32)), 23040)
        ah = Arena(es.enter_context(nc.sbuf_tensor("af2h", [128, 36608], BF16)), 36608)
        ai = Arena(es.enter_context(nc.sbuf_tensor("ai2", [128, 2944], I32)), 2944)
        ci2 = ai.alloc(2944, "ci32b")
        s_s2 = C.s_scal[:, 0:1]
        s_inv2 = C.s_scal[:, 2:3]

        if C.debug:
            dd = p.dma("sync", T["dbg_sc"][:, 10:14], C.s_scal[:, 0:4], sem="dqo")
        h2 = ah.alloc(11264, "h2")
        wdT = ah.alloc(22528, "wdT")
        w16d = ah.alloc(2816, "w16d")
        s_wd = af.alloc(2816, "wd")
        s_u = af.alloc(2816, "ud")
        s_cu = af.alloc(2816, "cud")
        s_sdo = af.alloc(32, "sdo")
        s_tslab = af.alloc(2816, "tslab")
        s_y1tg = af.alloc(4096, "y1tg")
        s_y2 = af.alloc(4096, "y2")
        s_mb = af.alloc(512, "mb2")
        s_t2s = af.alloc(512, "t22")
        s_uu = af.alloc(512, "uu2")
        s_var = af.alloc(512, "var2")
        s_r = af.alloc(512, "r2")
        s_ysq = af.alloc(512, "ysq2")
        s_tmp1 = af.alloc(4, "tmp1b")

        def quant_to_int16b(u_ap, cu_ap, out16_ap, n, clip):
            if clip:
                p.op("vector", lambda e: e.tensor_scalar(
                    out=cu_ap, in0=u_ap, scalar1=CLIPLO, scalar2=CLIPHI,
                    op0=ALU.max, op1=ALU.min), inc=None)
                cua = cu_ap
            else:
                cua = u_ap
            p.op("vector", lambda e, cua=cua: e.tensor_copy(out=ci2[:, 0:n], in_=cua), inc=None)
            dd = cu_ap if not clip else u_ap
            p.op("vector", lambda e, cua=cua, dd=dd: e.tensor_tensor(
                out=dd, in0=ci2[:, 0:n], in1=cua, op=ALU.is_gt), inc=None)
            p.op("vector", lambda e, dd=dd: e.scalar_tensor_tensor(
                out=out16_ap, in0=ci2[:, 0:n], scalar=-256.0, in1=dd,
                op0=ALU.add, op1=ALU.subtract), inc="vv")
            return p.c("vv")

        wd_ld = 0
        w16d_users = 0
        for hc in range(8):
            p.wait("sync", "vv", p.c("vv"))
            wd_ld = p.dma("sync", s_wd, T["w_down"][hc * 128:(hc + 1) * 128, :], sem="dwd")
            p.wait("vector", "dwd", wd_ld)
            p.op("vector", lambda e: e.tensor_reduce(
                out=s_tmp1[:, 2:3], in_=s_wd, axis=mybir.AxisListType.X, op=ALU.max,
                apply_absolute_value=True), inc=None)
            p.op("vector", lambda e: e.tensor_reduce(
                out=s_tmp1[:, 3:4], in_=s_wd, axis=mybir.AxisListType.X, op=ALU.max,
                apply_absolute_value=True), inc=None)
            p.op("vector", lambda e, hc=hc: e.tensor_scalar(
                out=s_tmp1[:, 0:2], in0=s_tmp1[:, 2:4], scalar1=1.0 / QM,
                scalar2=None, op0=ALU.mult), inc=None)
            p.op("vector", lambda e, hc=hc: e.scalar_tensor_tensor(
                out=s_sdo[:, 2 * hc:2 * hc + 2],
                in0=s_tmp1[:, 0:2], scalar=1.0, in1=C.s_scal[:, 0:2],
                op0=ALU.mult, op1=ALU.mult), inc=None)
            p.op("vector", lambda e: e.reciprocal(s_tmp1[:, 2:4], s_tmp1[:, 2:4]), inc=None)
            p.op("vector", lambda e: e.tensor_scalar(
                out=s_tmp1[:, 2:4], in0=s_tmp1[:, 2:4], scalar1=QM, scalar2=None,
                op0=ALU.mult), inc="vv")
            p.op("vector", lambda e: e.tensor_scalar(
                out=s_u[:, 0:2816], in0=s_wd, scalar1=s_tmp1[:, 2:3],
                scalar2=256.5, op0=ALU.mult, op1=ALU.add), inc="vv")
            p.wait("vector", "tt", w16d_users)
            w_ok = quant_to_int16b(s_u[:, 0:2816], s_cu[:, 0:2816], w16d[:, 0:2816],
                                   2816, clip=False)
            p.wait("tensor", "vv", w_ok)
            done = 0
            for wv in range(3):
                ntile = 8 if wv < 2 else 6
                for i in range(ntile):
                    ti = wv * 8 + i
                    kb = 4 + i // 4
                    p.op("tensor", lambda e, ti=ti, i=i, kb=kb: e.matmul(
                        pb[kb][:, (i % 4) * 128:(i % 4) * 128 + 128],
                        w16d[:, ti * 128:(ti + 1) * 128], C.s_id16,
                        start=True, stop=True), inc="tt" if i == ntile - 1 else None)
                trw = p.c("tt")
                p.wait("vector", "tt", trw)
                for kb in (4, 5):
                    nn = min(512, (ntile - (kb - 4) * 4) * 128)
                    if nn <= 0:
                        continue
                    p.op("vector", lambda e, kb=kb, nn=nn, wv=wv, hc=hc: e.tensor_copy(
                        out=wdT[:, hc * 2816 + wv * 1024 + (kb - 4) * 512:
                                hc * 2816 + wv * 1024 + (kb - 4) * 512 + nn],
                        in_=pb[kb][:, 0:nn]), inc="vv")
                evw = p.c("vv")
                p.wait("tensor", "vv", evw)
            w16d_users = p.c("tt")
        wdT_ok = p.c("vv")

        s_tslab2 = [s_tslab, s_wd]
        i1_cnt = [p.c("vv"), p.c("vv")]
        t_ld2 = [0, 0]
        h2_cons = 0
        y2_out_done = 0
        qk = 0
        y1_cons = 0
        for tg in range(NG):
            p.wait("sync", "vv", y1_cons)
            y1tg_ld = p.dma("sync", s_y1tg,
                            dap(SP["y1_sp"], tg * 512, [[8 * TCORE, 128], [TCORE, 8], [1, 512]]),
                            sem="dyr")
            for sb in range(4):
                gp = (tg * 4 + sb) % 2
                p.wait("sync", "vv", i1_cnt[gp])
                t_ld2[gp] = p.dma("sync", s_tslab2[gp],
                             SP["t_sp"][:, tg * NKI * 512 + sb * 2816:
                                        tg * NKI * 512 + (sb + 1) * 2816], sem=f"dtr{gp}")
                p.wait("vector", f"dtr{gp}", t_ld2[gp])
                p.op("vector", lambda e, gp=gp: e.tensor_scalar(
                    out=s_u[:, 0:2816], in0=s_tslab2[gp], scalar1=s_inv2,
                    scalar2=256.5, op0=ALU.mult, op1=ALU.add), inc="vv")
                i1_cnt[gp] = p.c("vv")
                p.wait("vector", "tt", h2_cons)
                qk = quant_to_int16b(s_u[:, 0:2816], s_cu[:, 0:2816],
                                h2[:, sb * 2816:(sb + 1) * 2816], 2816, clip=True)
                if C.debug and tg == 0 and sb == 0:
                    p.wait("sync", "vv", qk)
                    dd = p.dma("sync", T["dbg_q"][:, 0:2816], s_u[:, 0:2816], sem="dqo")
                    dd = p.dma("sync", T["dbg_q"][:, 2816:5632], s_cu[:, 0:2816], sem="dqo")
                    p.wait("scalar", "dqo", dd)
                    p.wait("vector", "dqo", dd)
            h2_ok = p.c("vv")
            if C.debug and tg == 0:
                p.wait("sync", "vv", h2_ok)
                dd = p.dma("sync", T["dbg_h2"], h2[:, 0:11264], sem="dqo")
                p.wait("vector", "dqo", dd)
            p.wait("tensor", "vv", max(h2_ok, wdT_ok))
            bank_vv = [0, 0]
            for hc in range(8):
                bank = hc % 2
                p.wait("tensor", "vv", bank_vv[bank])
                for ki in range(NKI):
                    p.op("tensor", lambda e, hc=hc, ki=ki, bank=bank: e.matmul(
                        pb[bank][:, 0:512], wdT[:, (hc * 22 + ki) * 128:(hc * 22 + ki + 1) * 128],
                        h2[:, ki * 512:(ki + 1) * 512],
                        start=(ki == 0), stop=(ki == NKI - 1)), inc="tt" if ki == NKI - 1 else None)
                dmm = p.c("tt")
                p.wait("vector", "tt", dmm)
                p.wait("vector", "dyr", y1tg_ld)
                p.op("vector", lambda e, hc=hc, bank=bank: e.scalar_tensor_tensor(
                    out=s_y2[:, hc * 512:(hc + 1) * 512], in0=pb[bank][:, 0:512],
                    scalar=s_sdo[:, 2 * hc:2 * hc + 1], in1=s_y1tg[:, hc * 512:(hc + 1) * 512],
                    op0=ALU.mult, op1=ALU.add), inc="vv")
                bank_vv[bank] = p.c("vv")
            y2_ok = p.c("vv")
            y1_cons = y2_ok
            if C.debug and tg == 0:
                p.wait("sync", "vv", y2_ok)
                dd = p.dma("sync", T["dbg_y2"], s_y2[:, 0:4096], sem="dqo")
                p.wait("scalar", "dqo", dd)
                p.wait("vector", "dqo", dd)
            h2_cons = p.c("tt")
            s2mm = 0
            p.wait("scalar", "vv", y2_ok)
            for mg in range(8):
                p.wait("scalar", "tt", s2mm)
                p.op("scalar", lambda e, mg=mg: e.activation(
                    out=s_ysq[:, 0:512], in_=s_y2[:, mg * 512:(mg + 1) * 512],
                    func=AF.Square, bias=C.constv(0.0), scale=1.0), inc="ss")
                sqd = p.c("ss")
                p.wait("tensor", "vv", y2_ok)
                p.op("tensor", lambda e, mg=mg: e.matmul(
                    pb[4][:, 0:512], C.s_ones, s_y2[:, mg * 512:(mg + 1) * 512],
                    start=(mg == 0), stop=(mg == 7)), inc=None)
                p.wait("tensor", "ss", sqd)
                p.op("tensor", lambda e, mg=mg: e.matmul(
                    pb[5][:, 0:512], C.s_ones, s_ysq[:, 0:512],
                    start=(mg == 0), stop=(mg == 7)), inc="tt")
                s2mm = p.c("tt")
            statmm = p.c("tt")
            p.wait("vector", "tt", statmm)
            p.op("vector", lambda e: e.tensor_scalar(
                out=s_mb[:, 0:512], in0=pb[4][:, 0:512], scalar1=1.0 / D, scalar2=None,
                op0=ALU.mult), inc=None)
            p.op("vector", lambda e: e.tensor_scalar(
                out=s_t2s[:, 0:512], in0=pb[5][:, 0:512], scalar1=1.0 / D, scalar2=None,
                op0=ALU.mult), inc=None)
            p.op("vector", lambda e: e.tensor_tensor(
                out=s_uu[:, 0:512], in0=s_mb[:, 0:512], in1=s_mb[:, 0:512], op=ALU.mult),
                inc=None)
            p.op("vector", lambda e: e.scalar_tensor_tensor(
                out=s_var[:, 0:512], in0=s_t2s[:, 0:512], scalar=EPS_LN, in1=s_uu[:, 0:512],
                op0=ALU.add, op1=ALU.subtract), inc="vv")
            var_ok = p.c("vv")
            p.wait("scalar", "vv", var_ok)
            p.op("scalar", lambda e: e.activation(
                out=s_r[:, 0:512], in_=s_var[:, 0:512], func=AF.Ln,
                bias=C.constv(0.0), scale=1.0), inc=None)
            p.op("scalar", lambda e: e.activation(
                out=s_r[:, 0:512], in_=s_r[:, 0:512], func=AF.Exp,
                bias=C.constv(0.0), scale=-0.5), inc="ss")
            r_ok = p.c("ss")
            p.wait("vector", "ss", r_ok)
            p.wait("vector", "dout", y2_out_done)
            for mg in range(8):
                p.op("vector", lambda e, mg=mg: e.tensor_tensor(
                    out=s_ysq[:, 0:512], in0=s_y2[:, mg * 512:(mg + 1) * 512],
                    in1=s_mb[:, 0:512], op=ALU.subtract), inc=None)
                p.op("vector", lambda e, mg=mg: e.tensor_tensor(
                    out=s_y2[:, mg * 512:(mg + 1) * 512], in0=s_ysq[:, 0:512],
                    in1=s_r[:, 0:512], op=ALU.mult), inc="vv" if mg == 7 else None)
            y2n_ok = p.c("vv")
            p.wait("sync", "vv", y2n_ok)
            y2_out_done = p.dma("sync",
                dap(T["out_t"], tg * 512, [[TCORE, 128], [128 * TCORE, 8], [1, 512]]),
                s_y2[:, 0:4096].rearrange("p (c t) -> p c t", c=8), sem="dout")
    C.out_done = p.c("dout")


def emit_finish(C):
    p, T = C.p, C.T
    p.barrier()
    if C.stage == "F":
        p.wait("sync", "dout", C.out_done)
    else:
        fin = p.dma("sync", T["out_t"][0:128, 0:128], C.s_state[:, 0:128])
        p.wait("sync", "dq", fin)


def _stage_inputs(L0_attn_x, L0_attn_norm_w, L0_attn_Wq, L0_attn_Wk, L0_attn_Wv,
                  L0_attn_Wqf, L0_attn_Wkf, L0_attn_Wout, L0_ffn_clip_x,
                  L0_ffn_clip_down, L0_ffn_w_gate, L0_ffn_w_up, L0_ffn_w_down):
    f32 = np.float32
    x = np.asarray(L0_attn_x, f32)

    def fpad(w, hi):
        wp = np.zeros((128, 128), f32)
        if hi:
            wp[64:128, 0:16] = np.asarray(w, f32)
        else:
            wp[0:64, 0:16] = np.asarray(w, f32)
        return wp

    r = np.arange(128)
    m_diag = ((r[:, None] <= r[None, :]) & (r[:, None] >= r[None, :] - 64)).astype(f32)
    m_prev = (r[:, None] >= r[None, :] + 64).astype(f32)
    m_tril = (r[:, None] <= r[None, :]).astype(f32)
    ident = np.eye(128, dtype=f32)

    import ml_dtypes
    ident16 = np.eye(128, dtype=ml_dtypes.bfloat16)

    common = dict(
        wq=np.ascontiguousarray(np.asarray(L0_attn_Wq, f32)),
        wk=np.ascontiguousarray(np.asarray(L0_attn_Wk, f32)),
        wv=np.ascontiguousarray(np.asarray(L0_attn_Wv, f32)),
        wqf_lo=fpad(L0_attn_Wqf, 0), wqf_hi=fpad(L0_attn_Wqf, 1),
        wkf_lo=fpad(L0_attn_Wkf, 0), wkf_hi=fpad(L0_attn_Wkf, 1),
        wout=np.ascontiguousarray(np.asarray(L0_attn_Wout, f32)),
        normw=np.ascontiguousarray(np.asarray(L0_attn_norm_w, f32).reshape(8, 128).T),
        m_diag=m_diag, m_prev=m_prev, m_tril=m_tril, ident=ident, ident16=ident16,
        clip_x=np.full((128, 2), np.asarray(L0_ffn_clip_x, f32)[0], f32),
        clip_dn=np.full((128, 2), np.asarray(L0_ffn_clip_down, f32)[0], f32),
        w_gate=np.ascontiguousarray(np.asarray(L0_ffn_w_gate, f32)),
        w_up=np.ascontiguousarray(np.asarray(L0_ffn_w_up, f32)),
        w_down=np.ascontiguousarray(np.asarray(L0_ffn_w_down, f32)),
    )

    in_maps = []
    for core in range(8):
        b, half = core // 2, core % 2
        t0 = half * TCORE
        xs = np.zeros((NTT * 128, D), f32)
        if half == 1:
            xs[0:128] = x[b, t0 - 128:t0]
        xs[128:] = x[b, t0:t0 + TCORE]
        m = dict(common)
        m["x_t"] = np.ascontiguousarray(xs.T)
        m["m_first"] = m_prev if half == 1 else np.zeros_like(m_prev)
        m["lin_flag"] = np.full((128, 1), f32(half))
        in_maps.append(m)
    return in_maps


def kernel(**inputs):
    in_maps = _stage_inputs(**inputs)
    nc = build_program(stage="F", debug=False)
    res = run_bass_kernel_spmd(nc, in_maps, core_ids=list(range(8)))
    out = np.empty((4, 4096, D), np.float32)
    for core in range(8):
        b, half = core // 2, core % 2
        out[b, half * TCORE:(half + 1) * TCORE] = np.asarray(res.results[core]["out_t"]).T
    return out
